# version 06
# baseline (speedup 1.0000x reference)
import numpy as np
from contextlib import ExitStack

import concourse.bass as bass
import concourse.mybir as mybir
from concourse.bass_utils import run_bass_kernel_spmd

F32 = mybir.dt.float32
BF16 = mybir.dt.bfloat16
I32 = mybir.dt.int32
ALU = mybir.AluOpType
ACTF = mybir.ActivationFunctionType
AX = mybir.AxisListType

T = 2048
C = 1024
NTC = 16
NCC = 8
NH = 16
DV = 32
ROPE = 64
KEEP = 256
HF = 4096
NE = 20
MAGIC = 1.5 * 2.0 ** 23
SQK = 1.0 / np.sqrt(96.0)


class B:

    def __init__(self, dbg=()):
        self.dbg = set(dbg)
        self.nc = bass.Bass("TRN2", target_bir_lowering=False, detect_race_conditions=False)
        self.dram = {}
        self.DBG = {}
        self.es = ExitStack()
        self.sems = {}
        self.block = self.es.enter_context(self.nc.Block())

    def din(self, name, shape, dtype=F32):
        ap = self.nc.dram_tensor(name, shape, dtype, kind="ExternalInput").ap()
        self.dram[name] = ap
        return ap

    def tap(self, name, shape, dtype=F32):
        if name in self.dbg:
            self.DBG[name] = self.nc.dram_tensor("dbg_" + name, shape, dtype,
                                                 kind="ExternalOutput").ap()
            return self.DBG[name]
        return None

    def sem(self, name):
        if name not in self.sems:
            self.sems[name] = [self.es.enter_context(self.nc.semaphore(name)), 0]
        return self.sems[name][0]

    def inc(self, name, inst, n=1):
        h = self.sem(name)
        inst.then_inc(h, n)
        self.sems[name][1] += n
        return self.sems[name][1]

    def val(self, name):
        self.sem(name)
        return self.sems[name][1]

    def wt(self, eng, name, v=None):
        eng.wait_ge(self.sem(name), self.val(name) if v is None else v)

    def bar(self, eng, inst, name="sebar"):
        n = self.inc(name, inst)
        eng.wait_ge(self.sem(name), n)

    def TE(self, fn):
        @self.block.tensor
        def _(t):
            fn(t)

    def VE(self, fn):
        @self.block.vector
        def _(v):
            fn(v)

    def SE(self, fn):
        @self.block.scalar
        def _(s):
            fn(s)

    def GE(self, fn):
        @self.block.gpsimd
        def _(g):
            fn(g)

    def SY(self, fn):
        @self.block.sync
        def _(s):
            fn(s)

    def sbuf(self, name, shape, dtype, stack=None):
        return (stack or self.es).enter_context(self.nc.sbuf_tensor(name, shape, dtype))

    def psum(self, name, shape, dtype=F32, stack=None):
        return (stack or self.es).enter_context(self.nc.psum_tensor(name, shape, dtype))

    def dbg_dma(self, name, src_ap, waits=()):
        if name not in self.dbg:
            return
        t = self.tap(name, list(src_ap.shape), src_ap.dtype)
        def S(sy):
            for w in waits:
                self.wt(sy, w)
            self.inc("stq", sy.dma_start(t, src_ap), 16)
        self.SY(S)


def build_nc(dbg=(), stop_after=None):
    b = B(dbg)
    nc = b.nc
    es = b.es
    sbuf, psum, TE, VE, SE, GE, SY, inc, wt, val = (
        b.sbuf, b.psum, b.TE, b.VE, b.SE, b.GE, b.SY, b.inc, b.wt, b.val)

    x_in = b.din("x", [T, C])
    w_cq = b.din("w_cq", [C, 96]); g_qnorm = b.din("g_qnorm", [96, 1])
    wq_nope = b.din("wq_nope", [96, NH * 32])
    wq_x1 = b.din("wq_x1", [96, NH * 32]); wq_x2 = b.din("wq_x2", [96, NH * 32])
    w_ckv = b.din("w_ckv", [C, 32]); g_kvnorm = b.din("g_kvnorm", [32, 1])
    wk_nope = b.din("wk_nope", [32, NH * 32])
    w_dv = b.din("w_dv", [32, NH * 32])
    w_krope = b.din("w_krope", [C, ROPE])
    w_small = b.din("w_small", [C, 4])
    ww_nope = b.din("ww_nope", [C, NH * 32]); ww_x1 = b.din("ww_x1", [C, NH * 32]); ww_x2 = b.din("ww_x2", [C, NH * 32])
    w_winv = b.din("w_winv", [C, NH * 32])
    ws_nope = b.din("ws_nope", [C, NH * 32]); ws_x1 = b.din("ws_x1", [C, NH * 32]); ws_x2 = b.din("ws_x2", [C, NH * 32])
    w_selv = b.din("w_selv", [C, NH * 32])
    w_proj = b.din("w_proj", [NH * DV, C])
    w1t = b.din("w1t", [C, HF]); b1d = b.din("b1", [1, HF])
    w2t = b.din("w2t", [HF, C]); b2d = b.din("b2", [1, C])
    wg1 = b.din("wg1", [C, 256]); bg1 = b.din("bg1", [256, 1])
    wg2 = b.din("wg2", [256, 40]); bg2 = b.din("bg2", [40, 1])
    we = b.din("we", [NE, C, C]); be = b.din("be", [NE, C])
    constsf = b.din("constsf", [128, 2 * T + 264], F32)
    constsk = b.din("constsk", [128, 2 * T], F32)
    out = nc.dram_tensor("out", [T, C], F32, kind="ExternalOutput").ap()

    resT = sbuf("resT", [128, NTC * C], F32)
    identf = sbuf("identf", [128, 128], F32)
    gcol3 = sbuf("gcol3", [128, 3], BF16)
    iota_p = sbuf("iota_p", [128, 1], F32)
    tiab = sbuf("tiab", [128, 264], BF16)
    identb = tiab[:, 0:128]
    Eseg4 = tiab[:, 128:132]
    Eall64 = tiab[:, 132:136]
    trimask = tiab[:, 136:264]
    scr = sbuf("scr", [128, 2 * C], F32)
    scr2 = sbuf("scr2", [128, 2 * C], F32)
    mcol = sbuf("mcol", [128, NTC], F32)
    rcol = sbuf("rcol", [128, NTC], F32)
    vcol = sbuf("vcol", [128, NTC], F32)
    qmaxcol = sbuf("qmaxcol", [128, NTC], F32)
    qssqcol = sbuf("qssqcol", [128, NTC], F32)
    bnst = sbuf("bnst", [128, 6], F32)
    wsm = sbuf("wsm", [128, 4 * NCC], F32)
    wsm_hi = sbuf("wsm_hi", [128, 4 * NCC], BF16)
    wsm_lo = sbuf("wsm_lo", [128, 4 * NCC], BF16)
    onesrow = sbuf("onesrow", [1, 128], BF16)

    TCs = [slice(tc * C, (tc + 1) * C) for tc in range(NTC)]
    R_SC, R_WRK, R_MSK, R_RANK = 0, 32, 64, 96

    def G(g):
        inc("gp", g.iota(iota_p[:], pattern=[[0, 1]], base=0, channel_multiplier=1,
               allow_small_or_imprecise_dtypes=True))
    GE(G)

    def G(g):
        inc("ldcb", g.dma_start(tiab[:], constsf[:, 2 * T:2 * T + 264]), 16)
        inc("ldcb", g.dma_start(identf[:], constsf[:, 2 * T:2 * T + 128]), 16)
    GE(G)

    def S(sy):
        inc("ldsm", sy.dma_start(wsm[:].rearrange("p (cc k) -> p cc k", cc=NCC), w_small.rearrange("(cc p) k -> p cc k", p=128)), 16)
    SY(S)

    def V(v):
        wt(v, "gp")
        v.memset(onesrow[:], 1.0)
        wt(v, "ldcb")
        b.bar(v, v.tensor_copy(identf[:], identb[:]))
        wt(v, "ldsm")
        b.bar(v, v.tensor_copy(wsm_hi[:], wsm[:]))
        b.bar(v, v.tensor_sub(scr[:, 0:4 * NCC], wsm[:], wsm_hi[:]))
        v.tensor_copy(wsm_lo[:], scr[:, 0:4 * NCC])
        inc("dve", v.engine_nop())
    VE(V)

    if stop_after == "init":
        def S(sy):
            wt(sy, "auxq")
            wt(sy, "xload")
            wt(sy, "dve")
            inc("outq", sy.dma_start(out.rearrange("(tc p) c -> p tc c", p=128), resT[:].rearrange("p (tc c) -> p tc c", tc=NTC)), 16)
            wt(sy, "outq")
            wt(sy, "stq", val("stq"))
        SY(S)
        b.es.close()
        return nc, b.dram, b.DBG

    stBC = ExitStack()
    xnF = sbuf("xnF", [128, NCC * T], BF16, stBC)
    selF = sbuf("selF", [128, NCC * KEEP], BF16, stBC)
    Ocmb = sbuf("Ocmb", [128, NTC * 512], BF16, stBC)

    stA = ExitStack()
    rows = sbuf("rows", [128, T], F32, stA)
    xnT = sbuf("xnT", [128, NTC * C], BF16, stA)
    stA2 = ExitStack()
    xlF_tc = sbuf("xlF_tc", [128, C], BF16, stA2)
    xn32 = scr2[:, 0:C]
    stP1 = ExitStack()
    p_trh = [psum("p_trh%d" % i, [128, 1024], BF16, stP1) for i in range(2)]
    p_trl = [psum("p_trl%d" % i, [128, 1024], BF16, stP1) for i in range(2)]
    p_sc = psum("p_sc", [4, T], F32, stP1)

    xc = sbuf("xc", [128, 2 * C], F32, stA2)

    def V(v):
        v.memset(rows[:], 0.0)
        inc("dve", v.engine_nop())
    VE(V)

    for tc in range(NTC):
        xbuf = xc[:, (tc % 2) * C:(tc % 2) * C + C]

        def S(sy, tc=tc):
            if tc >= 2:
                wt(sy, "xdone", 2 * (tc - 1))
            inc("xload", sy.dma_start(xc[:, (tc % 2) * C:(tc % 2) * C + C],
                                      x_in[tc * 128:(tc + 1) * 128, :]), 16)
        SY(S)

        def V(v, tc=tc, xbuf=xbuf):
            wt(v, "xload", (tc + 1) * 16)
            inc("xdone", v.reduce_sum(mcol[:, tc:tc + 1], xbuf, AX.X))
        VE(V)

        if tc % 8 == 7:
            def S(s, tc=tc):
                for t2 in range(tc - 7, tc + 1):
                    wt(s, "xload", (t2 + 1) * 16)
                    inc("act", s.activation(scr2[:, C:2 * C], xc[:, (t2 % 2) * C:(t2 % 2) * C + C],
                                            ACTF.Square, accum_out=vcol[:, t2:t2 + 1]))
                    inc("xdone", s.nop())
            SE(S)

    def V(v):
        wt(v, "act")
        b.bar(v, v.engine_nop())
        b.bar(v, v.tensor_scalar(mcol[:], mcol[:], 1.0 / C, None, ALU.mult))
        b.bar(v, v.tensor_mul(scr2[:, 0:NTC], mcol[:], mcol[:]))
        b.bar(v, v.scalar_tensor_tensor(vcol[:], vcol[:], 1.0 / C, scr2[:, 0:NTC],
                                        ALU.mult, ALU.subtract))
        v.tensor_scalar(scr2[:, 0:NTC], vcol[:], 1e-5, None, ALU.add)
        inc("dve", v.engine_nop())
    VE(V)

    def S(s):
        wt(s, "dve")
        inc("act", s.activation(scr2[:, NTC:2 * NTC], scr2[:, 0:NTC], ACTF.Sqrt))
    SE(S)

    def V(v):
        wt(v, "act")
        b.bar(v, v.reciprocal(rcol[:], scr2[:, NTC:2 * NTC]))
        inc("lnstats", v.engine_nop())
    VE(V)

    _p2act = []
    for tc in range(NTC):
        def S(sy, tc=tc):
            if tc == 0:
                wt(sy, "xdone")
            else:
                wt(sy, "xdone", 32 + tc)
            inc("xload", sy.dma_start(xc[:, (tc % 2) * C:(tc % 2) * C + C],
                                      x_in[tc * 128:(tc + 1) * 128, :]), 16)
        SY(S)

        def V(v, tc=tc):
            wt(v, "xload", (NTC + tc + 1) * 16)
            if tc == 0:
                wt(v, "lnstats")
            else:
                wt(v, "peS", tc)
                wt(v, "act", _p2act[tc - 1])
            v.tensor_scalar(xn32, xc[:, (tc % 2) * C:(tc % 2) * C + C],
                            mcol[:, tc:tc + 1], rcol[:, tc:tc + 1],
                            ALU.subtract, ALU.mult)
            v.tensor_copy(xnT[:, TCs[tc]], xn32)
            v.tensor_sub(xlF_tc[:], xn32, xnT[:, TCs[tc]])
            v.tensor_reduce(qmaxcol[:, tc:tc + 1], xn32, AX.X, ALU.max, apply_absolute_value=True)
            inc("xdone", v.engine_nop())
            inc("dve", v.engine_nop())
        VE(V)

        def S(s, tc=tc):
            wt(s, "dve")
            inc("act", s.activation(scr2[:, C:2 * C], xn32, ACTF.Square, accum_out=qssqcol[:, tc:tc + 1]))
        SE(S)
        _p2act.append(val("act"))

        buf = tc % 2

        def Tt(t, tc=tc, buf=buf):
            wt(t, "dve")
            if tc >= 2:
                wt(t, "evA", tc - 1)
            for cc in range(NCC):
                t.transpose(p_trh[buf][:, cc * 128:(cc + 1) * 128],
                            xnT[:, tc * C + cc * 128:tc * C + (cc + 1) * 128], identb)
            for cc in range(NCC):
                last = t.transpose(p_trl[buf][:, cc * 128:(cc + 1) * 128],
                                   xlF_tc[:, cc * 128:(cc + 1) * 128], identb)
            inc("peT", last)
        TE(Tt)

        def V(v, tc=tc, buf=buf):
            wt(v, "peT")
            v.tensor_copy(
                xnF[:].rearrange("p (cc t) -> p cc t", cc=NCC)[:, :, tc * 128:(tc + 1) * 128],
                p_trh[buf][:].rearrange("p (cc t) -> p cc t", cc=NCC))
            v.tensor_copy(xlF_tc[:].rearrange("p (cc t) -> p cc t", cc=NCC),
                          p_trl[buf][:].rearrange("p (cc t) -> p cc t", cc=NCC))
            inc("evA", v.engine_nop())
        VE(V)

        def Tt(t, tc=tc):
            wt(t, "evA")
            for cc in range(NCC):
                xh_tile = xnF[:, cc * T + tc * 128: cc * T + (tc + 1) * 128]
                xl_tile = xlF_tc[:, cc * 128:(cc + 1) * 128]
                t.matmul(p_sc[:, tc * 128:(tc + 1) * 128], wsm_hi[:, 4 * cc:4 * cc + 4], xh_tile,
                         start=(cc == 0), stop=False)
                t.matmul(p_sc[:, tc * 128:(tc + 1) * 128], wsm_lo[:, 4 * cc:4 * cc + 4], xh_tile,
                         start=False, stop=False)
                last = t.matmul(p_sc[:, tc * 128:(tc + 1) * 128], wsm_hi[:, 4 * cc:4 * cc + 4], xl_tile,
                                start=False, stop=(cc == NCC - 1))
            inc("peS", last)
        TE(Tt)

    def V(v):
        wt(v, "peS")
        v.tensor_copy(rows[0:4, :], p_sc[0:4, :])
        v.tensor_copy(rows[R_WRK:R_WRK + 1, :], p_sc[0:1, :])
        inc("scores", v.engine_nop())
    VE(V)

    stP1.close()

    stG = ExitStack()
    pG = psum("pG", [128, 512], F32, stG)

    def V(v):
        wt(v, "scores")
        b.bar(v, v.reduce_sum(scr[0:4, 0:1], rows[0:4, :], AX.X))
        inc("dve", v.engine_nop())
    VE(V)

    def Tt(t):
        wt(t, "dve")
        inc("peG", t.transpose(pG[0:1, 0:4], scr[0:4, 0:1], identf[0:4, 0:4]))
    TE(Tt)

    def V(v):
        wt(v, "peG")
        b.bar(v, v.tensor_scalar(scr2[0:1, 0:4], pG[0:1, 0:4], 1.0 / T, None, ALU.mult))
        b.bar(v, v.tensor_reduce(scr2[0:1, 8:9], scr2[0:1, 1:4], AX.X, ALU.max))
        b.bar(v, v.tensor_scalar(scr2[0:1, 12:15], scr2[0:1, 1:4], scr2[0:1, 8:9], None, ALU.subtract))
        inc("dve", v.engine_nop())
    VE(V)

    def S(s):
        wt(s, "dve")
        inc("act", s.activation(scr2[0:1, 16:19], scr2[0:1, 12:15], ACTF.Exp))
    SE(S)

    def V(v):
        wt(v, "act")
        b.bar(v, v.reduce_sum(scr2[0:1, 20:21], scr2[0:1, 16:19], AX.X))
        b.bar(v, v.reciprocal(scr2[0:1, 24:27], scr2[0:1, 16:19]))
        b.bar(v, v.tensor_scalar(scr2[0:1, 28:31], scr2[0:1, 24:27], scr2[0:1, 20:21], None, ALU.mult))
        b.bar(v, v.tensor_copy(scr.bitcast(BF16)[0:1, 2048:2051], scr2[0:1, 28:31]))
        inc("dve", v.engine_nop())
    VE(V)

    def Tt(t):
        wt(t, "dve")
        inc("peG", t.matmul(pG[:, 8:11], onesrow[:], scr.bitcast(BF16)[0:1, 2048:2051], start=True, stop=True))
    TE(Tt)

    def V(v):
        wt(v, "peG")
        b.bar(v, v.tensor_copy(gcol3[:], pG[:, 8:11]))
        inc("evG", v.engine_nop())
    VE(V)

    stG.close()


    b.dbg_dma("xnF", xnF[:], waits=("scores",))
    b.dbg_dma("mcol", mcol[:], waits=("scores",))
    b.dbg_dma("vcol", vcol[:], waits=("scores",))
    b.dbg_dma("rcol", rcol[:], waits=("scores",))
    b.dbg_dma("xnT", xnT[:], waits=("scores",))

    if stop_after == "lnloop":
        def S(sy):
            wt(sy, "scores")
            inc("outq", sy.dma_start(out.rearrange("(tc p) c -> p tc c", p=128), resT[:].rearrange("p (tc c) -> p tc c", tc=NTC)), 16)
            wt(sy, "outq")
            wt(sy, "stq", val("stq"))
        SY(S)
        stA.close()
        b.es.close()
        return nc, b.dram, b.DBG

    stA2.close()
    mx8 = sbuf("mx8", [1, 8], F32, stA)
    rankb = scr2.bitcast(BF16)[0:1, 2048:2048 + T]
    STOPTOPK = (stop_after == "topk")
    pselT = sbuf("pselT", [128, 2 * T], BF16, stA)
    psel = sbuf("psel", [128, NTC * KEEP], BF16, stA)
    stP2 = ExitStack()
    p_bc = psum("p_bc", [128, T], F32, stP2)

    def V(v):
        wt(v, "scores")
        for i in range(KEEP // 8):
            b.bar(v, v.max(out=mx8[:], in_=rows[R_WRK:R_WRK + 1, :]))
            v.match_replace(out=rows[R_WRK:R_WRK + 1, :], in_to_replace=mx8[:],
                            in_values=rows[R_WRK:R_WRK + 1, :], imm_value=-1e30)
        v.tensor_scalar(scr[32:33, 0:T], rows[R_WRK:R_WRK + 1, :], -1e29, None, ALU.is_le)
        v.tensor_copy(rows[R_MSK:R_MSK + 1, :], scr[32:33, 0:T])
        v.tensor_tensor_scan(rows[R_WRK:R_WRK + 1, :], scr[32:33, 0:T],
                             scr[32:33, 0:T], 0.0, ALU.add, ALU.bypass)
        v.tensor_copy(rows[R_RANK:R_RANK + 1, :], rows[R_WRK:R_WRK + 1, :])
        v.tensor_mul(rows[R_WRK:R_WRK + 1, :], rows[R_WRK:R_WRK + 1, :], scr[32:33, 0:T])
        v.scalar_tensor_tensor(rows[R_WRK:R_WRK + 1, :], scr[32:33, 0:T], -300.0,
                               rows[R_WRK:R_WRK + 1, :], ALU.mult, ALU.add)
        v.tensor_scalar(rows[R_WRK:R_WRK + 1, :], rows[R_WRK:R_WRK + 1, :], 300.0, None, ALU.add)
        v.tensor_copy(rankb, rows[R_WRK:R_WRK + 1, :])
        inc("dve", v.engine_nop())
    VE(V)

    b.dbg_dma("rows", rows[:], waits=("dve",))

    if STOPTOPK:
        def S(sy):
            wt(sy, "dve")
            inc("outq", sy.dma_start(out.rearrange("(tc p) c -> p tc c", p=128), resT[:].rearrange("p (tc c) -> p tc c", tc=NTC)), 16)
            wt(sy, "outq")
            wt(sy, "stq", val("stq"))
        SY(S)
        stP2.close()
        stA.close()
        b.es.close()
        return nc, b.dram, b.DBG

    def Tt(t):
        wt(t, "dve")
        for q4 in range(4):
            last = t.matmul(p_bc[:, q4 * 512:(q4 + 1) * 512], onesrow[:], rankb[0:1, q4 * 512:(q4 + 1) * 512],
                            start=True, stop=True)
        inc("peB", last)
    TE(Tt)

    def V(v):
        wt(v, "peB")
        v.tensor_scalar(scr[:, 0:1], iota_p[:], 1.0, None, ALU.add)
        b.bar(v, v.tensor_scalar(scr[:, 1:2], iota_p[:], 129.0, None, ALU.add))
        v.tensor_scalar(pselT[:, 0:T], p_bc[:], scr[:, 0:1], None, ALU.is_equal)
        v.tensor_scalar(pselT[:, T:2 * T], p_bc[:], scr[:, 1:2], None, ALU.is_equal)
        inc("evB", v.engine_nop())
    VE(V)

    p_trA2 = [psum("p_trA2_%d" % i, [128, 1024], BF16, stP2) for i in range(2)]
    for grp in range(4):
        h, jg = grp // 2, grp % 2
        gbuf = grp % 2

        def Tt(t, h=h, jg=jg, gbuf=gbuf, grp=grp):
            wt(t, "evB")
            if grp >= 2:
                wt(t, "evB2", grp - 1)
            for i in range(8):
                j = 8 * jg + i
                last = t.transpose(p_trA2[gbuf][:, i * 128:(i + 1) * 128],
                                   pselT[:, h * T + j * 128: h * T + (j + 1) * 128], identb)
            inc("peB", last)
        TE(Tt)

        def V(v, h=h, jg=jg, gbuf=gbuf):
            wt(v, "peB")
            v.tensor_copy(
                psel[:].rearrange("p (j r) -> p j r", r=KEEP)[:, 8 * jg:8 * jg + 8,
                                                              128 * h:128 * h + 128],
                p_trA2[gbuf][:].rearrange("p (i t) -> p i t", i=8))
            inc("evB2", v.engine_nop())
        VE(V)

    if stop_after == "psel":
        def S(sy):
            wt(sy, "evB2")
            inc("outq", sy.dma_start(out.rearrange("(tc p) c -> p tc c", p=128), resT[:].rearrange("p (tc c) -> p tc c", tc=NTC)), 16)
            wt(sy, "outq")
            wt(sy, "stq", val("stq"))
        SY(S)
        stP2.close()
        stA.close()
        b.es.close()
        return nc, b.dram, b.DBG

    p_sel = [psum("p_selg%d" % i, [128, 256], F32, stP2) for i in range(2)]
    for cs in range(NCC):
        def Tt(t, cs=cs):
            wt(t, "evB2")
            if cs >= 2:
                wt(t, "evSel", cs - 1)
            for tc in range(NTC):
                last = t.matmul(p_sel[cs % 2][:],
                                xnT[:, tc * C + cs * 128: tc * C + cs * 128 + 128],
                                psel[:, tc * KEEP:(tc + 1) * KEEP],
                                start=(tc == 0), stop=(tc == NTC - 1))
            inc("peSel", last)
        TE(Tt)

        def V(v, cs=cs):
            wt(v, "peSel")
            v.tensor_copy(selF[:, cs * KEEP:(cs + 1) * KEEP], p_sel[cs % 2][:])
            inc("evSel", v.engine_nop())
        VE(V)

    b.dbg_dma("selF", selF[:], waits=("evSel",))

    stP2.close()
    stA.close()

    stB = ExitStack()
    c4s4 = sbuf("c4s4", [128, 2 * T], BF16, stB)
    C4s = c4s4[:, 0:T]; S4s = c4s4[:, T:2 * T]

    def G(g):
        wt(g, "evSel")
        wt(g, "peSel")
        inc("ldcb", g.dma_start(c4s4[:], constsf[:, 0:2 * T]), 16)
    GE(G)

    nqb = sbuf("nqb", [96, T], BF16, stB)
    ckvb = sbuf("ckvb", [32, T], BF16, stB)
    krr = scr2.bitcast(BF16)[0:64, 2048:2048 + T]
    sqs = scr.bitcast(BF16)[:, 2048:2048 + T]
    qpkn = sbuf("qpkn", [128, T], BF16, stB)
    qpk1 = sbuf("qpk1", [128, T], BF16, stB)
    qpk2 = sbuf("qpk2", [128, T], BF16, stB)
    Qq = sbuf("Qq", [128, 4 * T], BF16, stB)
    Kq = sbuf("Kq", [128, T], BF16, stB)
    V1T = sbuf("V1T", [128, 16 * 4 * 33], BF16, stB)
    PTa = sbuf("PTa", [128, 16 * 512], BF16, stB)
    KSq = PTa[:, 4096:4096 + 4 * KEEP]
    VST = PTa[:, 5200:5200 + 2 * 4 * 33]
    Bcol = sbuf("Bcol", [128, 4], F32, stB)
    bq4 = sbuf("bq4", [4, 8], F32, stB)
    wq3 = sbuf("wq3", [96, 3 * 128], BF16, stB)
    wk1q = sbuf("wk1q", [32, 2 * 128], BF16, stB)
    wwX = sbuf("wwX", [128, NCC * 128], BF16, stB)
    onescolb = sbuf("onescolb", [128, 1], BF16, stB)
    NQC = 4
    SC2B = scr2.bitcast(BF16)

    def V(v):
        v.memset(onescolb[:], 1.0)
        inc("dve", v.engine_nop())
    VE(V)

    stN = ExitStack()
    wcq_s = PTa[:, 4096:4096 + NCC * 96]
    wckv_s = PTa[:, 4864:4864 + NCC * 32]
    wkr_s = PTa[:, 5120:5120 + NCC * 64]
    kcst = PTa[:, 0:2 * T]
    KCS = kcst[:, 0:T]; KCS2 = kcst[:, T:2 * T]
    gq_s = sbuf("gq_s", [96, 1], F32, stN)
    gkv_s = sbuf("gkv_s", [32, 1], F32, stN)
    pA = psum("pA", [128, T], F32, stN)
    pB = psum("pB", [128, T], F32, stN)

    def S(sy):
        wt(sy, "evSel")
        inc("ldq2", sy.dma_start(gq_s[:], g_qnorm), 16)
        inc("ldq2", sy.dma_start(gkv_s[:], g_kvnorm), 16)
    SY(S)

    def G(g):
        wt(g, "evSel")
        wt(g, "peSel")
        inc("ldb", g.dma_start(wcq_s[:].rearrange("p (cc k) -> p cc k", cc=NCC),
                               w_cq.rearrange("(cc p) k -> p cc k", p=128)), 16)
        inc("ldb", g.dma_start(wckv_s[:].rearrange("p (cc k) -> p cc k", cc=NCC),
                               w_ckv.rearrange("(cc p) k -> p cc k", p=128)), 16)
        inc("ldb", g.dma_start(wkr_s[:].rearrange("p (cc k) -> p cc k", cc=NCC),
                               w_krope.rearrange("(cc p) k -> p cc k", p=128)), 16)
        inc("ldb", g.dma_start(kcst[:], constsk), 16)
    GE(G)

    def lora_norm(dst, w_s, kdim, g_ap):
        def Tt(t):
            wt(t, "ldb")
            wt(t, "evSel")
            wt(t, "evG")
            for qc in range(NQC):
                for cc in range(NCC):
                    last = t.matmul(pA[0:kdim, qc * 512:(qc + 1) * 512],
                                    w_s[:, cc * kdim:(cc + 1) * kdim],
                                    xnF[:, cc * T + qc * 512: cc * T + (qc + 1) * 512],
                                    start=(cc == 0), stop=(cc == NCC - 1))
            inc("peN", last)
        TE(Tt)

        def S(s):
            wt(s, "peN")
            s.activation(sqs[0:kdim, 0:T], pA[0:kdim, :], ACTF.Square)
            inc("act", s.activation(qpkn[0:kdim, :], pA[0:kdim, :], ACTF.Copy))
        SE(S)

        def Tt(t):
            wt(t, "act")
            for qc in range(NQC):
                last = t.matmul(pB[0:1, qc * 512:(qc + 1) * 512], onescolb[0:kdim, 0:1],
                                sqs[0:kdim, qc * 512:(qc + 1) * 512], start=True, stop=True)
            inc("peN", last)
        TE(Tt)

        def V(v):
            wt(v, "peN")
            v.tensor_scalar(scr2[0:1, 0:T], pB[0:1, 0:T], 1.0 / kdim, 1e-6, ALU.mult, ALU.add)
            inc("dve", v.engine_nop())
        VE(V)

        def S(s):
            wt(s, "dve")
            inc("act", s.activation(scr2[0:1, 1024:1024 + T] if False else scr2[0:1, 0:T],
                                    scr2[0:1, 0:T], ACTF.Sqrt))
        SE(S)

        def V(v):
            wt(v, "act")
            b.bar(v, v.reciprocal(scr2[0:1, 0:T], scr2[0:1, 0:T]))
            b.bar(v, v.tensor_copy(SC2B[0:1, 2048:2048 + T], scr2[0:1, 0:T]))
            inc("dve", v.engine_nop())
        VE(V)

        def Tt(t):
            wt(t, "dve")
            for qc in range(NQC):
                last = t.matmul(pB[:, qc * 512:(qc + 1) * 512], onesrow[:],
                                SC2B[0:1, 2048 + qc * 512:2048 + (qc + 1) * 512],
                                start=True, stop=True)
            inc("peN", last)
        TE(Tt)

        def V(v):
            wt(v, "peN")
            v.scalar_tensor_tensor(dst[:], qpkn[0:kdim, :], g_ap, pB[0:kdim, 0:T],
                                   ALU.mult, ALU.mult)
            inc("evN", v.engine_nop())
        VE(V)

    lora_norm(nqb, wcq_s, 96, gq_s[0:96, 0:1])
    lora_norm(ckvb, wckv_s, 32, gkv_s[0:32, 0:1])

    def Tt(t):
        wt(t, "evN")
        for qc in range(NQC):
            for cc in range(NCC):
                last = t.matmul(pA[0:64, qc * 512:(qc + 1) * 512],
                                wkr_s[:, cc * 64:(cc + 1) * 64],
                                xnF[:, cc * T + qc * 512: cc * T + (qc + 1) * 512],
                                start=(cc == 0), stop=(cc == NCC - 1))
        inc("peN", last)
    TE(Tt)

    def V(v):
        wt(v, "peN")
        v.tensor_mul(krr[0:32, :], pA[0:32, 0:T], KCS[0:32, :])
        v.tensor_mul(scr[0:32, 0:T], pA[32:64, 0:T], KCS[32:64, :])
        v.tensor_sub(krr[0:32, :], krr[0:32, :], scr[0:32, 0:T])
        v.tensor_mul(krr[32:64, :], pA[0:32, 0:T], KCS2[0:32, :])
        v.tensor_mul(scr[32:64, 0:T], pA[32:64, 0:T], KCS2[32:64, :])
        v.tensor_add(krr[32:64, :], krr[32:64, :], scr[32:64, 0:T])
        inc("evN", v.engine_nop())
    VE(V)

    stN.close()

    def rope_packs(pX1, pX2, dst1, dst2, ncols, sem_in, sem_out):
        def V(v):
            wt(v, sem_in)
            v.tensor_mul(dst1[:, 0:ncols], pX1[:, 0:ncols], C4s[:, 0:ncols])
            v.tensor_mul(scr[:, 0:ncols], pX2[:, 0:ncols], S4s[:, 0:ncols])
            v.tensor_sub(dst1[:, 0:ncols], dst1[:, 0:ncols], scr[:, 0:ncols])
            v.tensor_mul(dst2[:, 0:ncols], pX1[:, 0:ncols], S4s[:, 0:ncols])
            v.tensor_mul(scr[:, 0:ncols], pX2[:, 0:ncols], C4s[:, 0:ncols])
            v.tensor_add(dst2[:, 0:ncols], dst2[:, 0:ncols], scr[:, 0:ncols])
            inc(sem_out, v.engine_nop())
        VE(V)

    def seg_sq(packs, ncols, out_col, extra=None, sem_pe="peQ", sem_ve="evQ"):
        items = [(pk, Eseg4) for pk in packs] + ([extra] if extra else [])
        for i, (pk, EM) in enumerate(items):
            def V(v, pk=pk):
                wt(v, sem_pe, val(sem_pe))
                kd = pk.shape[0]
                v.tensor_mul(sqs[0:kd, 0:ncols], pk[:, 0:ncols], pk[:, 0:ncols])
                inc(sem_ve, v.engine_nop())
            VE(V)

            def Tt(t, i=i, EM=EM, pk=pk):
                wt(t, sem_ve)
                kd = pk.shape[0]
                nch = (ncols + 511) // 512
                for qc in range(nch):
                    w = min(512, ncols - qc * 512)
                    last = t.matmul(pB[0:4, qc * 512:qc * 512 + w], EM[0:kd, :],
                                    sqs[0:kd, qc * 512:qc * 512 + w],
                                    start=(i == 0), stop=(i == len(items) - 1))
                inc(sem_pe, last)
            TE(Tt)

        def V(v):
            wt(v, sem_pe)
            b.bar(v, v.tensor_reduce(bq4[0:4, out_col:out_col + 1], pB[0:4, 0:ncols], AX.X, ALU.max))
            inc(sem_ve, v.engine_nop())
        VE(V)

    for qq in range(4):
        stP = ExitStack()
        pA = psum("pAq%d" % qq, [128, T], F32, stP)
        pB = psum("pBq%d" % qq, [128, T], F32, stP)

        def G(g, qq=qq):
            wt(g, "evQ", val("evQ"))
            inc("ldb", g.dma_start(wq3[:, 0:128], wq_nope[:, qq * 128:(qq + 1) * 128]), 16)
            inc("ldb", g.dma_start(wq3[:, 128:256], wq_x1[:, qq * 128:(qq + 1) * 128]), 16)
            inc("ldb", g.dma_start(wq3[:, 256:384], wq_x2[:, qq * 128:(qq + 1) * 128]), 16)
            inc("ldb", g.dma_start(wk1q[:, 0:128], wk_nope[:, qq * 128:(qq + 1) * 128]), 16)
            inc("ldb", g.dma_start(wk1q[:, 128:256], w_dv[:, qq * 128:(qq + 1) * 128]), 16)
        GE(G)

        def Tt(t):
            wt(t, "ldb")
            wt(t, "evN")
            wt(t, "evQ", val("evQ"))
            wt(t, "evO", val("evO"))
            for qc in range(NQC):
                t.matmul(pA[:, qc * 512:(qc + 1) * 512], wq3[:, 128:256],
                         nqb[:, qc * 512:(qc + 1) * 512], start=True, stop=True)
            for qc in range(NQC):
                last = t.matmul(pB[:, qc * 512:(qc + 1) * 512], wq3[:, 256:384],
                                nqb[:, qc * 512:(qc + 1) * 512], start=True, stop=True)
            inc("peQ", last)
        TE(Tt)

        rope_packs(pA, pB, qpk1, qpk2, T, "peQ", "evQ")

        def Tt(t):
            wt(t, "evQ")
            for qc in range(NQC):
                last = t.matmul(pA[:, qc * 512:(qc + 1) * 512], wq3[:, 0:128],
                                nqb[:, qc * 512:(qc + 1) * 512], start=True, stop=True)
            inc("peQ", last)
        TE(Tt)

        def V(v):
            wt(v, "peQ")
            v.tensor_copy(qpkn[:], pA[:])
            inc("evQ", v.engine_nop())
        VE(V)

        seg_sq([qpkn, qpk1, qpk2], T, 0, sem_pe="peQ", sem_ve="evQ")

        def S(sy):
            wt(sy, "evQ")
            for hh in range(4):
                inc("asmq", sy.dma_start(Qq[0:32, hh * T:(hh + 1) * T], qpkn[32 * hh:32 * hh + 32, :]), 16)
                inc("asmq", sy.dma_start(Qq[32:64, hh * T:(hh + 1) * T], qpk1[32 * hh:32 * hh + 32, :]), 16)
                inc("asmq", sy.dma_start(Qq[64:96, hh * T:(hh + 1) * T], qpk2[32 * hh:32 * hh + 32, :]), 16)
        SY(S)

        def Tt(t):
            wt(t, "evQ")
            for qc in range(NQC):
                last = t.matmul(pA[:, qc * 512:(qc + 1) * 512], wk1q[:, 0:128],
                                ckvb[:, qc * 512:(qc + 1) * 512], start=True, stop=True)
            inc("peQ", last)
        TE(Tt)

        def V(v):
            wt(v, "peQ")
            wt(v, "asmq")
            v.tensor_copy(qpkn[:], pA[:])
            inc("evQ", v.engine_nop())
        VE(V)

        seg_sq([qpkn], T, 1, extra=(krr, Eall64), sem_pe="peQ", sem_ve="evQ")

        def Tt(t):
            wt(t, "evQ")
            for qc in range(NQC):
                last = t.matmul(pA[:, qc * 512:(qc + 1) * 512], wk1q[:, 128:256],
                                ckvb[:, qc * 512:(qc + 1) * 512], start=True, stop=True)
            inc("peQ", last)
        TE(Tt)

        def V(v):
            wt(v, "peQ")
            wt(v, "asmq")
            v.tensor_copy(qpk1[:], pA[:])
            inc("evQ", v.engine_nop())
        VE(V)

        stP.close()

        def attn_branch(qq, br, ktn, ncols, Ktiles, vtab, vn, causal):
            def kslice(j, hh, kt):
                if causal:
                    return Kq[0:96, kt * 128:(kt + 1) * 128]
                return KSq[0:96, hh * KEEP + kt * 128: hh * KEEP + (kt + 1) * 128]
            kcol = {0: 1, 2: 2, 1: 3}[br]

            def V(v):
                wt(v, "evVT", val("evVT"))
                b.bar(v, v.tensor_mul(scr2[0:4, 32:33], bq4[0:4, 0:1], bq4[0:4, kcol:kcol + 1]))
                inc("dve", v.engine_nop())
            VE(V)

            def S(s):
                wt(s, "dve")
                inc("act", s.activation(scr2[0:4, 33:34], scr2[0:4, 32:33], ACTF.Sqrt))
            SE(S)

            def V(v):
                wt(v, "act")
                b.bar(v, v.tensor_scalar(scr2[0:4, 34:35], scr2[0:4, 33:34], -SQK, None, ALU.mult))
                inc("dve", v.engine_nop())
            VE(V)

            def Tt(t):
                wt(t, "dve")
                wt(t, "evQ", val("evQ"))
                wt(t, "evO", val("evO"))
                inc("peB2", t.transpose(pS[0][0:1, 1000:1004], scr2[0:4, 34:35], identf[0:4, 0:4]))
            TE(Tt)

            def V(v):
                wt(v, "peB2")
                b.bar(v, v.tensor_copy(scr.bitcast(BF16)[0:1, 3800:3804], pS[0][0:1, 1000:1004]))
                inc("dve", v.engine_nop())
            VE(V)

            def Tt(t):
                wt(t, "dve")
                inc("peB2", t.matmul(pS[1][:, 1000:1004], onesrow[:], scr.bitcast(BF16)[0:1, 3800:3804],
                                     start=True, stop=True))
            TE(Tt)

            def V(v):
                wt(v, "peB2")
                b.bar(v, v.tensor_copy(Bcol[:], pS[1][:, 1000:1004]))
                inc("dve", v.engine_nop())
            VE(V)

            for hh in range(4):
                hq = hh
                h_g = qq * 4 + hh
                def S(sy, hh=hh):
                    wt(sy, "evQ", val("evQ"))
                    wt(sy, "peO", val("peO"))
                    wt(sy, "expS", val("expS"))
                    for part, buf in Ktiles:
                        inc("asmq", sy.dma_start(
                            Kq[part, 0:ncols],
                            buf[32 * hh:32 * hh + 32, 0:ncols] if buf.shape[0] == 128
                            else buf[:, 0:ncols]), 16)
                SY(S)

                if True:
                    exb = val("expS")
                    psb = val("peS2")
                    pob = val("peO")
                    sched = []
                    cum = 0
                    for qb in range(4):
                        q0 = qb * 512
                        ktmax = min(ktn, 4 * qb + 4) if causal else ktn
                        npair = (ktmax + 1) // 2
                        for jp, ktp in enumerate(range(0, ktmax, 2)):
                            gp = cum + jp

                            def Tt(t, hh=hh, qb=qb, ktp=ktp, ktmax=ktmax, gp=gp, q0=q0):
                                if ktp == 0 and qb == 0:
                                    wt(t, "asmq")
                                    wt(t, "evVT", val("evVT"))
                                par = (ktp // 2) % 2
                                if exb + gp - 1 > 0:
                                    wt(t, "expS", exb + gp - 1)
                                for kk in range(2):
                                    kt = ktp + kk
                                    if kt >= ktmax:
                                        break
                                    last = t.matmul(
                                        pS[par][:, kk * 512:(kk + 1) * 512],
                                        kslice(0, hh, kt),
                                        Qq[0:96, hh * T + q0:hh * T + q0 + 512],
                                        start=True, stop=True)
                                inc("peS2", last)
                            TE(Tt)
                            sched.append((qb, ktp, gp, min(2, ktmax - ktp)))

                        if causal:
                            def V(v, qb=qb, cum=cum, npair=npair, exb=exb):
                                wt(v, "expS", exb + cum + npair)
                                for qs in range(4):
                                    qt_g = 4 * qb + qs
                                    v.tensor_mul(PTa[:, qt_g * 512 + qs * 128:qt_g * 512 + qs * 128 + 128],
                                                 PTa[:, qt_g * 512 + qs * 128:qt_g * 512 + qs * 128 + 128],
                                                 trimask)
                                inc("dveM", v.engine_nop())
                            VE(V)

                        def Tt(t, qb=qb, hq=hq, causal=causal, ktn=ktn, vtab=vtab,
                               cum=cum, npair=npair, exb=exb):
                            wt(t, "expS", exb + cum + npair)
                            if qb == 0:
                                wt(t, "evO", val("evO"))
                            if causal:
                                wt(t, "dveM")
                            for qs in range(4):
                                qt_g = 4 * qb + qs
                                nkt = (qt_g + 1) if causal else ktn
                                half = qt_g // 8
                                col = 33 * (qt_g % 8)
                                for kt in range(nkt):
                                    last = t.matmul(
                                        pO[half][:, col:col + 33],
                                        PTa[:, kt * 512 + qs * 128:kt * 512 + qs * 128 + 128],
                                        vtab[:, (kt * 4 + hq) * 33:(kt * 4 + hq) * 33 + 33],
                                        start=(kt == 0), stop=(kt == nkt - 1))
                            inc("peO", last)
                        TE(Tt)
                        cum += npair

                    def S(s, hh=hh, sched=tuple(sched), exb=exb, psb=psb, pob=pob):
                        first = True
                        for (qb, ktp, gp, nk) in sched:
                            if ktp == 0:
                                wt(s, "peO", pob + qb)
                            wt(s, "peS2", psb + gp + 1)
                            if first and hh == 0:
                                wt(s, "dve")
                            first = False
                            par = (ktp // 2) % 2
                            inc("expS", s.activation(
                                PTa[:, ktp * 512:(ktp + nk) * 512],
                                pS[par][:, 0:nk * 512], ACTF.Exp,
                                bias=Bcol[:, hh:hh + 1], scale=SQK))
                    SE(S)

                    def V(v, br=br, h_g=h_g):
                        wt(v, "peO")
                        for half in range(2):
                            b.bar(v, v.reciprocal(
                                scr[:, 120 + half * 8:128 + half * 8],
                                pO[half][:].rearrange("p (q c) -> p q c", c=33)[:, :, 32:33].squeeze(2)))
                        for qt in range(NTC):
                            half, q8 = qt // 8, qt % 8
                            dst = Ocmb[:, qt * 512 + 32 * h_g:qt * 512 + 32 * h_g + 32]
                            src = pO[half][:, 33 * q8:33 * q8 + 32]
                            zs = scr[:, 120 + half * 8 + q8:121 + half * 8 + q8]
                            if br == 0:
                                v.tensor_scalar(dst, src, zs, None, ALU.mult)
                            else:
                                v.scalar_tensor_tensor(dst, src, zs, dst, ALU.mult, ALU.add)
                        inc("evO", v.engine_nop())
                    VE(V)

        stL = ExitStack()
        pS = [psum("pSq%d_%d" % (qq, i), [128, 1024], F32, stL) for i in range(2)]
        pO = [psum("pOq%d_%d" % (qq, i), [128, 264], F32, stL) for i in range(2)]
        pVT = psum("pVTq%d" % qq, [128, 1024], BF16, stL)

        def vt_build(vtab, src_pack, nkt, invg_slice):
            for ktg in range(0, nkt, 8):
                def Tt(t, ktg=ktg, nkt=nkt, src_pack=src_pack):
                    wt(t, "evVT", val("evVT"))
                    wt(t, "evQ", val("evQ"))
                    wt(t, "evO", val("evO"))
                    for i in range(min(8, nkt - ktg)):
                        last = t.transpose(pVT[:, i * 128:(i + 1) * 128],
                                           src_pack[:, (ktg + i) * 128:(ktg + i + 1) * 128], identb)
                    inc("peVT", last)
                TE(Tt)

                def V(v, ktg=ktg, nkt=nkt, vtab=vtab):
                    wt(v, "peVT")
                    nn = min(8, nkt - ktg)
                    v.tensor_copy(
                        vtab[:].rearrange("p (k h c) -> p k h c", h=4, c=33)[:, ktg:ktg + nn, :, 0:32],
                        pVT[:, 0:nn * 128].rearrange("p (k h c) -> p k h c", h=4, c=32))
                    inc("evVT", v.engine_nop())
                VE(V)

            def V(v, vtab=vtab, nkt=nkt, invg_slice=invg_slice):
                wt(v, "evVT")
                v.tensor_copy(
                    vtab[:].rearrange("p (k c) -> p k c", c=33)[:, 0:nkt * 4, 32:33].squeeze(2),
                    invg_slice.to_broadcast([128, nkt * 4]))
                inc("evVT", v.engine_nop())
            VE(V)

        vt_build(V1T, qpk1, 16, gcol3[:, 0:1])

        attn_branch(qq, 0, 16, T, [(slice(0, 32), qpkn), (slice(32, 96), krr)], V1T, 1, True)

        stL.close()

        stP2b = ExitStack()
        pA = psum("pA2q%d" % qq, [128, T], F32, stP2b)
        pB = psum("pB2q%d" % qq, [128, T], F32, stP2b)

        def wwload(g, wsrc, qq=qq):
            inc("ldb", g.dma_start(wwX[:].rearrange("p (cc k) -> p cc k", k=128),
                                   wsrc.rearrange("(cc p) k -> p cc k", p=128)[:, :, qq * 128:(qq + 1) * 128]), 16)

        def wproj_T(dst_psum, wait_ld=True, extra_waits=()):
            def Tt(t):
                if wait_ld:
                    wt(t, "ldb")
                for w in extra_waits:
                    wt(t, w, val(w))
                for qc in range(NQC):
                    for cc in range(NCC):
                        last = t.matmul(dst_psum[:, qc * 512:(qc + 1) * 512], wwX[:, cc * 128:(cc + 1) * 128],
                                        xnF[:, cc * T + qc * 512:cc * T + (qc + 1) * 512],
                                        start=(cc == 0), stop=(cc == NCC - 1))
                inc("peQ", last)
            TE(Tt)

        def G(g, qq=qq):
            wt(g, "evQ", val("evQ"))
            wt(g, "evO", val("evO"))
            wwload(g, ww_x1)
        GE(G)

        wproj_T(pA, extra_waits=("evO", "evQ"))

        def G(g):
            wt(g, "peQ")
            wwload(g, ww_x2)
        GE(G)

        wproj_T(pB)

        def G(g):
            wt(g, "peQ")
            wwload(g, ww_nope)
        GE(G)

        rope_packs(pA, pB, qpk1, qpk2, T, "peQ", "evQ")

        wproj_T(pA, extra_waits=("evQ",))

        def V(v):
            wt(v, "peQ")
            v.tensor_copy(qpkn[:], pA[:])
            inc("evQ", v.engine_nop())
        VE(V)

        def G(g):
            wt(g, "peQ")
            wwload(g, w_winv)
        GE(G)

        seg_sq([qpkn, qpk1, qpk2], T, 2, sem_pe="peQ", sem_ve="evQ")

        wproj_T(pA, extra_waits=("evQ",))

        vwpack = sqs

        def V(v):
            wt(v, "peQ")
            v.tensor_copy(vwpack[0:128, :], pA[:])
            inc("evQ", v.engine_nop())
        VE(V)

        stP2b.close()

        stL2 = ExitStack()
        pS = [psum("pS3q%d_%d" % (qq, i), [128, 1024], F32, stL2) for i in range(2)]
        pO = [psum("pO3q%d_%d" % (qq, i), [128, 264], F32, stL2) for i in range(2)]
        pVT = psum("pVT3q%d" % qq, [128, 1024], BF16, stL2)

        vt_build(V1T, vwpack, 16, gcol3[:, 2:3])
        attn_branch(qq, 2, 16, T, [(slice(0, 32), qpkn), (slice(32, 64), qpk1), (slice(64, 96), qpk2)],
                    V1T, 1, True)

        def sproj_T(dst_psum, extra_waits=()):
            def Tt(t):
                wt(t, "ldb")
                for w in extra_waits:
                    wt(t, w, val(w))
                for cc in range(NCC):
                    last = t.matmul(dst_psum[:, 0:KEEP], wwX[:, cc * 128:(cc + 1) * 128],
                                    selF[:, cc * KEEP:(cc + 1) * KEEP],
                                    start=(cc == 0), stop=(cc == NCC - 1))
                inc("peQ", last)
            TE(Tt)

        def G(g, qq=qq):
            wt(g, "evQ", val("evQ"))
            wt(g, "evO", val("evO"))
            wwload(g, ws_x1)
        GE(G)

        sproj_T(pS[0], extra_waits=("evQ", "evO"))

        def G(g):
            wt(g, "peQ")
            wwload(g, ws_x2)
        GE(G)

        sproj_T(pS[1])

        def G(g):
            wt(g, "peQ")
            wwload(g, ws_nope)
        GE(G)

        rope_packs(pS[0], pS[1], qpk1, qpk2, KEEP, "peQ", "evQ")

        sproj_T(pS[0], extra_waits=("evQ",))

        def V(v):
            wt(v, "peQ")
            v.tensor_copy(qpkn[:, 0:KEEP], pS[0][:, 0:KEEP])
            inc("evQ", v.engine_nop())
        VE(V)

        def V(v):
            wt(v, "evQ")
            v.tensor_mul(sqs[:, 0:KEEP], qpkn[:, 0:KEEP], qpkn[:, 0:KEEP])
            inc("evQ", v.engine_nop())
        VE(V)

        def Tt(t):
            wt(t, "evQ")
            t.matmul(pS[1][0:4, 0:KEEP], Eseg4, sqs[:, 0:KEEP], start=True, stop=False)
            inc("peQ", t.matmul(pS[1][0:4, 0:KEEP], Eseg4, sqs[:, 0:KEEP], start=False, stop=False))
        TE(Tt)

        for pk in (qpk1, qpk2):
            def V(v, pk=pk):
                wt(v, "peQ")
                v.tensor_mul(sqs[:, 0:KEEP], pk[:, 0:KEEP], pk[:, 0:KEEP])
                inc("evQ", v.engine_nop())
            VE(V)

            def Tt(t, pk=pk):
                wt(t, "evQ")
                last = t.matmul(pS[1][0:4, 0:KEEP], Eseg4, sqs[:, 0:KEEP],
                                start=False, stop=(pk is qpk2))
                inc("peQ", last)
            TE(Tt)

        def V(v):
            wt(v, "peQ")
            b.bar(v, v.tensor_reduce(bq4[0:4, 3:4], pS[1][0:4, 0:KEEP], AX.X, ALU.max))
            for hh in range(4):
                v.tensor_copy(KSq[0:32, hh * KEEP:(hh + 1) * KEEP], qpkn[32 * hh:32 * hh + 32, 0:KEEP])
                v.tensor_copy(KSq[32:64, hh * KEEP:(hh + 1) * KEEP], qpk1[32 * hh:32 * hh + 32, 0:KEEP])
                v.tensor_copy(KSq[64:96, hh * KEEP:(hh + 1) * KEEP], qpk2[32 * hh:32 * hh + 32, 0:KEEP])
            inc("evQ", v.engine_nop())
        VE(V)

        def G(g):
            wt(g, "peQ")
            wwload(g, w_selv)
        GE(G)

        sproj_T(pS[0], extra_waits=("evQ",))

        def V(v):
            wt(v, "peQ")
            v.tensor_copy(vwpack[0:128, 0:KEEP], pS[0][:, 0:KEEP])
            inc("evQ", v.engine_nop())
        VE(V)

        vt_build(VST, vwpack, 2, gcol3[:, 1:2])
        attn_branch(qq, 1, 2, KEEP, [], VST, 1, False)

        stL2.close()

    b.dbg_dma("nqb", nqb[:], waits=("evN",))
    b.dbg_dma("ckvb", ckvb[:], waits=("evN",))
    b.dbg_dma("krr", krr[:], waits=("evN",))
    b.dbg_dma("Qq", Qq[0:96, :], waits=("evO",))
    b.dbg_dma("Ocmb", Ocmb[:], waits=("evO",))

    stB.close()

    w1i_d = nc.dram_tensor("w1i", [C, HF], BF16, kind="Internal").ap()
    w2i_d = nc.dram_tensor("w2i", [HF, C], BF16, kind="Internal").ap()
    sT_d = nc.dram_tensor("sTd", [T, C], BF16, kind="Internal").ap()
    TCH = 512
    NTCH = T // TCH

    def S(sy):
        inc("rstI", sy.dma_start(resT[:].rearrange("p (tc c) -> p tc c", tc=NTC),
                                 x_in.rearrange("(tc p) c -> p tc c", p=128)), 16)
    SY(S)

    stC1 = ExitStack()
    OT = sbuf("OT", [128, 4 * T], BF16, stC1)
    wprojb = sbuf("wprojb", [128, 4 * C], BF16, stC1)
    pstg = sbuf("pstg", [128, 2 * 512], BF16, stC1)
    stCP = ExitStack()
    pT1 = [psum("pT1%d" % i, [128, 1024], BF16, stCP) for i in range(2)]
    pP = [psum("pP%d" % i, [128, 512], F32, stCP) for i in range(2)]

    def G(g):
        wt(g, "evO")
        inc("ldc2", g.dma_start(wprojb[:].rearrange("p (j c) -> p j c", j=4),
                                w_proj.rearrange("(j p) c -> p j c", p=128)), 16)
    GE(G)

    for grp in range(8):
        def Tt(t, grp=grp):
            wt(t, "evO")
            if grp >= 2:
                wt(t, "evCT", grp - 1)
            for i in range(8):
                gi = grp * 8 + i
                qt, j = gi // 4, gi % 4
                last = t.transpose(pT1[grp % 2][:, i * 128:(i + 1) * 128],
                                   Ocmb[:, qt * 512 + j * 128:qt * 512 + (j + 1) * 128], identb)
            inc("peC", last)
        TE(Tt)

        def V(v, grp=grp):
            wt(v, "peC")
            for i in range(8):
                gi = grp * 8 + i
                qt, j = gi // 4, gi % 4
                v.tensor_copy(OT[:, j * T + qt * 128:j * T + (qt + 1) * 128],
                              pT1[grp % 2][:, i * 128:(i + 1) * 128])
            inc("evCT", v.engine_nop())
        VE(V)

    pair = 0
    for tile in range(NTC):
        for cb in range(2):
            def Tt(t, tile=tile, cb=cb, pair=pair):
                if pair == 0:
                    wt(t, "evCT")
                    wt(t, "ldc2")
                if pair >= 2:
                    wt(t, "evCs", pair - 1)
                for j in range(4):
                    last = t.matmul(pP[pair % 2][:],
                                    OT[:, j * T + tile * 128:j * T + (tile + 1) * 128],
                                    wprojb[:, j * C + cb * 512:j * C + (cb + 1) * 512],
                                    start=(j == 0), stop=(j == 3))
                inc("peC", last)
            TE(Tt)

            def V(v, pair=pair):
                wt(v, "peC")
                if pair >= 2:
                    wt(v, "sTo", 16 * (pair - 1))
                v.tensor_copy(pstg[:, (pair % 2) * 512:(pair % 2) * 512 + 512], pP[pair % 2][:])
                inc("evCs", v.engine_nop())
            VE(V)

            def S(sy, tile=tile, cb=cb, pair=pair):
                wt(sy, "evCs", pair + 1)
                inc("sTo", sy.dma_start(
                    sT_d[tile * 128:(tile + 1) * 128, cb * 512:(cb + 1) * 512],
                    pstg[:, (pair % 2) * 512:(pair % 2) * 512 + 512]), 16)
            SY(S)
            pair += 1

    b.dbg_dma("sTt", sT_d, waits=("sTo",))

    stCP.close()
    stC1.close()
    stBC.close()

    stD = ExitStack()
    U = sbuf("U", [128, 24576], BF16, stD)
    Ub = U
    h1B = sbuf("h1B", [128, 4 * HF], BF16, stD)
    yac = sbuf("yac", [128, 4096], F32, stD)
    sTb = sbuf("sTb", [128, 2 * 512], BF16, stD)
    xq = sbuf("xq", [128, 2 * C], F32, stD)
    b1b = sbuf("b1b", [1, HF], BF16, stD)
    b2b = sbuf("b2b", [1, C], BF16, stD)
    rowA = sbuf("rowA", [16, 2688], BF16, stD)
    A1R = rowA[0:1, 0:2048]
    A2R = rowA[0:1, 2048:2560]
    rstg = rowA[0:16, 2560:2688]
    ki32 = sbuf("ki32", [128, 1024], I32, stD)
    wcols = sbuf("wcols", [128, 80], F32, stD)
    fcols = sbuf("fcols", [128, 64], F32, stD)
    cm1c = fcols[:, 0:16]
    al1c = fcols[:, 16:32]
    a1c = fcols[:, 32:48]
    ftmp = fcols[:, 48:64]

    def G(g):
        wt(g, "evCs")
        wt(g, "sTo")
        inc("ldc2", g.dma_start(b1b[:], b1d), 16)
        inc("ldc2", g.dma_start(b2b[:], b2d), 16)
    GE(G)


    def V(v):
        wt(v, "peC")
        wt(v, "sTo")
        v.memset(wcols[:, 66:67], 1.0)
        inc("dve", v.engine_nop())
    VE(V)

    stW = ExitStack()
    wbuf = xq
    wqb = Ub[:, 4096:6144]
    pW = psum("pW", [128, 512], F32, stW)

    def _wslice(wi, ch, dst):
        a = w1t if wi == 0 else w2t
        d = w1i_d if wi == 0 else w2i_d
        src = a if not dst else d
        if wi == 0:
            rc, kc = ch // 4, ch % 4
            return src[rc * 128:(rc + 1) * 128, kc * 1024:(kc + 1) * 1024]
        return src[ch * 128:(ch + 1) * 128, :]

    for wi in range(2):
        base = wi * 32
        for ch in range(32):
            def S(sy, wi=wi, ch=ch, base=base):
                if base + ch < 2:
                    wt(sy, "evCs")
                    wt(sy, "sTo")
                if wi == 1 and ch < 2:
                    wt(sy, "qql", 31 + ch)
                if base + ch >= 2:
                    wt(sy, "qab", base + ch - 1)
                inc("lwq", sy.dma_start(wbuf[:, (ch % 2) * 1024:(ch % 2) * 1024 + 1024],
                                        _wslice(wi, ch, False)), 16)
            SY(S)

            if ch % 8 == 7:
                def S(s, wi=wi, ch=ch, base=base):
                    for c2 in range(ch - 7, ch + 1):
                        wt(s, "lwq", 16 * (64 * wi + c2 + 1))
                        inc("qab", s.activation(scr2[:, 0:1024],
                                                wbuf[:, (c2 % 2) * 1024:(c2 % 2) * 1024 + 1024],
                                                ACTF.Abs, accum_out=wcols[:, c2:c2 + 1]))
                SE(S)

        def V(v, base=base):
            wt(v, "qab", base + 32)
            b.bar(v, v.reduce_sum(wcols[:, 64:65], wcols[:, 0:32], AX.X))
            inc("dve", v.engine_nop())
        VE(V)

        def Tt(t):
            wt(t, "dve")
            inc("peW", t.matmul(pW[0:1, 0:1], wcols[:, 64:65], wcols[:, 66:67],
                                start=True, stop=True))
        TE(Tt)

        def V(v):
            wt(v, "peW")
            b.bar(v, v.tensor_scalar(wcols[0:1, 67:68], pW[0:1, 0:1],
                                     1.0 / (4096.0 * 1024.0), 1e-5, ALU.mult, ALU.max))
            b.bar(v, v.reciprocal(wcols[0:1, 68:69], wcols[0:1, 67:68]))
            b.bar(v, v.memset(wbuf[0:1, 0:128], 1.0))
            inc("dve", v.engine_nop())
        VE(V)

        def Tt(t):
            wt(t, "dve")
            inc("peW", t.matmul(pW[:, 4:5], wbuf[0:1, 0:128], wcols[0:1, 68:69],
                                start=True, stop=True))
        TE(Tt)

        def V(v, wi=wi):
            wt(v, "peW")
            b.bar(v, v.tensor_copy(wcols[:, 78 + wi:79 + wi], pW[:, 4:5]))
            inc("qsc", v.engine_nop())
        VE(V)

        for ch in range(32):
            def S(sy, wi=wi, ch=ch, base=base):
                wt(sy, "qsc", wi + 1)
                if base + ch >= 2:
                    wt(sy, "qql", base + ch - 1)
                inc("lwq", sy.dma_start(wbuf[:, (ch % 2) * 1024:(ch % 2) * 1024 + 1024],
                                        _wslice(wi, ch, False)), 16)
            SY(S)

            def V(v, wi=wi, ch=ch, base=base):
                wt(v, "lwq", 16 * (64 * wi + 32 + ch + 1))
                if base + ch >= 2:
                    wt(v, "qout", 16 * (base + ch - 1))
                hb = (ch % 2) * 1024
                v.tensor_scalar(scr[:, 0:1024], wbuf[:, hb:hb + 1024],
                                wcols[:, 78 + wi:79 + wi], MAGIC, ALU.mult, ALU.add)
                v.tensor_scalar(ki32[:], scr[:, 0:1024], MAGIC, None, ALU.subtract)
                v.tensor_scalar(wqb[:, hb:hb + 1024], ki32[:], -1.0, 1.0, ALU.max, ALU.min)
                inc("qql", v.engine_nop())
            VE(V)

            def G(g, wi=wi, ch=ch, base=base):
                wt(g, "qql", base + ch + 1)
                inc("qout", g.dma_start(_wslice(wi, ch, True),
                                        wqb[:, (ch % 2) * 1024:(ch % 2) * 1024 + 1024]), 16)
            GE(G)

    stW.close()

    b.dbg_dma("wq1i", w1i_d, waits=("qout",))
    b.dbg_dma("wq2i", w2i_d, waits=("qout",))

    stFp = ExitStack()
    pFF = [psum("pFF%d" % i, [128, 512], F32, stFp) for i in range(2)]
    pF2 = [psum("pF2%d" % i, [128, 512], F32, stFp) for i in range(2)]
    pK = psum("pK", [128, 1024], BF16, stFp)
    pCR = psum("pCR", [128, 128], F32, stFp)

    def V(v):
        wt(v, "qsc", 2)
        b.bar(v, v.tensor_scalar(ftmp[:], qssqcol[:], 1.0 / C, 1e-6, ALU.mult, ALU.add))
        inc("dve", v.engine_nop())
    VE(V)

    def S(s):
        wt(s, "dve")
        inc("act", s.activation(wcols[:, 0:16], ftmp[:], ACTF.Sqrt))
    SE(S)

    def V(v):
        wt(v, "act")
        b.bar(v, v.reciprocal(ftmp[:], wcols[:, 0:16]))
        b.bar(v, v.tensor_mul(wcols[:, 16:32], ftmp[:], qmaxcol[:]))
        b.bar(v, v.tensor_scalar(wcols[:, 16:32], wcols[:, 16:32], 1e-5, None, ALU.max))
        b.bar(v, v.reciprocal(wcols[:, 32:48], wcols[:, 16:32]))
        b.bar(v, v.tensor_scalar(wcols[:, 32:48], wcols[:, 32:48], 127.0, None, ALU.mult))
        b.bar(v, v.tensor_mul(cm1c[:], ftmp[:], wcols[:, 32:48]))
        b.bar(v, v.tensor_mul(cm1c[:], cm1c[:], rcol[:]))
        b.bar(v, v.tensor_scalar(a1c[:], wcols[:, 32:48], wcols[:, 78:79], None, ALU.mult))
        b.bar(v, v.reciprocal(al1c[:], a1c[:]))
        inc("q1c", v.engine_nop())
    VE(V)

    def Tt(t):
        wt(t, "q1c", 1)
        inc("peW", t.transpose(pCR[0:16, 0:128], a1c[:], identf[:]))
    TE(Tt)

    def V(v):
        wt(v, "peW")
        b.bar(v, v.tensor_copy(rstg[0:16, 0:128], pCR[0:16, 0:128]))
        inc("evCR", v.engine_nop())
    VE(V)

    def S(sy):
        wt(sy, "evCR", 1)
        for j in range(16):
            inc("arow", sy.dma_start(A1R[0:1, j * 128:(j + 1) * 128], rstg[j:j + 1, 0:128]), 16)
    SY(S)

    b.dbg_dma("a1r", rowA[0:1, 0:2048], waits=("arow",))
    b.dbg_dma("q1cols", fcols[:], waits=("q1c",))

    for ck in range(NTCH):
        for ti in range(4):
            tg = ck * 4 + ti

            def S(sy, tg=tg):
                if tg < 2:
                    wt(sy, "evCs")
                    wt(sy, "sTo")
                    wt(sy, "qout")
                if tg >= 2:
                    wt(sy, "k1v", tg - 1)
                inc("xq2", sy.dma_start(xq[:, (tg % 2) * C:(tg % 2) * C + C],
                                        x_in[tg * 128:(tg + 1) * 128, :]), 16)
            SY(S)

            def V(v, tg=tg):
                wt(v, "xq2", 16 * (tg + 1))
                hb = (tg % 2) * C
                v.tensor_scalar(scr[:, 0:C], xq[:, hb:hb + C],
                                mcol[:, tg:tg + 1], cm1c[:, tg:tg + 1],
                                ALU.subtract, ALU.mult)
                v.tensor_scalar(scr[:, C:2 * C], scr[:, 0:C], MAGIC, None, ALU.add)
                v.tensor_scalar(ki32[:], scr[:, C:2 * C], MAGIC, -128.0,
                                ALU.subtract, ALU.max)
                v.tensor_copy(scr2.bitcast(BF16)[:, 0:1024], ki32[:])
                inc("k1v", v.engine_nop())
            VE(V)

            kb = ck * 20 + ti

            def Tt(t, tg=tg, kb=kb):
                wt(t, "k1v", tg + 1)
                if kb >= 1:
                    wt(t, "evK", kb)
                for j in range(NCC):
                    last = t.transpose(pK[:, j * 128:(j + 1) * 128],
                                       scr2.bitcast(BF16)[:, j * 128:(j + 1) * 128], identb)
                inc("peK", last)
            TE(Tt)

            def V(v, ti=ti, kb=kb, ck=ck):
                wt(v, "peK", kb + 1)
                if ck >= 1 and ti == 0:
                    wt(v, "peF2", 64 * ck)
                for j in range(NCC):
                    v.tensor_copy(Ub[:, j * 512 + ti * 128:j * 512 + (ti + 1) * 128],
                                  pK[:, j * 128:(j + 1) * 128])
                inc("evK", v.engine_nop())
            VE(V)

        if ck == 0:
            b.dbg_dma("k1c0", Ub[:, 0:4096], waits=("evK",))

        for hb in range(8):
            idx = ck * 8 + hb

            def S(sy, ck=ck, hb=hb, idx=idx):
                if idx == 0:
                    wt(sy, "evCs")
                    wt(sy, "sTo")
                    wt(sy, "qout")
                if idx >= 2:
                    wt(sy, "peF", 4 * (idx - 1))
                wt(sy, "peF2", 64 * ck)
                buf = idx % 2
                inc("lw1b", sy.dma_start(
                    Ub[:, 4096 + buf * 4096:8192 + buf * 4096].rearrange("p (cc h) -> p cc h", cc=8),
                    w1i_d[:, hb * 512:(hb + 1) * 512].rearrange("(cc p) h -> p cc h", p=128)), 16)
            SY(S)

            for ti in range(4):
                tg = ck * 4 + ti
                fidx = ck * 32 + hb * 4 + ti
                buf = idx % 2

                def Tt(t, ck=ck, hb=hb, ti=ti, tg=tg, fidx=fidx, idx=idx, buf=buf):
                    wt(t, "lw1b", 16 * (idx + 1))
                    if hb == 0 and ti == 0:
                        wt(t, "evK", ck * 20 + 4)
                        wt(t, "arow", 16 * 16)
                    if fidx >= 2:
                        wt(t, "evF", fidx - 1)
                    t.matmul(pFF[fidx % 2][:], rowA[0:1, tg * 128:(tg + 1) * 128],
                             b1b[0:1, hb * 512:(hb + 1) * 512], start=True, stop=False)
                    for cc in range(8):
                        last = t.matmul(pFF[fidx % 2][:],
                                        Ub[:, cc * 512 + ti * 128:cc * 512 + (ti + 1) * 128],
                                        Ub[:, 4096 + buf * 4096 + cc * 512:4096 + buf * 4096 + (cc + 1) * 512],
                                        start=False, stop=(cc == 7))
                    inc("peF", last)
                TE(Tt)

                def V(v, ck=ck, tg=tg, fidx=fidx):
                    wt(v, "peF", fidx + 1)
                    wt(v, "acS", 8 * ck)
                    if fidx >= 2:
                        wt(v, "acG", fidx - 1)
                    v.tensor_scalar(scr[:, (fidx % 2) * 512:(fidx % 2) * 512 + 512],
                                    pFF[fidx % 2][:], al1c[:, tg:tg + 1], None, ALU.mult)
                    inc("evF", v.engine_nop())
                VE(V)

                def V(v, ti=ti, hb=hb, fidx=fidx):
                    wt(v, "acG", fidx + 1)
                    v.tensor_reduce(wcols[:, ti * 8 + hb:ti * 8 + hb + 1],
                                    h1B[:, ti * 4096 + hb * 512:ti * 4096 + (hb + 1) * 512],
                                    AX.X, ALU.max, apply_absolute_value=True)
                    inc("hmx", v.engine_nop())
                VE(V)

            def S(s, ck=ck, hb=hb):
                for ti in range(4):
                    fidx = ck * 32 + hb * 4 + ti
                    sl = h1B[:, ti * 4096 + hb * 512:ti * 4096 + (hb + 1) * 512]
                    wt(s, "evF", fidx + 1)
                    inc("acG", s.activation(sl, scr[:, (fidx % 2) * 512:(fidx % 2) * 512 + 512],
                                            ACTF.Gelu))
                    inc("hsq", s.activation(scr2[:, 1024:1536], sl, ACTF.Square,
                                            accum_out=wcols[:, 32 + ti * 8 + hb:32 + ti * 8 + hb + 1]))
            SE(S)

        if ck == 0:
            b.dbg_dma("h1c0", h1B[:], waits=("hsq", "hmx"))

        def V(v, ck=ck):
            wt(v, "hsq", 32 * (ck + 1))
            b.bar(v, v.engine_nop())
            for ti in range(4):
                b.bar(v, v.tensor_reduce(wcols[:, 64 + ti:65 + ti],
                                         wcols[:, ti * 8:ti * 8 + 8], AX.X, ALU.max))
                b.bar(v, v.reduce_sum(wcols[:, 68 + ti:69 + ti],
                                      wcols[:, 32 + ti * 8:32 + ti * 8 + 8], AX.X))
            b.bar(v, v.tensor_scalar(wcols[:, 72:76], wcols[:, 68:72], 1.0 / HF, 1e-6,
                                     ALU.mult, ALU.add))
            inc("dve", v.engine_nop())
        VE(V)

        def S(s):
            wt(s, "dve")
            inc("act", s.activation(wcols[:, 68:72], wcols[:, 72:76], ACTF.Sqrt))
        SE(S)

        def V(v, ck=ck):
            wt(v, "act")
            b.bar(v, v.reciprocal(wcols[:, 72:76], wcols[:, 68:72]))
            b.bar(v, v.tensor_mul(wcols[:, 68:72], wcols[:, 72:76], wcols[:, 64:68]))
            b.bar(v, v.tensor_scalar(wcols[:, 68:72], wcols[:, 68:72], 1e-5, None, ALU.max))
            b.bar(v, v.reciprocal(wcols[:, 64:68], wcols[:, 68:72]))
            b.bar(v, v.tensor_scalar(wcols[:, 64:68], wcols[:, 64:68], 127.0, None, ALU.mult))
            b.bar(v, v.tensor_mul(wcols[:, 68:72], wcols[:, 72:76], wcols[:, 64:68]))
            b.bar(v, v.tensor_scalar(wcols[:, 72:76], wcols[:, 64:68], wcols[:, 79:80], None, ALU.mult))
            b.bar(v, v.reciprocal(wcols[:, 64:68], wcols[:, 72:76]))
            inc("q2c", v.engine_nop())
        VE(V)

        def Tt(t, ck=ck):
            wt(t, "q2c", ck + 1)
            inc("peW", t.transpose(pCR[0:4, 0:128], wcols[:, 72:76], identf[:]))
        TE(Tt)

        def V(v, ck=ck):
            wt(v, "peW")
            wt(v, "arow", 16 * (16 + 4 * ck))
            b.bar(v, v.tensor_copy(rstg[0:4, 0:128], pCR[0:4, 0:128]))
            inc("evCR", v.engine_nop())
        VE(V)

        def S(sy, ck=ck):
            wt(sy, "evCR", 2 + ck)
            if ck >= 1:
                wt(sy, "peF2", 64 * (ck - 1) + 8)
            for j in range(4):
                inc("arow", sy.dma_start(A2R[0:1, j * 128:(j + 1) * 128], rstg[j:j + 1, 0:128]), 16)
        SY(S)

        for ti in range(4):
            for hb in range(8):
                def V(v, ck=ck, ti=ti, hb=hb):
                    if ti == 0 and hb == 0:
                        wt(v, "q2c", ck + 1)
                        wt(v, "stq")
                    sl = h1B[:, ti * 4096 + hb * 512:ti * 4096 + (hb + 1) * 512]
                    v.tensor_scalar(scr[:, 0:512], sl, wcols[:, 68 + ti:69 + ti], MAGIC,
                                    ALU.mult, ALU.add)
                    v.tensor_scalar(ki32[:, 0:512], scr[:, 0:512], MAGIC, -128.0,
                                    ALU.subtract, ALU.max)
                    v.tensor_copy(sl, ki32[:, 0:512])
                    inc("k2v", v.engine_nop())
                VE(V)

        for ti in range(4):
            for jb in range(4):
                kb = ck * 20 + 4 + ti * 4 + jb

                def Tt(t, ck=ck, ti=ti, jb=jb, kb=kb):
                    wt(t, "k2v", ck * 32 + ti * 8 + 8)
                    wt(t, "evK", kb)
                    for j in range(8):
                        hc = jb * 8 + j
                        last = t.transpose(pK[:, j * 128:(j + 1) * 128],
                                           h1B[:, ti * 4096 + hc * 128:ti * 4096 + (hc + 1) * 128],
                                           identb)
                    inc("peK", last)
                TE(Tt)

                def V(v, ti=ti, jb=jb, kb=kb):
                    wt(v, "peK", kb + 1)
                    for j in range(8):
                        hc = jb * 8 + j
                        v.tensor_copy(Ub[:, hc * 512 + ti * 128:hc * 512 + (ti + 1) * 128],
                                      pK[:, j * 128:(j + 1) * 128])
                    inc("evK", v.engine_nop())
                VE(V)

        for hcg in range(8):
            gidx = ck * 8 + hcg

            def G(g, ck=ck, hcg=hcg, gidx=gidx):
                if gidx == 0:
                    wt(g, "evCs")
                    wt(g, "sTo")
                    wt(g, "qout")
                if gidx >= 2:
                    wt(g, "peF2", 8 * (gidx - 1))
                buf = gidx % 2
                inc("lw2b", g.dma_start(
                    Ub[:, 16384 + buf * 4096:16384 + (buf + 1) * 4096].rearrange("p (hc c) -> p hc c", hc=4),
                    w2i_d[hcg * 512:(hcg + 1) * 512, :].rearrange("(hc p) c -> p hc c", p=128)), 16)
            GE(G)

            for ti in range(4):
                for cb in range(2):
                    fidx2 = ck * 64 + hcg * 8 + ti * 2 + cb
                    buf = gidx % 2

                    def Tt(t, ck=ck, hcg=hcg, ti=ti, cb=cb, fidx2=fidx2, gidx=gidx, buf=buf):
                        wt(t, "lw2b", 16 * (gidx + 1))
                        if hcg == 0 and ti == 0 and cb == 0:
                            wt(t, "evK", 20 * (ck + 1))
                        if hcg == 0:
                            wt(t, "arow", 16 * (16 + 4 * (ck + 1)))
                        if fidx2 >= 2:
                            wt(t, "evF2", fidx2 - 1)
                        if hcg == 0:
                            t.matmul(pF2[fidx2 % 2][:], rowA[0:1, 2048 + ti * 128:2048 + (ti + 1) * 128],
                                     b2b[0:1, cb * 512:(cb + 1) * 512], start=True, stop=False)
                        for q in range(4):
                            hc = hcg * 4 + q
                            last = t.matmul(pF2[fidx2 % 2][:],
                                            Ub[:, hc * 512 + ti * 128:hc * 512 + (ti + 1) * 128],
                                            Ub[:, 16384 + buf * 4096 + q * 1024 + cb * 512:16384 + buf * 4096 + q * 1024 + (cb + 1) * 512],
                                            start=(q == 0 and hcg != 0), stop=(q == 3))
                        inc("peF2", last)
                    TE(Tt)

                    def V(v, hcg=hcg, ti=ti, cb=cb, fidx2=fidx2):
                        wt(v, "peF2", fidx2 + 1)
                        dst = yac[:, ti * 1024 + cb * 512:ti * 1024 + (cb + 1) * 512]
                        if hcg == 0:
                            v.tensor_copy(dst, pF2[fidx2 % 2][:])
                        else:
                            v.tensor_add(dst, dst, pF2[fidx2 % 2][:])
                        inc("evF2", v.engine_nop())
                    VE(V)

        if ck == 0:
            b.dbg_dma("yac0", yac[:], waits=("evF2",))
            b.dbg_dma("q2w", wcols[:, 64:76], waits=("q2c",))
            b.dbg_dma("wsc", wcols[:, 78:80], waits=("q2c",))
            b.dbg_dma("k2c0", Ub[:, 0:16384], waits=("evK",))

        for ti in range(4):
            for cb in range(2):
                tg = ck * 4 + ti
                eidx = ck * 8 + ti * 2 + cb

                def G(g, tg=tg, cb=cb, eidx=eidx):
                    if eidx == 0:
                        wt(g, "sTo")
                    if eidx >= 2:
                        wt(g, "u0d", eidx - 1)
                    inc("lsT", g.dma_start(sTb[:, (eidx % 2) * 512:(eidx % 2) * 512 + 512],
                                           sT_d[tg * 128:(tg + 1) * 128, cb * 512:(cb + 1) * 512]), 16)
                GE(G)

                def V(v, ck=ck, ti=ti, cb=cb, eidx=eidx):
                    wt(v, "evF2", ck * 64 + 56 + ti * 2 + cb + 1)
                    if eidx >= 2:
                        wt(v, "acS", eidx - 1)
                    v.tensor_scalar(scr[:, (eidx % 2) * 512:(eidx % 2) * 512 + 512],
                                    yac[:, ti * 1024 + cb * 512:ti * 1024 + (cb + 1) * 512],
                                    wcols[:, 64 + ti:65 + ti], None, ALU.mult)
                    inc("evGs", v.engine_nop())
                VE(V)

                if ti == 3 and cb == 1:
                    def S(s, ck=ck):
                        for e2 in range(ck * 8, ck * 8 + 8):
                            wt(s, "evGs", e2 + 1)
                            inc("acS", s.activation(
                                scr2.bitcast(BF16)[:, 3072 + (e2 % 2) * 512:3072 + (e2 % 2) * 512 + 512],
                                scr[:, (e2 % 2) * 512:(e2 % 2) * 512 + 512], ACTF.Silu))
                    SE(S)

                def V(v, tg=tg, cb=cb, eidx=eidx):
                    wt(v, "acS", eidx + 1)
                    wt(v, "lsT", 16 * (eidx + 1))
                    if eidx == 0:
                        wt(v, "rstI")
                    v.tensor_mul(scr[:, 1024:1536],
                                 scr2.bitcast(BF16)[:, 3072 + (eidx % 2) * 512:3072 + (eidx % 2) * 512 + 512],
                                 sTb[:, (eidx % 2) * 512:(eidx % 2) * 512 + 512])
                    rs = resT[:, tg * 1024 + cb * 512:tg * 1024 + (cb + 1) * 512]
                    v.tensor_add(rs, rs, scr[:, 1024:1536])
                    inc("u0d", v.engine_nop())
                VE(V)

    b.dbg_dma("x1", resT[:], waits=("u0d",))

    s1_d = nc.dram_tensor("s1d", [T, C], BF16, kind="Internal").ap()
    weF = we.rearrange("E r c -> (E r) c")

    stD.close()

    _dnew = ["evA2", "evGR", "evHG", "evLG", "evP", "evS2", "evT", "exq",
             "gbb", "gex", "gmx", "gpr", "ln1q", "ln1r", "ln1s", "lnx", "mx1",
             "peA", "peGB", "peGT", "peHG", "peLG", "peM", "peP", "pSL",
             "peT1", "psc", "rz1", "u1d", "u1s",
             "gr1", "s1o", "ldg", "ldwe", "ls1"]
    _dead = ["scores", "lnstats", "peT", "evA", "peS", "evB", "peB", "evB2",
             "peB2", "peG", "evG", "peN", "evN", "peQ", "evQ", "peVT", "evVT",
             "peO", "evO", "dveM", "expS", "acG", "evGs", "peSel", "evSel",
             "xdone", "peS2", "qab", "k1v", "k2v",
             "xload", "xq2", "ldcb", "lsT", "qout"]
    assert len(_dnew) == len(set(_dnew)) and len(_dead) == len(set(_dead))
    _SM = dict(zip(_dnew, _dead))
    _BASE = {m: val(m) for m in _SM.values()}
    _inc0, _wt0 = inc, wt

    def inc(name, inst, n=1):
        return _inc0(_SM.get(name, name), inst, n)

    def wt(eng, name, v=None):
        m = _SM.get(name, name)
        if v is None:
            _wt0(eng, m)
        else:
            _wt0(eng, m, _BASE.get(m, 0) + v)

    stE = ExitStack()
    xn1F = sbuf("xn1F", [128, NCC * T], BF16, stE)
    Grow = sbuf("Grow", [20, T], BF16, stE)
    bebuf = sbuf("bebuf", [20, C], BF16, stE)

    for tl in range(NTC):
        def V(v, tl=tl):
            if tl == 0:
                wt(v, "peF2", 64 * NTCH)
            v.reduce_sum(mcol[:, tl:tl + 1], resT[:, tl * C:(tl + 1) * C], AX.X)
            inc("ln1s", v.engine_nop())
        VE(V)

        if tl == NTC - 1:
            def S(s):
                wt(s, "u0d", 8 * NTCH)
                for t2 in range(NTC):
                    wt(s, "ln1s", t2 + 1)
                    inc("ln1q", s.activation(scr2[:, 0:C], resT[:, t2 * C:(t2 + 1) * C],
                                             ACTF.Square, accum_out=vcol[:, t2:t2 + 1]))
            SE(S)

    def V(v):
        wt(v, "ln1q", NTC)
        b.bar(v, v.engine_nop())
        b.bar(v, v.tensor_scalar(mcol[:], mcol[:], 1.0 / C, None, ALU.mult))
        b.bar(v, v.tensor_mul(scr2[:, 0:NTC], mcol[:], mcol[:]))
        b.bar(v, v.scalar_tensor_tensor(vcol[:], vcol[:], 1.0 / C, scr2[:, 0:NTC],
                                        ALU.mult, ALU.subtract))
        v.tensor_scalar(scr2[:, 0:NTC], vcol[:], 1e-5, None, ALU.add)
        inc("dve", v.engine_nop())
    VE(V)

    def S(s):
        wt(s, "dve")
        inc("act", s.activation(scr2[:, NTC:2 * NTC], scr2[:, 0:NTC], ACTF.Sqrt))
    SE(S)

    def V(v):
        wt(v, "act")
        b.bar(v, v.reciprocal(rcol[:], scr2[:, NTC:2 * NTC]))
        inc("ln1r", v.engine_nop())
    VE(V)

    stEa = ExitStack()
    xn1T = sbuf("xn1T", [128, NTC * C], BF16, stEa)
    PT = sbuf("PT", [128, T], BF16, stEa)
    PTkF = sbuf("PTkF", [128, T], BF16, stEa)
    s1st = sbuf("s1st", [128, 2 * 512], BF16, stEa)
    scb = sbuf("scb", [128, 2 * T], F32, stEa)

    for tl in range(NTC):
        def V(v, tl=tl):
            if tl == 0:
                wt(v, "ln1r", 1)
            v.tensor_scalar(scr[:, 0:C], resT[:, tl * C:(tl + 1) * C],
                            mcol[:, tl:tl + 1], rcol[:, tl:tl + 1],
                            ALU.subtract, ALU.mult)
            v.tensor_copy(xn1T[:, tl * C:(tl + 1) * C], scr[:, 0:C])
            inc("lnx", v.engine_nop())
        VE(V)

        def Tt(t, tl=tl):
            wt(t, "lnx", tl + 1)
            if tl >= 1:
                wt(t, "evT", tl)
            for j in range(NCC):
                last = t.transpose(pK[:, j * 128:(j + 1) * 128],
                                   xn1T[:, tl * C + j * 128:tl * C + (j + 1) * 128], identb)
            inc("peT1", last)
        TE(Tt)

        def V(v, tl=tl):
            wt(v, "peT1", tl + 1)
            for j in range(NCC):
                v.tensor_copy(xn1F[:, j * T + tl * 128:j * T + (tl + 1) * 128],
                              pK[:, j * 128:(j + 1) * 128])
            inc("evT", v.engine_nop())
        VE(V)

    for qt in range(NTC):
        for kb in range(4):
            sidx = qt * 4 + kb

            def Tt(t, qt=qt, kb=kb, sidx=sidx):
                if sidx == 0:
                    wt(t, "evT", NTC)
                if sidx >= 2:
                    wt(t, "evS2", sidx - 1)
                for cc in range(NCC):
                    last = t.matmul(pFF[sidx % 2][:],
                                    xn1F[:, cc * T + qt * 128:cc * T + (qt + 1) * 128],
                                    xn1F[:, cc * T + kb * 512:cc * T + (kb + 1) * 512],
                                    start=(cc == 0), stop=(cc == 7))
                inc("pSL", last)
            TE(Tt)

            def V(v, qt=qt, kb=kb, sidx=sidx):
                wt(v, "pSL", sidx + 1)
                if sidx % 4 == 0 and sidx >= 8:
                    wt(v, "exq", sidx // 4 - 1)
                v.tensor_copy(scb[:, (qt % 2) * T + kb * 512:(qt % 2) * T + (kb + 1) * 512],
                              pFF[sidx % 2][:])
                inc("evS2", v.engine_nop())
            VE(V)

        def V(v, qt=qt):
            wt(v, "evS2", qt * 4 + 4)
            pq = qt % 2
            b.bar(v, v.tensor_reduce(scr2[:, 40 + pq:41 + pq], scb[:, pq * T:pq * T + T], AX.X, ALU.max))
            b.bar(v, v.tensor_scalar(scr2[:, 44 + pq:45 + pq], scr2[:, 40 + pq:41 + pq], -1.0, None, ALU.mult))
            inc("mx1", v.engine_nop())
        VE(V)

        if qt % 4 == 3:
            def S(s, qt=qt):
                for q2 in range(qt - 3, qt + 1):
                    p2 = q2 % 2
                    wt(s, "mx1", q2 + 1)
                    if q2 >= 1:
                        wt(s, "evP", 2 * q2)
                    inc("exq", s.activation(PT[:], scb[:, p2 * T:p2 * T + T], ACTF.Exp,
                                            bias=scr2[:, 44 + p2:45 + p2],
                                            accum_out=scr2[:, 46 + p2:47 + p2]))
            SE(S)

        def V(v, qt=qt):
            wt(v, "exq", qt + 1)
            pq = qt % 2
            b.bar(v, v.reciprocal(scr2[:, 48 + pq:49 + pq], scr2[:, 46 + pq:47 + pq]))
            inc("rz1", v.engine_nop())
        VE(V)

        for half in range(2):
            def Tt(t, qt=qt, half=half):
                wt(t, "exq", qt + 1)
                wt(t, "evP", 2 * qt + half)
                for j in range(8):
                    last = t.transpose(pK[:, j * 128:(j + 1) * 128],
                                       PT[:, half * 1024 + j * 128:half * 1024 + (j + 1) * 128],
                                       identb)
                inc("peP", last)
            TE(Tt)

            def V(v, qt=qt, half=half):
                wt(v, "peP", 2 * qt + half + 1)
                for j in range(8):
                    v.tensor_copy(PTkF[:, half * 1024 + j * 128:half * 1024 + (j + 1) * 128],
                                  pK[:, j * 128:(j + 1) * 128])
                inc("evP", v.engine_nop())
            VE(V)

        for cb in range(2):
            aidx = qt * 2 + cb

            def Tt(t, qt=qt, cb=cb, aidx=aidx):
                wt(t, "evP", 2 * qt + 2)
                if aidx >= 2:
                    wt(t, "evA2", aidx - 1)
                for kt in range(NTC):
                    last = t.matmul(pF2[aidx % 2][:],
                                    PTkF[:, kt * 128:(kt + 1) * 128],
                                    xn1T[:, kt * C + cb * 512:kt * C + (cb + 1) * 512],
                                    start=(kt == 0), stop=(kt == NTC - 1))
                inc("peA", last)
            TE(Tt)

            def V(v, cb=cb, qt=qt, aidx=aidx):
                wt(v, "peA", aidx + 1)
                wt(v, "rz1", qt + 1)
                if aidx >= 2:
                    wt(v, "s1o", 16 * (aidx - 1))
                v.tensor_scalar(s1st[:, (aidx % 2) * 512:(aidx % 2) * 512 + 512],
                                pF2[aidx % 2][:], scr2[:, 48 + (qt % 2):49 + (qt % 2)], None, ALU.mult)
                inc("evA2", v.engine_nop())
            VE(V)

            def S(sy, qt=qt, cb=cb, aidx=aidx):
                wt(sy, "evA2", aidx + 1)
                inc("s1o", sy.dma_start(
                    s1_d[qt * 128:(qt + 1) * 128, cb * 512:(cb + 1) * 512],
                    s1st[:, (aidx % 2) * 512:(aidx % 2) * 512 + 512]), 16)
            SY(S)

    stEa.close()

    stEg = ExitStack()
    wg1b = sbuf("wg1b", [128, NCC * 256], BF16, stEg)
    wg2b = sbuf("wg2b", [128, 2 * 40], BF16, stEg)
    hgB = sbuf("hgB", [128, 2 * T], BF16, stEg)
    lgF = sbuf("lgF", [40, T], F32, stEg)
    bgc = sbuf("bgc", [128, 3], F32, stEg)

    def G(g):
        wt(g, "peF2", 64 * NTCH)
        wt(g, "acS", 8 * NTCH)
        wt(g, "s1o")
        inc("ldg", g.dma_start(wg1b[:].rearrange("p (cc h) -> p cc h", cc=NCC),
                               wg1.rearrange("(cc p) h -> p cc h", p=128)), 16)
        inc("ldg", g.dma_start(wg2b[:].rearrange("p (hh j) -> p hh j", hh=2),
                               wg2.rearrange("(hh p) j -> p hh j", p=128)), 16)
        inc("ldg", g.dma_start(bgc[:, 0:2], bg1.rearrange("(hh p) o -> p (hh o)", p=128),
                               allow_slow_non_contiguous=True), 16)
        inc("ldg", g.dma_start(bgc[0:40, 2:3], bg2), 16)
        inc("ldg", g.dma_start(bebuf[:], be), 16)
    GE(G)

    for hh in range(2):
        for qc in range(4):
            gidx = hh * 4 + qc

            def Tt(t, hh=hh, qc=qc, gidx=gidx):
                if gidx == 0:
                    wt(t, "ldg")
                    wt(t, "evS2", 64)
                if gidx >= 2:
                    wt(t, "evHG", gidx - 1)
                for cc in range(NCC):
                    last = t.matmul(pFF[gidx % 2][:],
                                    wg1b[:, cc * 256 + hh * 128:cc * 256 + (hh + 1) * 128],
                                    xn1F[:, cc * T + qc * 512:cc * T + (qc + 1) * 512],
                                    start=(cc == 0), stop=(cc == 7))
                inc("peHG", last)
            TE(Tt)

            if qc == 3:
                def S(s, hh=hh):
                    for q2 in range(4):
                        g2 = hh * 4 + q2
                        wt(s, "peHG", g2 + 1)
                        inc("evHG", s.activation(hgB[:, hh * T + q2 * 512:hh * T + (q2 + 1) * 512],
                                                 pFF[g2 % 2][:], ACTF.Gelu,
                                                 bias=bgc[:, hh:hh + 1]))
                SE(S)

    for qc in range(4):
        def Tt(t, qc=qc):
            wt(t, "evHG", 4 + qc + 1)
            if qc == 0:
                wt(t, "evA2", 32)
            if qc >= 2:
                wt(t, "evLG", qc - 1)
            for hh in range(2):
                last = t.matmul(pF2[qc % 2][0:40, :],
                                wg2b[:, hh * 40:(hh + 1) * 40],
                                hgB[:, hh * T + qc * 512:hh * T + (qc + 1) * 512],
                                start=(hh == 0), stop=(hh == 1))
            inc("peLG", last)
        TE(Tt)

        def V(v, qc=qc):
            wt(v, "peLG", qc + 1)
            v.tensor_scalar(lgF[:, qc * 512:(qc + 1) * 512], pF2[qc % 2][0:40, :],
                            bgc[0:40, 2:3], None, ALU.add)
            inc("evLG", v.engine_nop())
        VE(V)

    for tl in range(NTC):
        def Tt(t, tl=tl):
            wt(t, "evLG", 4)
            if tl >= 1:
                wt(t, "evGR", 2 * tl)
            inc("peGT", t.transpose(pCR[:, 0:40], lgF[:, tl * 128:(tl + 1) * 128],
                                    identf[0:40, 0:40]))
        TE(Tt)

        def V(v, tl=tl):
            wt(v, "peGT", 2 * tl + 1)
            b.bar(v, v.tensor_copy(scr[:, 0:20], pCR[:, 0:20]))
            b.bar(v, v.tensor_reduce(scr[:, 24:25], scr[:, 0:20], AX.X, ALU.max))
            b.bar(v, v.tensor_scalar(scr[:, 25:26], scr[:, 24:25], -1.0, None, ALU.mult))
            inc("gmx", v.engine_nop())
        VE(V)

        def S(s, tl=tl):
            wt(s, "gmx", tl + 1)
            inc("gex", s.activation(scr[:, 64:84], scr[:, 0:20], ACTF.Exp,
                                    bias=scr[:, 25:26], accum_out=scr[:, 26:27]))
        SE(S)

        def V(v, tl=tl):
            wt(v, "gex", tl + 1)
            b.bar(v, v.reciprocal(scr[:, 27:28], scr[:, 26:27]))
            b.bar(v, v.tensor_scalar(scr[:, 64:84], scr[:, 64:84], scr[:, 27:28],
                                     None, ALU.mult))
            b.bar(v, v.tensor_reduce(scr[:, 24:25], scr[:, 64:84], AX.X, ALU.max))
            b.bar(v, v.tensor_scalar(scr[:, 40:60], scr[:, 64:84], scr[:, 24:25],
                                     None, ALU.is_equal))
            b.bar(v, v.scalar_tensor_tensor(scr[:, 40:60], scr[:, 40:60], -2e30,
                                            scr[:, 64:84], ALU.mult, ALU.add))
            b.bar(v, v.tensor_reduce(scr[:, 33:34], scr[:, 40:60], AX.X, ALU.max))
            b.bar(v, v.tensor_scalar(scr[:, 40:60], scr[:, 64:84], scr[:, 33:34],
                                     None, ALU.is_ge))
            b.bar(v, v.tensor_mul(scr[:, 64:84], scr[:, 64:84], scr[:, 40:60]))
            inc("gpr", v.engine_nop())
        VE(V)

        def Tt(t, tl=tl):
            wt(t, "gpr", tl + 1)
            inc("peGT", t.transpose(pCR[0:20, 0:128], scr[:, 64:84], identf[:]))
        TE(Tt)

        def V(v, tl=tl):
            wt(v, "peGT", 2 * tl + 2)
            b.bar(v, v.tensor_copy(Grow[0:20, tl * 128:(tl + 1) * 128], pCR[0:20, 0:128]))
            inc("evGR", v.engine_nop(), 2)
        VE(V)

    b.dbg_dma("s1t", s1_d, waits=(_SM["s1o"],))
    b.dbg_dma("grw", Grow[:], waits=(_SM["evGR"],))
    b.dbg_dma("lgt", lgF[:], waits=(_SM["evLG"],))

    stEg.close()
    stFp.close()

    stEm = ExitStack()
    gbcAll = sbuf("gbcAll", [128, (NE - 3) * 1024], BF16, stEm)

    def gsl(e):
        if e < 3:
            return scr2.bitcast(BF16)[:, e * 1024:(e + 1) * 1024]
        return gbcAll[:, (e - 3) * 1024:(e - 2) * 1024]
    xn1G = sbuf("xn1G", [128, 2 * NCC * 1024], BF16, stEm)
    webuf = sbuf("webuf", [128, 2 * NCC * 512], BF16, stEm)
    grow1 = sbuf("grow1", [1, 1024], BF16, stEm)
    s1b = sbuf("s1b", [128, 2 * 512], BF16, stEm)
    stMp = ExitStack()
    pM = [psum("pM%d" % i, [128, 512], F32, stMp) for i in range(8)]

    for grp in range(2):
        for e in range(NE):
            def S(sy, grp=grp, e=e):
                if grp == 0 and e == 0:
                    wt(sy, "evGR", 2 * NTC)
                if grp == 1 and e == 0:
                    wt(sy, "peM", 8 * 2 * NE)
                wt(sy, "gbb", 2 * (grp * NE + e))
                inc("gr1", sy.dma_start(grow1[:], Grow[e:e + 1, grp * 1024:(grp + 1) * 1024]), 16)
            SY(S)

            def Tt(t, grp=grp, e=e):
                wt(t, "gr1", 16 * (grp * NE + e + 1))
                wt(t, "gbb", 2 * (grp * NE + e))
                if grp == 1 and e == 0:
                    wt(t, "u1s", 16)
                for st in range(2):
                    inc("peGB", t.matmul(pM[st][:],
                                         onesrow[:],
                                         grow1[0:1, st * 512:(st + 1) * 512],
                                         start=True, stop=True))
            TE(Tt)

            def V(v, grp=grp, e=e):
                wt(v, "peGB", (grp * NE + e + 1) * 2)
                for st in range(2):
                    v.tensor_copy(gsl(e)[:, st * 512:(st + 1) * 512], pM[st][:])
                inc("gbb", v.engine_nop(), 2)
            VE(V)

        for cb in range(2):
            sw = grp * 2 + cb
            for e in range(NE):
                eidx = sw * NE + e

                def G(g, grp=grp, cb=cb, e=e, eidx=eidx):
                    if eidx < 2:
                        wt(g, "evGR", 2 * NTC)
                    if eidx >= 2:
                        wt(g, "peM", 8 * (eidx - 1))
                    buf = eidx % 2
                    inc("ldwe", g.dma_start(
                        webuf[:, buf * 4096:(buf + 1) * 4096].rearrange("p (cc c) -> p cc c", cc=NCC),
                        weF[e * C:(e + 1) * C, cb * 512:(cb + 1) * 512].rearrange("(cc p) c -> p cc c", p=128)), 16)
                GE(G)

                def V(v, grp=grp, e=e, eidx=eidx):
                    if eidx >= 2:
                        wt(v, "peM", 8 * (eidx - 1))
                    xb = (eidx % 2) * 8192
                    for cc in range(NCC):
                        v.tensor_mul(xn1G[:, xb + cc * 1024:xb + (cc + 1) * 1024],
                                     xn1F[:, cc * T + grp * 1024:cc * T + (grp + 1) * 1024],
                                     gsl(e))
                    inc("psc", v.engine_nop())
                VE(V)

                def Tt(t, grp=grp, cb=cb, e=e, eidx=eidx, sw=sw):
                    wt(t, "ldwe", 16 * (eidx + 1))
                    wt(t, "psc", eidx + 1)
                    if e == 0:
                        wt(t, "gbb", (grp + 1) * NE * 2)
                        wt(t, "u1s", 8 * sw)
                    buf = eidx % 2
                    for tl in range(8):
                        if e == 0:
                            t.matmul(pM[tl][:],
                                     Grow[0:20, grp * 1024 + tl * 128:grp * 1024 + (tl + 1) * 128],
                                     bebuf[0:20, cb * 512:(cb + 1) * 512],
                                     start=True, stop=False)
                        xb = (eidx % 2) * 8192
                        for cc in range(NCC):
                            last = t.matmul(pM[tl][:],
                                            xn1G[:, xb + cc * 1024 + tl * 128:xb + cc * 1024 + (tl + 1) * 128],
                                            webuf[:, buf * 4096 + cc * 512:buf * 4096 + (cc + 1) * 512],
                                            start=False, stop=(e == NE - 1 and cc == NCC - 1))
                    inc("peM", last, 8)
                TE(Tt)

            for tl in range(8):
                tg = grp * 8 + tl
                vidx = sw * 8 + tl

                def G(g, tg=tg, cb=cb, vidx=vidx):
                    if vidx < 2:
                        wt(g, "evGR", 2 * NTC)
                        wt(g, "s1o")
                    if vidx >= 2:
                        wt(g, "u1d", vidx - 1)
                    inc("ls1", g.dma_start(s1b[:, (vidx % 2) * 512:(vidx % 2) * 512 + 512],
                                           s1_d[tg * 128:(tg + 1) * 128, cb * 512:(cb + 1) * 512]), 16)
                GE(G)

                if tl == 7:
                    def S(s, sw=sw):
                        wt(s, "peM", 8 * (sw + 1) * NE)
                        for t2 in range(8):
                            v2 = sw * 8 + t2
                            if v2 >= 2:
                                wt(s, "u1d", v2 - 1)
                            inc("u1s", s.activation(
                                scr2.bitcast(BF16)[:, 3072 + (v2 % 2) * 512:3072 + (v2 % 2) * 512 + 512],
                                pM[t2][:], ACTF.Silu))
                    SE(S)

                def V(v, tg=tg, cb=cb, vidx=vidx):
                    wt(v, "u1s", vidx + 1)
                    wt(v, "ls1", 16 * (vidx + 1))
                    v.tensor_mul(scr[:, 1024:1536],
                                 scr2.bitcast(BF16)[:, 3072 + (vidx % 2) * 512:3072 + (vidx % 2) * 512 + 512],
                                 s1b[:, (vidx % 2) * 512:(vidx % 2) * 512 + 512])
                    rs = resT[:, tg * 1024 + cb * 512:tg * 1024 + (cb + 1) * 512]
                    v.tensor_add(rs, rs, scr[:, 1024:1536])
                    inc("u1d", v.engine_nop())
                VE(V)

    stMp.close()
    stEm.close()
    stE.close()

    def G(g):
        wt(g, "u1d")
        inc("outq", g.dma_start(out.rearrange("(tc p) c -> p tc c", p=128),
                                resT[:].rearrange("p (tc c) -> p tc c", tc=NTC)), 16)
        for nm in sorted(b.sems):
            wt(g, nm)
    GE(G)

    b.es.close()
    return nc, b.dram, b.DBG



def _prep(inputs):
    f = np.ascontiguousarray
    dqn = inputs["L0_attn_w_dq_nope"].reshape(96, NH, 32)
    dqr = inputs["L0_attn_w_dq_rope"].reshape(96, NH, 64)
    wink = inputs["L0_attn_w_wink"].reshape(C, NH, 96)
    selk = inputs["L0_attn_w_selk"].reshape(C, NH, 96)
    dkn = inputs["L0_attn_w_dk_nope"].reshape(32, NH, 32)
    dv = inputs["L0_attn_w_dv"].reshape(32, NH, 32)
    winv = inputs["L0_attn_w_winv"].reshape(C, NH, 32)
    selv = inputs["L0_attn_w_selv"].reshape(C, NH, 32)
    t = np.arange(T, dtype=np.float64)
    invf = 10000.0 ** (-np.arange(0, ROPE, 2, dtype=np.float64) / ROPE)
    ang = np.outer(invf, t)
    cosv, sinv = np.cos(ang), np.sin(ang)
    C4 = np.tile(cosv, (4, 1)); S4 = np.tile(sinv, (4, 1))
    c16, s16 = cosv / 16.0, sinv / 16.0
    KCS = np.concatenate([c16, s16, s16, c16], 0)
    KCS2 = np.concatenate([s16, c16, c16, s16], 0)
    ident = np.eye(128)
    Eseg4 = np.zeros((128, 4))
    for j in range(4):
        Eseg4[32 * j:32 * j + 32, j] = 1.0
    Eall64 = np.zeros((128, 4)); Eall64[0:64, :] = 1.0
    tri = (np.arange(128)[None, :] >= np.arange(128)[:, None]).astype(np.float64)
    constsf = np.concatenate([C4, S4, ident, Eseg4, Eall64, tri], axis=1).astype(np.float32)
    constsk = np.concatenate([KCS, KCS2], axis=1).astype(np.float32)
    return {
        "constsf": f(constsf),
        "constsk": f(constsk),
        "w_cq": f(inputs["L0_attn_w_cq"]),
        "g_qnorm": f(inputs["L0_attn_g_qnorm"].reshape(96, 1)),
        "wq_nope": f(dqn.reshape(96, NH * 32)),
        "wq_x1": f(dqr[:, :, 0:32].reshape(96, NH * 32)),
        "wq_x2": f(dqr[:, :, 32:64].reshape(96, NH * 32)),
        "w_ckv": f(inputs["L0_attn_w_ckv"]),
        "g_kvnorm": f(inputs["L0_attn_g_kvnorm"].reshape(32, 1)),
        "wk_nope": f(dkn.reshape(32, NH * 32)),
        "w_dv": f(dv.reshape(32, NH * 32)),
        "w_krope": f(inputs["L0_attn_w_krope"]),
        "w_small": f(np.concatenate([inputs["L0_attn_w_imp"], inputs["L0_attn_w_gate"]], axis=1)),
        "ww_nope": f(wink[:, :, 0:32].reshape(C, NH * 32)),
        "ww_x1": f(wink[:, :, 32:64].reshape(C, NH * 32)),
        "ww_x2": f(wink[:, :, 64:96].reshape(C, NH * 32)),
        "w_winv": f(winv.reshape(C, NH * 32)),
        "ws_nope": f(selk[:, :, 0:32].reshape(C, NH * 32)),
        "ws_x1": f(selk[:, :, 32:64].reshape(C, NH * 32)),
        "ws_x2": f(selk[:, :, 64:96].reshape(C, NH * 32)),
        "w_selv": f(selv.reshape(C, NH * 32)),
        "w_proj": f(inputs["L0_attn_w_proj"]),
        "w1t": f(inputs["L0_ffn_w1"].T),
        "b1": f(inputs["L0_ffn_b1"].reshape(1, HF)),
        "w2t": f(inputs["L0_ffn_w2"].T),
        "b2": f(inputs["L0_ffn_b2"].reshape(1, C)),
        "wg1": f(inputs["L1_ffn_wg1"]),
        "bg1": f(inputs["L1_ffn_bg1"].reshape(256, 1)),
        "wg2": f(inputs["L1_ffn_wg2"]),
        "bg2": f(inputs["L1_ffn_bg2"].reshape(40, 1)),
        "we": f(inputs["L1_ffn_We"]),
        "be": f(inputs["L1_ffn_be"]),
    }


def kernel(**inputs):
    com = _prep(inputs)
    x = np.ascontiguousarray(inputs["L0_attn_x"].astype(np.float32, copy=False))
    nc, dram, _ = build_nc()
    in_maps = []
    for i in range(8):
        m = dict(com)
        m["x"] = np.ascontiguousarray(x[i])
        in_maps.append(m)
    res = run_bass_kernel_spmd(nc, in_maps, core_ids=list(range(8)))
    return np.stack([res.results[i]["out"] for i in range(8)], axis=0)


# version 07 (final)
# speedup vs baseline: 1.0135x; 1.0135x over previous
import numpy as np
from contextlib import ExitStack

import concourse.bass as bass
import concourse.mybir as mybir
from concourse.bass_utils import run_bass_kernel_spmd

F32 = mybir.dt.float32
BF16 = mybir.dt.bfloat16
I32 = mybir.dt.int32
ALU = mybir.AluOpType
ACTF = mybir.ActivationFunctionType
AX = mybir.AxisListType

T = 2048
C = 1024
NTC = 16
NCC = 8
NH = 16
DV = 32
ROPE = 64
KEEP = 256
HF = 4096
NE = 20
MAGIC = 1.5 * 2.0 ** 23
SQK = 1.0 / np.sqrt(96.0)


class B:

    def __init__(self, dbg=()):
        self.dbg = set(dbg)
        self.nc = bass.Bass("TRN2", target_bir_lowering=False, detect_race_conditions=False)
        self.dram = {}
        self.DBG = {}
        self.es = ExitStack()
        self.sems = {}
        self.block = self.es.enter_context(self.nc.Block())

    def din(self, name, shape, dtype=F32):
        ap = self.nc.dram_tensor(name, shape, dtype, kind="ExternalInput").ap()
        self.dram[name] = ap
        return ap

    def tap(self, name, shape, dtype=F32):
        if name in self.dbg:
            self.DBG[name] = self.nc.dram_tensor("dbg_" + name, shape, dtype,
                                                 kind="ExternalOutput").ap()
            return self.DBG[name]
        return None

    def sem(self, name):
        if name not in self.sems:
            self.sems[name] = [self.es.enter_context(self.nc.semaphore(name)), 0]
        return self.sems[name][0]

    def inc(self, name, inst, n=1):
        h = self.sem(name)
        inst.then_inc(h, n)
        self.sems[name][1] += n
        return self.sems[name][1]

    def val(self, name):
        self.sem(name)
        return self.sems[name][1]

    def wt(self, eng, name, v=None):
        eng.wait_ge(self.sem(name), self.val(name) if v is None else v)

    def bar(self, eng, inst, name="sebar"):
        n = self.inc(name, inst)
        eng.wait_ge(self.sem(name), n)

    def TE(self, fn):
        @self.block.tensor
        def _(t):
            fn(t)

    def VE(self, fn):
        @self.block.vector
        def _(v):
            fn(v)

    def SE(self, fn):
        @self.block.scalar
        def _(s):
            fn(s)

    def GE(self, fn):
        @self.block.gpsimd
        def _(g):
            fn(g)

    def SY(self, fn):
        @self.block.sync
        def _(s):
            fn(s)

    def sbuf(self, name, shape, dtype, stack=None):
        return (stack or self.es).enter_context(self.nc.sbuf_tensor(name, shape, dtype))

    def psum(self, name, shape, dtype=F32, stack=None):
        return (stack or self.es).enter_context(self.nc.psum_tensor(name, shape, dtype))

    def dbg_dma(self, name, src_ap, waits=()):
        if name not in self.dbg:
            return
        t = self.tap(name, list(src_ap.shape), src_ap.dtype)
        def S(sy):
            for w in waits:
                self.wt(sy, w)
            self.inc("stq", sy.dma_start(t, src_ap), 16)
        self.SY(S)


def build_nc(dbg=(), stop_after=None):
    b = B(dbg)
    nc = b.nc
    es = b.es
    sbuf, psum, TE, VE, SE, GE, SY, inc, wt, val = (
        b.sbuf, b.psum, b.TE, b.VE, b.SE, b.GE, b.SY, b.inc, b.wt, b.val)

    x_in = b.din("x", [T, C])
    w_cq = b.din("w_cq", [C, 96]); g_qnorm = b.din("g_qnorm", [96, 1])
    wq_nope = b.din("wq_nope", [96, NH * 32])
    wq_x1 = b.din("wq_x1", [96, NH * 32]); wq_x2 = b.din("wq_x2", [96, NH * 32])
    w_ckv = b.din("w_ckv", [C, 32]); g_kvnorm = b.din("g_kvnorm", [32, 1])
    wk_nope = b.din("wk_nope", [32, NH * 32])
    w_dv = b.din("w_dv", [32, NH * 32])
    w_krope = b.din("w_krope", [C, ROPE])
    w_small = b.din("w_small", [C, 4])
    ww_nope = b.din("ww_nope", [C, NH * 32]); ww_x1 = b.din("ww_x1", [C, NH * 32]); ww_x2 = b.din("ww_x2", [C, NH * 32])
    w_winv = b.din("w_winv", [C, NH * 32])
    ws_nope = b.din("ws_nope", [C, NH * 32]); ws_x1 = b.din("ws_x1", [C, NH * 32]); ws_x2 = b.din("ws_x2", [C, NH * 32])
    w_selv = b.din("w_selv", [C, NH * 32])
    w_proj = b.din("w_proj", [NH * DV, C])
    w1t = b.din("w1t", [C, HF]); b1d = b.din("b1", [1, HF])
    w2t = b.din("w2t", [HF, C]); b2d = b.din("b2", [1, C])
    wg1 = b.din("wg1", [C, 256]); bg1 = b.din("bg1", [256, 1])
    wg2 = b.din("wg2", [256, 40]); bg2 = b.din("bg2", [40, 1])
    we = b.din("we", [NE, C, C]); be = b.din("be", [NE, C])
    constsf = b.din("constsf", [128, 2 * T + 264], F32)
    constsk = b.din("constsk", [128, 2 * T], F32)
    out = nc.dram_tensor("out", [T, C], F32, kind="ExternalOutput").ap()

    resT = sbuf("resT", [128, NTC * C], F32)
    identf = sbuf("identf", [128, 128], F32)
    gcol3 = sbuf("gcol3", [128, 3], BF16)
    iota_p = sbuf("iota_p", [128, 1], F32)
    tiab = sbuf("tiab", [128, 264], BF16)
    identb = tiab[:, 0:128]
    Eseg4 = tiab[:, 128:132]
    Eall64 = tiab[:, 132:136]
    trimask = tiab[:, 136:264]
    scr = sbuf("scr", [128, 2 * C], F32)
    scr2 = sbuf("scr2", [128, 2 * C], F32)
    mcol = sbuf("mcol", [128, NTC], F32)
    rcol = sbuf("rcol", [128, NTC], F32)
    vcol = sbuf("vcol", [128, NTC], F32)
    qmaxcol = sbuf("qmaxcol", [128, NTC], F32)
    qssqcol = sbuf("qssqcol", [128, NTC], F32)
    bnst = sbuf("bnst", [128, 6], F32)
    wsm = sbuf("wsm", [128, 4 * NCC], F32)
    wsm_hi = sbuf("wsm_hi", [128, 4 * NCC], BF16)
    wsm_lo = sbuf("wsm_lo", [128, 4 * NCC], BF16)
    onesrow = sbuf("onesrow", [1, 128], BF16)

    TCs = [slice(tc * C, (tc + 1) * C) for tc in range(NTC)]
    R_SC, R_WRK, R_MSK, R_RANK = 0, 32, 64, 96

    def G(g):
        inc("gp", g.iota(iota_p[:], pattern=[[0, 1]], base=0, channel_multiplier=1,
               allow_small_or_imprecise_dtypes=True))
    GE(G)

    def G(g):
        inc("ldcb", g.dma_start(tiab[:], constsf[:, 2 * T:2 * T + 264]), 16)
        inc("ldcb", g.dma_start(identf[:], constsf[:, 2 * T:2 * T + 128]), 16)
    GE(G)

    def S(sy):
        inc("ldsm", sy.dma_start(wsm[:].rearrange("p (cc k) -> p cc k", cc=NCC), w_small.rearrange("(cc p) k -> p cc k", p=128)), 16)
    SY(S)

    def V(v):
        wt(v, "gp")
        v.memset(onesrow[:], 1.0)
        wt(v, "ldcb")
        b.bar(v, v.tensor_copy(identf[:], identb[:]))
        wt(v, "ldsm")
        b.bar(v, v.tensor_copy(wsm_hi[:], wsm[:]))
        b.bar(v, v.tensor_sub(scr[:, 0:4 * NCC], wsm[:], wsm_hi[:]))
        v.tensor_copy(wsm_lo[:], scr[:, 0:4 * NCC])
        inc("dve", v.engine_nop())
    VE(V)

    if stop_after == "init":
        def S(sy):
            wt(sy, "auxq")
            wt(sy, "xload")
            wt(sy, "dve")
            inc("outq", sy.dma_start(out.rearrange("(tc p) c -> p tc c", p=128), resT[:].rearrange("p (tc c) -> p tc c", tc=NTC)), 16)
            wt(sy, "outq")
            wt(sy, "stq", val("stq"))
        SY(S)
        b.es.close()
        return nc, b.dram, b.DBG

    stBC = ExitStack()
    xnF = sbuf("xnF", [128, NCC * T], BF16, stBC)
    selF = sbuf("selF", [128, NCC * KEEP], BF16, stBC)
    Ocmb = sbuf("Ocmb", [128, NTC * 512], BF16, stBC)

    stA = ExitStack()
    rows = sbuf("rows", [128, T], F32, stA)
    xnT = sbuf("xnT", [128, NTC * C], BF16, stA)
    stA2 = ExitStack()
    xlF_tc = sbuf("xlF_tc", [128, C], BF16, stA2)
    xn32 = scr2[:, 0:C]
    stP1 = ExitStack()
    p_trh = [psum("p_trh%d" % i, [128, 1024], BF16, stP1) for i in range(2)]
    p_trl = [psum("p_trl%d" % i, [128, 1024], BF16, stP1) for i in range(2)]
    p_sc = psum("p_sc", [4, T], F32, stP1)

    xc = sbuf("xc", [128, 2 * C], F32, stA2)

    def V(v):
        v.memset(rows[:], 0.0)
        inc("dve", v.engine_nop())
    VE(V)

    for tc in range(NTC):
        xbuf = xc[:, (tc % 2) * C:(tc % 2) * C + C]

        def S(sy, tc=tc):
            if tc >= 2:
                wt(sy, "xdone", 2 * (tc - 1))
            inc("xload", sy.dma_start(xc[:, (tc % 2) * C:(tc % 2) * C + C],
                                      x_in[tc * 128:(tc + 1) * 128, :]), 16)
        SY(S)

        def V(v, tc=tc, xbuf=xbuf):
            wt(v, "xload", (tc + 1) * 16)
            inc("xdone", v.reduce_sum(mcol[:, tc:tc + 1], xbuf, AX.X))
        VE(V)

        if tc % 8 == 7:
            def S(s, tc=tc):
                for t2 in range(tc - 7, tc + 1):
                    wt(s, "xload", (t2 + 1) * 16)
                    inc("act", s.activation(scr2[:, C:2 * C], xc[:, (t2 % 2) * C:(t2 % 2) * C + C],
                                            ACTF.Square, accum_out=vcol[:, t2:t2 + 1]))
                    inc("xdone", s.nop())
            SE(S)

    def V(v):
        wt(v, "act")
        b.bar(v, v.engine_nop())
        b.bar(v, v.tensor_scalar(mcol[:], mcol[:], 1.0 / C, None, ALU.mult))
        b.bar(v, v.tensor_mul(scr2[:, 0:NTC], mcol[:], mcol[:]))
        b.bar(v, v.scalar_tensor_tensor(vcol[:], vcol[:], 1.0 / C, scr2[:, 0:NTC],
                                        ALU.mult, ALU.subtract))
        v.tensor_scalar(scr2[:, 0:NTC], vcol[:], 1e-5, None, ALU.add)
        inc("dve", v.engine_nop())
    VE(V)

    def S(s):
        wt(s, "dve")
        inc("act", s.activation(scr2[:, NTC:2 * NTC], scr2[:, 0:NTC], ACTF.Sqrt))
    SE(S)

    def V(v):
        wt(v, "act")
        b.bar(v, v.reciprocal(rcol[:], scr2[:, NTC:2 * NTC]))
        inc("lnstats", v.engine_nop())
    VE(V)

    _p2act = []
    for tc in range(NTC):
        def S(sy, tc=tc):
            if tc == 0:
                wt(sy, "xdone")
            else:
                wt(sy, "xdone", 32 + tc)
            inc("xload", sy.dma_start(xc[:, (tc % 2) * C:(tc % 2) * C + C],
                                      x_in[tc * 128:(tc + 1) * 128, :]), 16)
        SY(S)

        def V(v, tc=tc):
            wt(v, "xload", (NTC + tc + 1) * 16)
            if tc == 0:
                wt(v, "lnstats")
            else:
                wt(v, "peS", tc)
                wt(v, "act", _p2act[tc - 1])
            v.tensor_scalar(xn32, xc[:, (tc % 2) * C:(tc % 2) * C + C],
                            mcol[:, tc:tc + 1], rcol[:, tc:tc + 1],
                            ALU.subtract, ALU.mult)
            v.tensor_copy(xnT[:, TCs[tc]], xn32)
            v.tensor_sub(xlF_tc[:], xn32, xnT[:, TCs[tc]])
            v.tensor_reduce(qmaxcol[:, tc:tc + 1], xn32, AX.X, ALU.max, apply_absolute_value=True)
            inc("xdone", v.engine_nop())
            inc("dve", v.engine_nop())
        VE(V)

        def S(s, tc=tc):
            wt(s, "dve")
            inc("act", s.activation(scr2[:, C:2 * C], xn32, ACTF.Square, accum_out=qssqcol[:, tc:tc + 1]))
        SE(S)
        _p2act.append(val("act"))

        buf = tc % 2

        def Tt(t, tc=tc, buf=buf):
            wt(t, "dve")
            if tc >= 2:
                wt(t, "evA", tc - 1)
            for cc in range(NCC):
                t.transpose(p_trh[buf][:, cc * 128:(cc + 1) * 128],
                            xnT[:, tc * C + cc * 128:tc * C + (cc + 1) * 128], identb)
            for cc in range(NCC):
                last = t.transpose(p_trl[buf][:, cc * 128:(cc + 1) * 128],
                                   xlF_tc[:, cc * 128:(cc + 1) * 128], identb)
            inc("peT", last)
        TE(Tt)

        def V(v, tc=tc, buf=buf):
            wt(v, "peT")
            v.tensor_copy(
                xnF[:].rearrange("p (cc t) -> p cc t", cc=NCC)[:, :, tc * 128:(tc + 1) * 128],
                p_trh[buf][:].rearrange("p (cc t) -> p cc t", cc=NCC))
            v.tensor_copy(xlF_tc[:].rearrange("p (cc t) -> p cc t", cc=NCC),
                          p_trl[buf][:].rearrange("p (cc t) -> p cc t", cc=NCC))
            inc("evA", v.engine_nop())
        VE(V)

        def Tt(t, tc=tc):
            wt(t, "evA")
            for cc in range(NCC):
                xh_tile = xnF[:, cc * T + tc * 128: cc * T + (tc + 1) * 128]
                xl_tile = xlF_tc[:, cc * 128:(cc + 1) * 128]
                t.matmul(p_sc[:, tc * 128:(tc + 1) * 128], wsm_hi[:, 4 * cc:4 * cc + 4], xh_tile,
                         start=(cc == 0), stop=False)
                t.matmul(p_sc[:, tc * 128:(tc + 1) * 128], wsm_lo[:, 4 * cc:4 * cc + 4], xh_tile,
                         start=False, stop=False)
                last = t.matmul(p_sc[:, tc * 128:(tc + 1) * 128], wsm_hi[:, 4 * cc:4 * cc + 4], xl_tile,
                                start=False, stop=(cc == NCC - 1))
            inc("peS", last)
        TE(Tt)

    def V(v):
        wt(v, "peS")
        v.tensor_copy(rows[0:4, :], p_sc[0:4, :])
        v.tensor_copy(rows[R_WRK:R_WRK + 1, :], p_sc[0:1, :])
        inc("scores", v.engine_nop())
    VE(V)

    stP1.close()

    stG = ExitStack()
    pG = psum("pG", [128, 512], F32, stG)

    def V(v):
        wt(v, "scores")
        b.bar(v, v.reduce_sum(scr[0:4, 0:1], rows[0:4, :], AX.X))
        inc("dve", v.engine_nop())
    VE(V)

    def Tt(t):
        wt(t, "dve")
        inc("peG", t.transpose(pG[0:1, 0:4], scr[0:4, 0:1], identf[0:4, 0:4]))
    TE(Tt)

    def V(v):
        wt(v, "peG")
        b.bar(v, v.tensor_scalar(scr2[0:1, 0:4], pG[0:1, 0:4], 1.0 / T, None, ALU.mult))
        b.bar(v, v.tensor_reduce(scr2[0:1, 8:9], scr2[0:1, 1:4], AX.X, ALU.max))
        b.bar(v, v.tensor_scalar(scr2[0:1, 12:15], scr2[0:1, 1:4], scr2[0:1, 8:9], None, ALU.subtract))
        inc("dve", v.engine_nop())
    VE(V)

    def S(s):
        wt(s, "dve")
        inc("act", s.activation(scr2[0:1, 16:19], scr2[0:1, 12:15], ACTF.Exp))
    SE(S)

    def V(v):
        wt(v, "act")
        b.bar(v, v.reduce_sum(scr2[0:1, 20:21], scr2[0:1, 16:19], AX.X))
        b.bar(v, v.reciprocal(scr2[0:1, 24:27], scr2[0:1, 16:19]))
        b.bar(v, v.tensor_scalar(scr2[0:1, 28:31], scr2[0:1, 24:27], scr2[0:1, 20:21], None, ALU.mult))
        b.bar(v, v.tensor_copy(scr.bitcast(BF16)[0:1, 2048:2051], scr2[0:1, 28:31]))
        inc("dve", v.engine_nop())
    VE(V)

    def Tt(t):
        wt(t, "dve")
        inc("peG", t.matmul(pG[:, 8:11], onesrow[:], scr.bitcast(BF16)[0:1, 2048:2051], start=True, stop=True))
    TE(Tt)

    def V(v):
        wt(v, "peG")
        b.bar(v, v.tensor_copy(gcol3[:], pG[:, 8:11]))
        inc("evG", v.engine_nop())
    VE(V)

    stG.close()


    b.dbg_dma("xnF", xnF[:], waits=("scores",))
    b.dbg_dma("mcol", mcol[:], waits=("scores",))
    b.dbg_dma("vcol", vcol[:], waits=("scores",))
    b.dbg_dma("rcol", rcol[:], waits=("scores",))
    b.dbg_dma("xnT", xnT[:], waits=("scores",))

    if stop_after == "lnloop":
        def S(sy):
            wt(sy, "scores")
            inc("outq", sy.dma_start(out.rearrange("(tc p) c -> p tc c", p=128), resT[:].rearrange("p (tc c) -> p tc c", tc=NTC)), 16)
            wt(sy, "outq")
            wt(sy, "stq", val("stq"))
        SY(S)
        stA.close()
        b.es.close()
        return nc, b.dram, b.DBG

    stA2.close()
    mx8 = sbuf("mx8", [1, 8], F32, stA)
    rankb = scr2.bitcast(BF16)[0:1, 2048:2048 + T]
    STOPTOPK = (stop_after == "topk")
    pselT = sbuf("pselT", [128, 2 * T], BF16, stA)
    psel = sbuf("psel", [128, NTC * KEEP], BF16, stA)
    stP2 = ExitStack()
    p_bc = psum("p_bc", [128, T], F32, stP2)

    def V(v):
        wt(v, "scores")
        for i in range(KEEP // 8):
            b.bar(v, v.max(out=mx8[:], in_=rows[R_WRK:R_WRK + 1, :]))
            v.match_replace(out=rows[R_WRK:R_WRK + 1, :], in_to_replace=mx8[:],
                            in_values=rows[R_WRK:R_WRK + 1, :], imm_value=-1e30)
        v.tensor_scalar(scr[32:33, 0:T], rows[R_WRK:R_WRK + 1, :], -1e29, None, ALU.is_le)
        v.tensor_copy(rows[R_MSK:R_MSK + 1, :], scr[32:33, 0:T])
        v.tensor_tensor_scan(rows[R_WRK:R_WRK + 1, :], scr[32:33, 0:T],
                             scr[32:33, 0:T], 0.0, ALU.add, ALU.bypass)
        v.tensor_copy(rows[R_RANK:R_RANK + 1, :], rows[R_WRK:R_WRK + 1, :])
        v.tensor_mul(rows[R_WRK:R_WRK + 1, :], rows[R_WRK:R_WRK + 1, :], scr[32:33, 0:T])
        v.scalar_tensor_tensor(rows[R_WRK:R_WRK + 1, :], scr[32:33, 0:T], -300.0,
                               rows[R_WRK:R_WRK + 1, :], ALU.mult, ALU.add)
        v.tensor_scalar(rows[R_WRK:R_WRK + 1, :], rows[R_WRK:R_WRK + 1, :], 300.0, None, ALU.add)
        v.tensor_copy(rankb, rows[R_WRK:R_WRK + 1, :])
        inc("dve", v.engine_nop())
    VE(V)

    b.dbg_dma("rows", rows[:], waits=("dve",))

    if STOPTOPK:
        def S(sy):
            wt(sy, "dve")
            inc("outq", sy.dma_start(out.rearrange("(tc p) c -> p tc c", p=128), resT[:].rearrange("p (tc c) -> p tc c", tc=NTC)), 16)
            wt(sy, "outq")
            wt(sy, "stq", val("stq"))
        SY(S)
        stP2.close()
        stA.close()
        b.es.close()
        return nc, b.dram, b.DBG

    def Tt(t):
        wt(t, "dve")
        for q4 in range(4):
            last = t.matmul(p_bc[:, q4 * 512:(q4 + 1) * 512], onesrow[:], rankb[0:1, q4 * 512:(q4 + 1) * 512],
                            start=True, stop=True)
        inc("peB", last)
    TE(Tt)

    def V(v):
        wt(v, "peB")
        v.tensor_scalar(scr[:, 0:1], iota_p[:], 1.0, None, ALU.add)
        b.bar(v, v.tensor_scalar(scr[:, 1:2], iota_p[:], 129.0, None, ALU.add))
        v.tensor_scalar(pselT[:, 0:T], p_bc[:], scr[:, 0:1], None, ALU.is_equal)
        v.tensor_scalar(pselT[:, T:2 * T], p_bc[:], scr[:, 1:2], None, ALU.is_equal)
        inc("evB", v.engine_nop())
    VE(V)

    p_trA2 = [psum("p_trA2_%d" % i, [128, 1024], BF16, stP2) for i in range(2)]
    for grp in range(4):
        h, jg = grp // 2, grp % 2
        gbuf = grp % 2

        def Tt(t, h=h, jg=jg, gbuf=gbuf, grp=grp):
            wt(t, "evB")
            if grp >= 2:
                wt(t, "evB2", grp - 1)
            for i in range(8):
                j = 8 * jg + i
                last = t.transpose(p_trA2[gbuf][:, i * 128:(i + 1) * 128],
                                   pselT[:, h * T + j * 128: h * T + (j + 1) * 128], identb)
            inc("peB", last)
        TE(Tt)

        def V(v, h=h, jg=jg, gbuf=gbuf):
            wt(v, "peB")
            v.tensor_copy(
                psel[:].rearrange("p (j r) -> p j r", r=KEEP)[:, 8 * jg:8 * jg + 8,
                                                              128 * h:128 * h + 128],
                p_trA2[gbuf][:].rearrange("p (i t) -> p i t", i=8))
            inc("evB2", v.engine_nop())
        VE(V)

    if stop_after == "psel":
        def S(sy):
            wt(sy, "evB2")
            inc("outq", sy.dma_start(out.rearrange("(tc p) c -> p tc c", p=128), resT[:].rearrange("p (tc c) -> p tc c", tc=NTC)), 16)
            wt(sy, "outq")
            wt(sy, "stq", val("stq"))
        SY(S)
        stP2.close()
        stA.close()
        b.es.close()
        return nc, b.dram, b.DBG

    p_sel = [psum("p_selg%d" % i, [128, 256], F32, stP2) for i in range(2)]
    for cs in range(NCC):
        def Tt(t, cs=cs):
            wt(t, "evB2")
            if cs >= 2:
                wt(t, "evSel", cs - 1)
            for tc in range(NTC):
                last = t.matmul(p_sel[cs % 2][:],
                                xnT[:, tc * C + cs * 128: tc * C + cs * 128 + 128],
                                psel[:, tc * KEEP:(tc + 1) * KEEP],
                                start=(tc == 0), stop=(tc == NTC - 1))
            inc("peSel", last)
        TE(Tt)

        def V(v, cs=cs):
            wt(v, "peSel")
            v.tensor_copy(selF[:, cs * KEEP:(cs + 1) * KEEP], p_sel[cs % 2][:])
            inc("evSel", v.engine_nop())
        VE(V)

    b.dbg_dma("selF", selF[:], waits=("evSel",))

    stP2.close()
    stA.close()

    stB = ExitStack()
    c4s4 = sbuf("c4s4", [128, 2 * T], BF16, stB)
    C4s = c4s4[:, 0:T]; S4s = c4s4[:, T:2 * T]

    def G(g):
        wt(g, "evSel")
        wt(g, "peSel")
        inc("ldcb", g.dma_start(c4s4[:], constsf[:, 0:2 * T]), 16)
    GE(G)

    nqb = sbuf("nqb", [96, T], BF16, stB)
    ckvb = sbuf("ckvb", [32, T], BF16, stB)
    krr = scr2.bitcast(BF16)[0:64, 2048:2048 + T]
    sqs = scr.bitcast(BF16)[:, 2048:2048 + T]
    qpkn = sbuf("qpkn", [128, T], BF16, stB)
    qpk1 = sbuf("qpk1", [128, T], BF16, stB)
    qpk2 = sbuf("qpk2", [128, T], BF16, stB)
    Qq = sbuf("Qq", [128, 4 * T], BF16, stB)
    Kq = sbuf("Kq", [128, T], BF16, stB)
    Kq1 = resT.bitcast(BF16)[:, 0:T]
    V1T = sbuf("V1T", [128, 16 * 4 * 33], BF16, stB)
    PTa = sbuf("PTa", [128, 16 * 512], BF16, stB)
    KSq = PTa[:, 4096:4096 + 4 * KEEP]
    VST = PTa[:, 5200:5200 + 2 * 4 * 33]
    Bcol = sbuf("Bcol", [128, 4], F32, stB)
    bq4 = sbuf("bq4", [4, 8], F32, stB)
    wq3 = sbuf("wq3", [96, 3 * 128], BF16, stB)
    wk1q = sbuf("wk1q", [32, 2 * 128], BF16, stB)
    wwX = sbuf("wwX", [128, NCC * 128], BF16, stB)
    onescolb = sbuf("onescolb", [128, 1], BF16, stB)
    NQC = 4
    SC2B = scr2.bitcast(BF16)

    def V(v):
        v.memset(onescolb[:], 1.0)
        inc("dve", v.engine_nop())
    VE(V)

    stN = ExitStack()
    wcq_s = PTa[:, 4096:4096 + NCC * 96]
    wckv_s = PTa[:, 4864:4864 + NCC * 32]
    wkr_s = PTa[:, 5120:5120 + NCC * 64]
    kcst = PTa[:, 0:2 * T]
    KCS = kcst[:, 0:T]; KCS2 = kcst[:, T:2 * T]
    gq_s = sbuf("gq_s", [96, 1], F32, stN)
    gkv_s = sbuf("gkv_s", [32, 1], F32, stN)
    pA = psum("pA", [128, T], F32, stN)
    pB = psum("pB", [128, T], F32, stN)

    def S(sy):
        wt(sy, "evSel")
        inc("ldq2", sy.dma_start(gq_s[:], g_qnorm), 16)
        inc("ldq2", sy.dma_start(gkv_s[:], g_kvnorm), 16)
    SY(S)

    def G(g):
        wt(g, "evSel")
        wt(g, "peSel")
        inc("ldb", g.dma_start(wcq_s[:].rearrange("p (cc k) -> p cc k", cc=NCC),
                               w_cq.rearrange("(cc p) k -> p cc k", p=128)), 16)
        inc("ldb", g.dma_start(wckv_s[:].rearrange("p (cc k) -> p cc k", cc=NCC),
                               w_ckv.rearrange("(cc p) k -> p cc k", p=128)), 16)
        inc("ldb", g.dma_start(wkr_s[:].rearrange("p (cc k) -> p cc k", cc=NCC),
                               w_krope.rearrange("(cc p) k -> p cc k", p=128)), 16)
        inc("ldb", g.dma_start(kcst[:], constsk), 16)
    GE(G)

    def lora_norm(dst, w_s, kdim, g_ap):
        def Tt(t):
            wt(t, "ldb")
            wt(t, "evSel")
            wt(t, "evG")
            for qc in range(NQC):
                for cc in range(NCC):
                    last = t.matmul(pA[0:kdim, qc * 512:(qc + 1) * 512],
                                    w_s[:, cc * kdim:(cc + 1) * kdim],
                                    xnF[:, cc * T + qc * 512: cc * T + (qc + 1) * 512],
                                    start=(cc == 0), stop=(cc == NCC - 1))
            inc("peN", last)
        TE(Tt)

        def S(s):
            wt(s, "peN")
            s.activation(sqs[0:kdim, 0:T], pA[0:kdim, :], ACTF.Square)
            inc("act", s.activation(qpkn[0:kdim, :], pA[0:kdim, :], ACTF.Copy))
        SE(S)

        def Tt(t):
            wt(t, "act")
            for qc in range(NQC):
                last = t.matmul(pB[0:1, qc * 512:(qc + 1) * 512], onescolb[0:kdim, 0:1],
                                sqs[0:kdim, qc * 512:(qc + 1) * 512], start=True, stop=True)
            inc("peN", last)
        TE(Tt)

        def V(v):
            wt(v, "peN")
            v.tensor_scalar(scr2[0:1, 0:T], pB[0:1, 0:T], 1.0 / kdim, 1e-6, ALU.mult, ALU.add)
            inc("dve", v.engine_nop())
        VE(V)

        def S(s):
            wt(s, "dve")
            inc("act", s.activation(scr2[0:1, 1024:1024 + T] if False else scr2[0:1, 0:T],
                                    scr2[0:1, 0:T], ACTF.Sqrt))
        SE(S)

        def V(v):
            wt(v, "act")
            b.bar(v, v.reciprocal(scr2[0:1, 0:T], scr2[0:1, 0:T]))
            b.bar(v, v.tensor_copy(SC2B[0:1, 2048:2048 + T], scr2[0:1, 0:T]))
            inc("dve", v.engine_nop())
        VE(V)

        def Tt(t):
            wt(t, "dve")
            for qc in range(NQC):
                last = t.matmul(pB[:, qc * 512:(qc + 1) * 512], onesrow[:],
                                SC2B[0:1, 2048 + qc * 512:2048 + (qc + 1) * 512],
                                start=True, stop=True)
            inc("peN", last)
        TE(Tt)

        def V(v):
            wt(v, "peN")
            v.scalar_tensor_tensor(dst[:], qpkn[0:kdim, :], g_ap, pB[0:kdim, 0:T],
                                   ALU.mult, ALU.mult)
            inc("evN", v.engine_nop())
        VE(V)

    lora_norm(nqb, wcq_s, 96, gq_s[0:96, 0:1])
    lora_norm(ckvb, wckv_s, 32, gkv_s[0:32, 0:1])

    def Tt(t):
        wt(t, "evN")
        for qc in range(NQC):
            for cc in range(NCC):
                last = t.matmul(pA[0:64, qc * 512:(qc + 1) * 512],
                                wkr_s[:, cc * 64:(cc + 1) * 64],
                                xnF[:, cc * T + qc * 512: cc * T + (qc + 1) * 512],
                                start=(cc == 0), stop=(cc == NCC - 1))
        inc("peN", last)
    TE(Tt)

    def V(v):
        wt(v, "peN")
        v.tensor_mul(krr[0:32, :], pA[0:32, 0:T], KCS[0:32, :])
        v.tensor_mul(scr[0:32, 0:T], pA[32:64, 0:T], KCS[32:64, :])
        v.tensor_sub(krr[0:32, :], krr[0:32, :], scr[0:32, 0:T])
        v.tensor_mul(krr[32:64, :], pA[0:32, 0:T], KCS2[0:32, :])
        v.tensor_mul(scr[32:64, 0:T], pA[32:64, 0:T], KCS2[32:64, :])
        v.tensor_add(krr[32:64, :], krr[32:64, :], scr[32:64, 0:T])
        inc("evN", v.engine_nop())
    VE(V)

    stN.close()

    def rope_packs(pX1, pX2, dst1, dst2, ncols, sem_in, sem_out):
        def V(v):
            wt(v, sem_in)
            v.tensor_mul(dst1[:, 0:ncols], pX1[:, 0:ncols], C4s[:, 0:ncols])
            v.tensor_mul(scr[:, 0:ncols], pX2[:, 0:ncols], S4s[:, 0:ncols])
            v.tensor_sub(dst1[:, 0:ncols], dst1[:, 0:ncols], scr[:, 0:ncols])
            v.tensor_mul(dst2[:, 0:ncols], pX1[:, 0:ncols], S4s[:, 0:ncols])
            v.tensor_mul(scr[:, 0:ncols], pX2[:, 0:ncols], C4s[:, 0:ncols])
            v.tensor_add(dst2[:, 0:ncols], dst2[:, 0:ncols], scr[:, 0:ncols])
            inc(sem_out, v.engine_nop())
        VE(V)

    def seg_sq(packs, ncols, out_col, extra=None, sem_pe="peQ", sem_ve="evQ"):
        items = [(pk, Eseg4) for pk in packs] + ([extra] if extra else [])
        for i, (pk, EM) in enumerate(items):
            def V(v, pk=pk):
                wt(v, sem_pe, val(sem_pe))
                kd = pk.shape[0]
                v.tensor_mul(sqs[0:kd, 0:ncols], pk[:, 0:ncols], pk[:, 0:ncols])
                inc(sem_ve, v.engine_nop())
            VE(V)

            def Tt(t, i=i, EM=EM, pk=pk):
                wt(t, sem_ve)
                kd = pk.shape[0]
                nch = (ncols + 511) // 512
                for qc in range(nch):
                    w = min(512, ncols - qc * 512)
                    last = t.matmul(pB[0:4, qc * 512:qc * 512 + w], EM[0:kd, :],
                                    sqs[0:kd, qc * 512:qc * 512 + w],
                                    start=(i == 0), stop=(i == len(items) - 1))
                inc(sem_pe, last)
            TE(Tt)

        def V(v):
            wt(v, sem_pe)
            b.bar(v, v.tensor_reduce(bq4[0:4, out_col:out_col + 1], pB[0:4, 0:ncols], AX.X, ALU.max))
            inc(sem_ve, v.engine_nop())
        VE(V)

    for qq in range(4):
        stP = ExitStack()
        pA = psum("pAq%d" % qq, [128, T], F32, stP)
        pB = psum("pBq%d" % qq, [128, T], F32, stP)

        def G(g, qq=qq):
            wt(g, "evQ", val("evQ"))
            inc("ldb", g.dma_start(wq3[:, 0:128], wq_nope[:, qq * 128:(qq + 1) * 128]), 16)
            inc("ldb", g.dma_start(wq3[:, 128:256], wq_x1[:, qq * 128:(qq + 1) * 128]), 16)
            inc("ldb", g.dma_start(wq3[:, 256:384], wq_x2[:, qq * 128:(qq + 1) * 128]), 16)
            inc("ldb", g.dma_start(wk1q[:, 0:128], wk_nope[:, qq * 128:(qq + 1) * 128]), 16)
            inc("ldb", g.dma_start(wk1q[:, 128:256], w_dv[:, qq * 128:(qq + 1) * 128]), 16)
        GE(G)

        def Tt(t):
            wt(t, "ldb")
            wt(t, "evN")
            wt(t, "evQ", val("evQ"))
            wt(t, "evO", val("evO"))
            for qc in range(NQC):
                t.matmul(pA[:, qc * 512:(qc + 1) * 512], wq3[:, 128:256],
                         nqb[:, qc * 512:(qc + 1) * 512], start=True, stop=True)
            for qc in range(NQC):
                last = t.matmul(pB[:, qc * 512:(qc + 1) * 512], wq3[:, 256:384],
                                nqb[:, qc * 512:(qc + 1) * 512], start=True, stop=True)
            inc("peQ", last)
        TE(Tt)

        rope_packs(pA, pB, qpk1, qpk2, T, "peQ", "evQ")

        def Tt(t):
            wt(t, "evQ")
            for qc in range(NQC):
                last = t.matmul(pA[:, qc * 512:(qc + 1) * 512], wq3[:, 0:128],
                                nqb[:, qc * 512:(qc + 1) * 512], start=True, stop=True)
            inc("peQ", last)
        TE(Tt)

        def V(v):
            wt(v, "peQ")
            v.tensor_copy(qpkn[:], pA[:])
            inc("evQ", v.engine_nop())
        VE(V)

        seg_sq([qpkn, qpk1, qpk2], T, 0, sem_pe="peQ", sem_ve="evQ")

        def S(sy):
            wt(sy, "evQ")
            for hh in range(4):
                inc("asmq", sy.dma_start(Qq[0:32, hh * T:(hh + 1) * T], qpkn[32 * hh:32 * hh + 32, :]), 16)
                inc("asmq", sy.dma_start(Qq[32:64, hh * T:(hh + 1) * T], qpk1[32 * hh:32 * hh + 32, :]), 16)
                inc("asmq", sy.dma_start(Qq[64:96, hh * T:(hh + 1) * T], qpk2[32 * hh:32 * hh + 32, :]), 16)
        SY(S)

        def Tt(t):
            wt(t, "evQ")
            for qc in range(NQC):
                last = t.matmul(pA[:, qc * 512:(qc + 1) * 512], wk1q[:, 0:128],
                                ckvb[:, qc * 512:(qc + 1) * 512], start=True, stop=True)
            inc("peQ", last)
        TE(Tt)

        def V(v):
            wt(v, "peQ")
            wt(v, "asmq")
            v.tensor_copy(qpkn[:], pA[:])
            inc("evQ", v.engine_nop())
        VE(V)

        seg_sq([qpkn], T, 1, extra=(krr, Eall64), sem_pe="peQ", sem_ve="evQ")

        def Tt(t):
            wt(t, "evQ")
            for qc in range(NQC):
                last = t.matmul(pA[:, qc * 512:(qc + 1) * 512], wk1q[:, 128:256],
                                ckvb[:, qc * 512:(qc + 1) * 512], start=True, stop=True)
            inc("peQ", last)
        TE(Tt)

        def V(v):
            wt(v, "peQ")
            wt(v, "asmq")
            v.tensor_copy(qpk1[:], pA[:])
            inc("evQ", v.engine_nop())
        VE(V)

        stP.close()

        def attn_branch(qq, br, ktn, ncols, Ktiles, vtab, vn, causal):
            def kslice(j, hh, kt):
                if causal:
                    kq = Kq if hh % 2 == 0 else Kq1
                    return kq[0:96, kt * 128:(kt + 1) * 128]
                return KSq[0:96, hh * KEEP + kt * 128: hh * KEEP + (kt + 1) * 128]
            kcol = {0: 1, 2: 2, 1: 3}[br]
            _brPeO = val("peO")
            _kqPeO = []

            def V(v):
                wt(v, "evVT", val("evVT"))
                b.bar(v, v.tensor_mul(scr2[0:4, 32:33], bq4[0:4, 0:1], bq4[0:4, kcol:kcol + 1]))
                inc("dve", v.engine_nop())
            VE(V)

            def S(s):
                wt(s, "dve")
                inc("act", s.activation(scr2[0:4, 33:34], scr2[0:4, 32:33], ACTF.Sqrt))
            SE(S)

            def V(v):
                wt(v, "act")
                b.bar(v, v.tensor_scalar(scr2[0:4, 34:35], scr2[0:4, 33:34], -SQK, None, ALU.mult))
                inc("dve", v.engine_nop())
            VE(V)

            def Tt(t):
                wt(t, "dve")
                wt(t, "evQ", val("evQ"))
                wt(t, "evO", val("evO"))
                inc("peB2", t.transpose(pS[0][0:1, 1000:1004], scr2[0:4, 34:35], identf[0:4, 0:4]))
            TE(Tt)

            def V(v):
                wt(v, "peB2")
                b.bar(v, v.tensor_copy(scr.bitcast(BF16)[0:1, 3800:3804], pS[0][0:1, 1000:1004]))
                inc("dve", v.engine_nop())
            VE(V)

            def Tt(t):
                wt(t, "dve")
                inc("peB2", t.matmul(pS[1][:, 1000:1004], onesrow[:], scr.bitcast(BF16)[0:1, 3800:3804],
                                     start=True, stop=True))
            TE(Tt)

            def V(v):
                wt(v, "peB2")
                b.bar(v, v.tensor_copy(Bcol[:], pS[1][:, 1000:1004]))
                inc("dve", v.engine_nop())
            VE(V)

            for hh in range(4):
                hq = hh
                h_g = qq * 4 + hh
                def S(sy, hh=hh):
                    wt(sy, "evQ", val("evQ"))
                    if hh >= 2:
                        wt(sy, "peO", _kqPeO[hh - 2])
                    else:
                        wt(sy, "peO", _brPeO)
                    kq = Kq if hh % 2 == 0 else Kq1
                    for part, buf in Ktiles:
                        inc("asmq", sy.dma_start(
                            kq[part, 0:ncols],
                            buf[32 * hh:32 * hh + 32, 0:ncols] if buf.shape[0] == 128
                            else buf[:, 0:ncols]), 16)
                SY(S)

                if True:
                    exb = val("expS")
                    psb = val("peS2")
                    pob = val("peO")
                    sched = []
                    cum = 0
                    for qb in range(4):
                        q0 = qb * 512
                        ktmax = min(ktn, 4 * qb + 4) if causal else ktn
                        npair = (ktmax + 1) // 2
                        for jp, ktp in enumerate(range(0, ktmax, 2)):
                            gp = cum + jp

                            def Tt(t, hh=hh, qb=qb, ktp=ktp, ktmax=ktmax, gp=gp, q0=q0):
                                if ktp == 0 and qb == 0:
                                    wt(t, "asmq")
                                    wt(t, "evVT", val("evVT"))
                                par = (ktp // 2) % 2
                                if exb + gp - 1 > 0:
                                    wt(t, "expS", exb + gp - 1)
                                for kk in range(2):
                                    kt = ktp + kk
                                    if kt >= ktmax:
                                        break
                                    last = t.matmul(
                                        pS[par][:, kk * 512:(kk + 1) * 512],
                                        kslice(0, hh, kt),
                                        Qq[0:96, hh * T + q0:hh * T + q0 + 512],
                                        start=True, stop=True)
                                inc("peS2", last)
                            TE(Tt)
                            sched.append((qb, ktp, gp, min(2, ktmax - ktp)))

                        if causal:
                            def V(v, qb=qb, cum=cum, npair=npair, exb=exb):
                                wt(v, "expS", exb + cum + npair)
                                for qs in range(4):
                                    qt_g = 4 * qb + qs
                                    v.tensor_mul(PTa[:, qt_g * 512 + qs * 128:qt_g * 512 + qs * 128 + 128],
                                                 PTa[:, qt_g * 512 + qs * 128:qt_g * 512 + qs * 128 + 128],
                                                 trimask)
                                inc("dveM", v.engine_nop())
                            VE(V)

                        def Tt(t, qb=qb, hq=hq, causal=causal, ktn=ktn, vtab=vtab,
                               cum=cum, npair=npair, exb=exb):
                            wt(t, "expS", exb + cum + npair)
                            if qb == 0:
                                wt(t, "evO", val("evO"))
                            if causal:
                                wt(t, "dveM")
                            for qs in range(4):
                                qt_g = 4 * qb + qs
                                nkt = (qt_g + 1) if causal else ktn
                                half = qt_g // 8
                                col = 33 * (qt_g % 8)
                                for kt in range(nkt):
                                    last = t.matmul(
                                        pO[half][:, col:col + 33],
                                        PTa[:, kt * 512 + qs * 128:kt * 512 + qs * 128 + 128],
                                        vtab[:, (kt * 4 + hq) * 33:(kt * 4 + hq) * 33 + 33],
                                        start=(kt == 0), stop=(kt == nkt - 1))
                            inc("peO", last)
                        TE(Tt)
                        cum += npair

                    def S(s, hh=hh, sched=tuple(sched), exb=exb, psb=psb, pob=pob):
                        first = True
                        for (qb, ktp, gp, nk) in sched:
                            if ktp == 0:
                                wt(s, "peO", pob + qb)
                            wt(s, "peS2", psb + gp + 1)
                            if first and hh == 0:
                                wt(s, "dve")
                            first = False
                            par = (ktp // 2) % 2
                            inc("expS", s.activation(
                                PTa[:, ktp * 512:(ktp + nk) * 512],
                                pS[par][:, 0:nk * 512], ACTF.Exp,
                                bias=Bcol[:, hh:hh + 1], scale=SQK))
                    SE(S)

                    def V(v, br=br, h_g=h_g):
                        wt(v, "peO")
                        for half in range(2):
                            b.bar(v, v.reciprocal(
                                scr[:, 120 + half * 8:128 + half * 8],
                                pO[half][:].rearrange("p (q c) -> p q c", c=33)[:, :, 32:33].squeeze(2)))
                        for qt in range(NTC):
                            half, q8 = qt // 8, qt % 8
                            dst = Ocmb[:, qt * 512 + 32 * h_g:qt * 512 + 32 * h_g + 32]
                            src = pO[half][:, 33 * q8:33 * q8 + 32]
                            zs = scr[:, 120 + half * 8 + q8:121 + half * 8 + q8]
                            if br == 0:
                                v.tensor_scalar(dst, src, zs, None, ALU.mult)
                            else:
                                v.scalar_tensor_tensor(dst, src, zs, dst, ALU.mult, ALU.add)
                        inc("evO", v.engine_nop())
                    VE(V)

                _kqPeO.append(val("peO"))

        stL = ExitStack()
        pS = [psum("pSq%d_%d" % (qq, i), [128, 1024], F32, stL) for i in range(2)]
        pO = [psum("pOq%d_%d" % (qq, i), [128, 264], F32, stL) for i in range(2)]
        pVT = psum("pVTq%d" % qq, [128, 1024], BF16, stL)

        def vt_build(vtab, src_pack, nkt, invg_slice):
            for ktg in range(0, nkt, 8):
                def Tt(t, ktg=ktg, nkt=nkt, src_pack=src_pack):
                    wt(t, "evVT", val("evVT"))
                    wt(t, "evQ", val("evQ"))
                    wt(t, "evO", val("evO"))
                    for i in range(min(8, nkt - ktg)):
                        last = t.transpose(pVT[:, i * 128:(i + 1) * 128],
                                           src_pack[:, (ktg + i) * 128:(ktg + i + 1) * 128], identb)
                    inc("peVT", last)
                TE(Tt)

                def V(v, ktg=ktg, nkt=nkt, vtab=vtab):
                    wt(v, "peVT")
                    nn = min(8, nkt - ktg)
                    v.tensor_copy(
                        vtab[:].rearrange("p (k h c) -> p k h c", h=4, c=33)[:, ktg:ktg + nn, :, 0:32],
                        pVT[:, 0:nn * 128].rearrange("p (k h c) -> p k h c", h=4, c=32))
                    inc("evVT", v.engine_nop())
                VE(V)

            def V(v, vtab=vtab, nkt=nkt, invg_slice=invg_slice):
                wt(v, "evVT")
                v.tensor_copy(
                    vtab[:].rearrange("p (k c) -> p k c", c=33)[:, 0:nkt * 4, 32:33].squeeze(2),
                    invg_slice.to_broadcast([128, nkt * 4]))
                inc("evVT", v.engine_nop())
            VE(V)

        vt_build(V1T, qpk1, 16, gcol3[:, 0:1])

        attn_branch(qq, 0, 16, T, [(slice(0, 32), qpkn), (slice(32, 96), krr)], V1T, 1, True)

        stL.close()

        stP2b = ExitStack()
        pA = psum("pA2q%d" % qq, [128, T], F32, stP2b)
        pB = psum("pB2q%d" % qq, [128, T], F32, stP2b)

        def wwload(g, wsrc, qq=qq):
            inc("ldb", g.dma_start(wwX[:].rearrange("p (cc k) -> p cc k", k=128),
                                   wsrc.rearrange("(cc p) k -> p cc k", p=128)[:, :, qq * 128:(qq + 1) * 128]), 16)

        def wproj_T(dst_psum, wait_ld=True, extra_waits=()):
            def Tt(t):
                if wait_ld:
                    wt(t, "ldb")
                for w in extra_waits:
                    wt(t, w, val(w))
                for qc in range(NQC):
                    for cc in range(NCC):
                        last = t.matmul(dst_psum[:, qc * 512:(qc + 1) * 512], wwX[:, cc * 128:(cc + 1) * 128],
                                        xnF[:, cc * T + qc * 512:cc * T + (qc + 1) * 512],
                                        start=(cc == 0), stop=(cc == NCC - 1))
                inc("peQ", last)
            TE(Tt)

        def G(g, qq=qq):
            wt(g, "evQ", val("evQ"))
            wt(g, "evO", val("evO"))
            wwload(g, ww_x1)
        GE(G)

        wproj_T(pA, extra_waits=("evO", "evQ"))

        def G(g):
            wt(g, "peQ")
            wwload(g, ww_x2)
        GE(G)

        wproj_T(pB)

        def G(g):
            wt(g, "peQ")
            wwload(g, ww_nope)
        GE(G)

        rope_packs(pA, pB, qpk1, qpk2, T, "peQ", "evQ")

        wproj_T(pA, extra_waits=("evQ",))

        def V(v):
            wt(v, "peQ")
            v.tensor_copy(qpkn[:], pA[:])
            inc("evQ", v.engine_nop())
        VE(V)

        def G(g):
            wt(g, "peQ")
            wwload(g, w_winv)
        GE(G)

        seg_sq([qpkn, qpk1, qpk2], T, 2, sem_pe="peQ", sem_ve="evQ")

        wproj_T(pA, extra_waits=("evQ",))

        vwpack = sqs

        def V(v):
            wt(v, "peQ")
            v.tensor_copy(vwpack[0:128, :], pA[:])
            inc("evQ", v.engine_nop())
        VE(V)

        stP2b.close()

        stL2 = ExitStack()
        pS = [psum("pS3q%d_%d" % (qq, i), [128, 1024], F32, stL2) for i in range(2)]
        pO = [psum("pO3q%d_%d" % (qq, i), [128, 264], F32, stL2) for i in range(2)]
        pVT = psum("pVT3q%d" % qq, [128, 1024], BF16, stL2)

        vt_build(V1T, vwpack, 16, gcol3[:, 2:3])
        attn_branch(qq, 2, 16, T, [(slice(0, 32), qpkn), (slice(32, 64), qpk1), (slice(64, 96), qpk2)],
                    V1T, 1, True)

        def sproj_T(dst_psum, extra_waits=()):
            def Tt(t):
                wt(t, "ldb")
                for w in extra_waits:
                    wt(t, w, val(w))
                for cc in range(NCC):
                    last = t.matmul(dst_psum[:, 0:KEEP], wwX[:, cc * 128:(cc + 1) * 128],
                                    selF[:, cc * KEEP:(cc + 1) * KEEP],
                                    start=(cc == 0), stop=(cc == NCC - 1))
                inc("peQ", last)
            TE(Tt)

        def G(g, qq=qq):
            wt(g, "evQ", val("evQ"))
            wt(g, "evO", val("evO"))
            wwload(g, ws_x1)
        GE(G)

        sproj_T(pS[0], extra_waits=("evQ", "evO"))

        def G(g):
            wt(g, "peQ")
            wwload(g, ws_x2)
        GE(G)

        sproj_T(pS[1])

        def G(g):
            wt(g, "peQ")
            wwload(g, ws_nope)
        GE(G)

        rope_packs(pS[0], pS[1], qpk1, qpk2, KEEP, "peQ", "evQ")

        sproj_T(pS[0], extra_waits=("evQ",))

        def V(v):
            wt(v, "peQ")
            v.tensor_copy(qpkn[:, 0:KEEP], pS[0][:, 0:KEEP])
            inc("evQ", v.engine_nop())
        VE(V)

        def V(v):
            wt(v, "evQ")
            v.tensor_mul(sqs[:, 0:KEEP], qpkn[:, 0:KEEP], qpkn[:, 0:KEEP])
            inc("evQ", v.engine_nop())
        VE(V)

        def Tt(t):
            wt(t, "evQ")
            t.matmul(pS[1][0:4, 0:KEEP], Eseg4, sqs[:, 0:KEEP], start=True, stop=False)
            inc("peQ", t.matmul(pS[1][0:4, 0:KEEP], Eseg4, sqs[:, 0:KEEP], start=False, stop=False))
        TE(Tt)

        for pk in (qpk1, qpk2):
            def V(v, pk=pk):
                wt(v, "peQ")
                v.tensor_mul(sqs[:, 0:KEEP], pk[:, 0:KEEP], pk[:, 0:KEEP])
                inc("evQ", v.engine_nop())
            VE(V)

            def Tt(t, pk=pk):
                wt(t, "evQ")
                last = t.matmul(pS[1][0:4, 0:KEEP], Eseg4, sqs[:, 0:KEEP],
                                start=False, stop=(pk is qpk2))
                inc("peQ", last)
            TE(Tt)

        def V(v):
            wt(v, "peQ")
            b.bar(v, v.tensor_reduce(bq4[0:4, 3:4], pS[1][0:4, 0:KEEP], AX.X, ALU.max))
            for hh in range(4):
                v.tensor_copy(KSq[0:32, hh * KEEP:(hh + 1) * KEEP], qpkn[32 * hh:32 * hh + 32, 0:KEEP])
                v.tensor_copy(KSq[32:64, hh * KEEP:(hh + 1) * KEEP], qpk1[32 * hh:32 * hh + 32, 0:KEEP])
                v.tensor_copy(KSq[64:96, hh * KEEP:(hh + 1) * KEEP], qpk2[32 * hh:32 * hh + 32, 0:KEEP])
            inc("evQ", v.engine_nop())
        VE(V)

        def G(g):
            wt(g, "peQ")
            wwload(g, w_selv)
        GE(G)

        sproj_T(pS[0], extra_waits=("evQ",))

        def V(v):
            wt(v, "peQ")
            v.tensor_copy(vwpack[0:128, 0:KEEP], pS[0][:, 0:KEEP])
            inc("evQ", v.engine_nop())
        VE(V)

        vt_build(VST, vwpack, 2, gcol3[:, 1:2])
        attn_branch(qq, 1, 2, KEEP, [], VST, 1, False)

        stL2.close()

    b.dbg_dma("nqb", nqb[:], waits=("evN",))
    b.dbg_dma("ckvb", ckvb[:], waits=("evN",))
    b.dbg_dma("krr", krr[:], waits=("evN",))
    b.dbg_dma("Qq", Qq[0:96, :], waits=("evO",))
    b.dbg_dma("Ocmb", Ocmb[:], waits=("evO",))

    stB.close()

    w1i_d = nc.dram_tensor("w1i", [C, HF], BF16, kind="Internal").ap()
    w2i_d = nc.dram_tensor("w2i", [HF, C], BF16, kind="Internal").ap()
    sT_d = nc.dram_tensor("sTd", [T, C], BF16, kind="Internal").ap()
    TCH = 512
    NTCH = T // TCH

    def S(sy):
        wt(sy, "evO")
        inc("rstI", sy.dma_start(resT[:].rearrange("p (tc c) -> p tc c", tc=NTC),
                                 x_in.rearrange("(tc p) c -> p tc c", p=128)), 16)
    SY(S)

    stC1 = ExitStack()
    OT = sbuf("OT", [128, 4 * T], BF16, stC1)
    wprojb = sbuf("wprojb", [128, 4 * C], BF16, stC1)
    pstg = sbuf("pstg", [128, 2 * 512], BF16, stC1)
    stCP = ExitStack()
    pT1 = [psum("pT1%d" % i, [128, 1024], BF16, stCP) for i in range(2)]
    pP = [psum("pP%d" % i, [128, 512], F32, stCP) for i in range(2)]

    def G(g):
        wt(g, "evO")
        inc("ldc2", g.dma_start(wprojb[:].rearrange("p (j c) -> p j c", j=4),
                                w_proj.rearrange("(j p) c -> p j c", p=128)), 16)
    GE(G)

    for grp in range(8):
        def Tt(t, grp=grp):
            wt(t, "evO")
            if grp >= 2:
                wt(t, "evCT", grp - 1)
            for i in range(8):
                gi = grp * 8 + i
                qt, j = gi // 4, gi % 4
                last = t.transpose(pT1[grp % 2][:, i * 128:(i + 1) * 128],
                                   Ocmb[:, qt * 512 + j * 128:qt * 512 + (j + 1) * 128], identb)
            inc("peC", last)
        TE(Tt)

        def V(v, grp=grp):
            wt(v, "peC")
            for i in range(8):
                gi = grp * 8 + i
                qt, j = gi // 4, gi % 4
                v.tensor_copy(OT[:, j * T + qt * 128:j * T + (qt + 1) * 128],
                              pT1[grp % 2][:, i * 128:(i + 1) * 128])
            inc("evCT", v.engine_nop())
        VE(V)

    pair = 0
    for tile in range(NTC):
        for cb in range(2):
            def Tt(t, tile=tile, cb=cb, pair=pair):
                if pair == 0:
                    wt(t, "evCT")
                    wt(t, "ldc2")
                if pair >= 2:
                    wt(t, "evCs", pair - 1)
                for j in range(4):
                    last = t.matmul(pP[pair % 2][:],
                                    OT[:, j * T + tile * 128:j * T + (tile + 1) * 128],
                                    wprojb[:, j * C + cb * 512:j * C + (cb + 1) * 512],
                                    start=(j == 0), stop=(j == 3))
                inc("peC", last)
            TE(Tt)

            def V(v, pair=pair):
                wt(v, "peC")
                if pair >= 2:
                    wt(v, "sTo", 16 * (pair - 1))
                v.tensor_copy(pstg[:, (pair % 2) * 512:(pair % 2) * 512 + 512], pP[pair % 2][:])
                inc("evCs", v.engine_nop())
            VE(V)

            def S(sy, tile=tile, cb=cb, pair=pair):
                wt(sy, "evCs", pair + 1)
                inc("sTo", sy.dma_start(
                    sT_d[tile * 128:(tile + 1) * 128, cb * 512:(cb + 1) * 512],
                    pstg[:, (pair % 2) * 512:(pair % 2) * 512 + 512]), 16)
            SY(S)
            pair += 1

    b.dbg_dma("sTt", sT_d, waits=("sTo",))

    stCP.close()
    stC1.close()
    stBC.close()

    stD = ExitStack()
    U = sbuf("U", [128, 24576], BF16, stD)
    Ub = U
    h1B = sbuf("h1B", [128, 4 * HF], BF16, stD)
    yac = sbuf("yac", [128, 4096], F32, stD)
    sTb = sbuf("sTb", [128, 2 * 512], BF16, stD)
    xq = sbuf("xq", [128, 2 * C], F32, stD)
    b1b = sbuf("b1b", [1, HF], BF16, stD)
    b2b = sbuf("b2b", [1, C], BF16, stD)
    rowA = sbuf("rowA", [16, 2688], BF16, stD)
    A1R = rowA[0:1, 0:2048]
    A2R = rowA[0:1, 2048:2560]
    rstg = rowA[0:16, 2560:2688]
    ki32 = sbuf("ki32", [128, 1024], I32, stD)
    wcols = sbuf("wcols", [128, 80], F32, stD)
    fcols = sbuf("fcols", [128, 64], F32, stD)
    cm1c = fcols[:, 0:16]
    al1c = fcols[:, 16:32]
    a1c = fcols[:, 32:48]
    ftmp = fcols[:, 48:64]

    def G(g):
        wt(g, "evCs")
        wt(g, "sTo")
        inc("ldc2", g.dma_start(b1b[:], b1d), 16)
        inc("ldc2", g.dma_start(b2b[:], b2d), 16)
    GE(G)


    def V(v):
        wt(v, "peC")
        wt(v, "sTo")
        v.memset(wcols[:, 66:67], 1.0)
        inc("dve", v.engine_nop())
    VE(V)

    stW = ExitStack()
    wbuf = xq
    wqb = Ub[:, 4096:6144]
    pW = psum("pW", [128, 512], F32, stW)

    def _wslice(wi, ch, dst):
        a = w1t if wi == 0 else w2t
        d = w1i_d if wi == 0 else w2i_d
        src = a if not dst else d
        if wi == 0:
            rc, kc = ch // 4, ch % 4
            return src[rc * 128:(rc + 1) * 128, kc * 1024:(kc + 1) * 1024]
        return src[ch * 128:(ch + 1) * 128, :]

    for wi in range(2):
        base = wi * 32
        for ch in range(32):
            def S(sy, wi=wi, ch=ch, base=base):
                if base + ch < 2:
                    wt(sy, "evCs")
                    wt(sy, "sTo")
                if wi == 1 and ch < 2:
                    wt(sy, "qql", 31 + ch)
                if base + ch >= 2:
                    wt(sy, "qab", base + ch - 1)
                inc("lwq", sy.dma_start(wbuf[:, (ch % 2) * 1024:(ch % 2) * 1024 + 1024],
                                        _wslice(wi, ch, False)), 16)
            SY(S)

            if ch % 8 == 7:
                def S(s, wi=wi, ch=ch, base=base):
                    for c2 in range(ch - 7, ch + 1):
                        wt(s, "lwq", 16 * (64 * wi + c2 + 1))
                        inc("qab", s.activation(scr2[:, 0:1024],
                                                wbuf[:, (c2 % 2) * 1024:(c2 % 2) * 1024 + 1024],
                                                ACTF.Abs, accum_out=wcols[:, c2:c2 + 1]))
                SE(S)

        def V(v, base=base):
            wt(v, "qab", base + 32)
            b.bar(v, v.reduce_sum(wcols[:, 64:65], wcols[:, 0:32], AX.X))
            inc("dve", v.engine_nop())
        VE(V)

        def Tt(t):
            wt(t, "dve")
            inc("peW", t.matmul(pW[0:1, 0:1], wcols[:, 64:65], wcols[:, 66:67],
                                start=True, stop=True))
        TE(Tt)

        def V(v):
            wt(v, "peW")
            b.bar(v, v.tensor_scalar(wcols[0:1, 67:68], pW[0:1, 0:1],
                                     1.0 / (4096.0 * 1024.0), 1e-5, ALU.mult, ALU.max))
            b.bar(v, v.reciprocal(wcols[0:1, 68:69], wcols[0:1, 67:68]))
            b.bar(v, v.memset(wbuf[0:1, 0:128], 1.0))
            inc("dve", v.engine_nop())
        VE(V)

        def Tt(t):
            wt(t, "dve")
            inc("peW", t.matmul(pW[:, 4:5], wbuf[0:1, 0:128], wcols[0:1, 68:69],
                                start=True, stop=True))
        TE(Tt)

        def V(v, wi=wi):
            wt(v, "peW")
            b.bar(v, v.tensor_copy(wcols[:, 78 + wi:79 + wi], pW[:, 4:5]))
            inc("qsc", v.engine_nop())
        VE(V)

        for ch in range(32):
            def S(sy, wi=wi, ch=ch, base=base):
                wt(sy, "qsc", wi + 1)
                if base + ch >= 2:
                    wt(sy, "qql", base + ch - 1)
                inc("lwq", sy.dma_start(wbuf[:, (ch % 2) * 1024:(ch % 2) * 1024 + 1024],
                                        _wslice(wi, ch, False)), 16)
            SY(S)

            def V(v, wi=wi, ch=ch, base=base):
                wt(v, "lwq", 16 * (64 * wi + 32 + ch + 1))
                if base + ch >= 2:
                    wt(v, "qout", 16 * (base + ch - 1))
                hb = (ch % 2) * 1024
                v.tensor_scalar(scr[:, 0:1024], wbuf[:, hb:hb + 1024],
                                wcols[:, 78 + wi:79 + wi], MAGIC, ALU.mult, ALU.add)
                v.tensor_scalar(ki32[:], scr[:, 0:1024], MAGIC, None, ALU.subtract)
                v.tensor_scalar(wqb[:, hb:hb + 1024], ki32[:], -1.0, 1.0, ALU.max, ALU.min)
                inc("qql", v.engine_nop())
            VE(V)

            def G(g, wi=wi, ch=ch, base=base):
                wt(g, "qql", base + ch + 1)
                inc("qout", g.dma_start(_wslice(wi, ch, True),
                                        wqb[:, (ch % 2) * 1024:(ch % 2) * 1024 + 1024]), 16)
            GE(G)

    stW.close()

    b.dbg_dma("wq1i", w1i_d, waits=("qout",))
    b.dbg_dma("wq2i", w2i_d, waits=("qout",))

    stFp = ExitStack()
    pFF = [psum("pFF%d" % i, [128, 512], F32, stFp) for i in range(2)]
    pF2 = [psum("pF2%d" % i, [128, 512], F32, stFp) for i in range(2)]
    pK = psum("pK", [128, 1024], BF16, stFp)
    pCR = psum("pCR", [128, 128], F32, stFp)

    def V(v):
        wt(v, "qsc", 2)
        b.bar(v, v.tensor_scalar(ftmp[:], qssqcol[:], 1.0 / C, 1e-6, ALU.mult, ALU.add))
        inc("dve", v.engine_nop())
    VE(V)

    def S(s):
        wt(s, "dve")
        inc("act", s.activation(wcols[:, 0:16], ftmp[:], ACTF.Sqrt))
    SE(S)

    def V(v):
        wt(v, "act")
        b.bar(v, v.reciprocal(ftmp[:], wcols[:, 0:16]))
        b.bar(v, v.tensor_mul(wcols[:, 16:32], ftmp[:], qmaxcol[:]))
        b.bar(v, v.tensor_scalar(wcols[:, 16:32], wcols[:, 16:32], 1e-5, None, ALU.max))
        b.bar(v, v.reciprocal(wcols[:, 32:48], wcols[:, 16:32]))
        b.bar(v, v.tensor_scalar(wcols[:, 32:48], wcols[:, 32:48], 127.0, None, ALU.mult))
        b.bar(v, v.tensor_mul(cm1c[:], ftmp[:], wcols[:, 32:48]))
        b.bar(v, v.tensor_mul(cm1c[:], cm1c[:], rcol[:]))
        b.bar(v, v.tensor_scalar(a1c[:], wcols[:, 32:48], wcols[:, 78:79], None, ALU.mult))
        b.bar(v, v.reciprocal(al1c[:], a1c[:]))
        inc("q1c", v.engine_nop())
    VE(V)

    def Tt(t):
        wt(t, "q1c", 1)
        inc("peW", t.transpose(pCR[0:16, 0:128], a1c[:], identf[:]))
    TE(Tt)

    def V(v):
        wt(v, "peW")
        b.bar(v, v.tensor_copy(rstg[0:16, 0:128], pCR[0:16, 0:128]))
        inc("evCR", v.engine_nop())
    VE(V)

    def S(sy):
        wt(sy, "evCR", 1)
        for j in range(16):
            inc("arow", sy.dma_start(A1R[0:1, j * 128:(j + 1) * 128], rstg[j:j + 1, 0:128]), 16)
    SY(S)

    b.dbg_dma("a1r", rowA[0:1, 0:2048], waits=("arow",))
    b.dbg_dma("q1cols", fcols[:], waits=("q1c",))

    for ck in range(NTCH):
        for ti in range(4):
            tg = ck * 4 + ti

            def S(sy, tg=tg):
                if tg < 2:
                    wt(sy, "evCs")
                    wt(sy, "sTo")
                    wt(sy, "qout")
                if tg >= 2:
                    wt(sy, "k1v", tg - 1)
                inc("xq2", sy.dma_start(xq[:, (tg % 2) * C:(tg % 2) * C + C],
                                        x_in[tg * 128:(tg + 1) * 128, :]), 16)
            SY(S)

            def V(v, tg=tg):
                wt(v, "xq2", 16 * (tg + 1))
                hb = (tg % 2) * C
                v.tensor_scalar(scr[:, 0:C], xq[:, hb:hb + C],
                                mcol[:, tg:tg + 1], cm1c[:, tg:tg + 1],
                                ALU.subtract, ALU.mult)
                v.tensor_scalar(scr[:, C:2 * C], scr[:, 0:C], MAGIC, None, ALU.add)
                v.tensor_scalar(ki32[:], scr[:, C:2 * C], MAGIC, -128.0,
                                ALU.subtract, ALU.max)
                v.tensor_copy(scr2.bitcast(BF16)[:, 0:1024], ki32[:])
                inc("k1v", v.engine_nop())
            VE(V)

            kb = ck * 20 + ti

            def Tt(t, tg=tg, kb=kb):
                wt(t, "k1v", tg + 1)
                if kb >= 1:
                    wt(t, "evK", kb)
                for j in range(NCC):
                    last = t.transpose(pK[:, j * 128:(j + 1) * 128],
                                       scr2.bitcast(BF16)[:, j * 128:(j + 1) * 128], identb)
                inc("peK", last)
            TE(Tt)

            def V(v, ti=ti, kb=kb, ck=ck):
                wt(v, "peK", kb + 1)
                if ck >= 1 and ti == 0:
                    wt(v, "peF2", 64 * ck)
                for j in range(NCC):
                    v.tensor_copy(Ub[:, j * 512 + ti * 128:j * 512 + (ti + 1) * 128],
                                  pK[:, j * 128:(j + 1) * 128])
                inc("evK", v.engine_nop())
            VE(V)

        if ck == 0:
            b.dbg_dma("k1c0", Ub[:, 0:4096], waits=("evK",))

        for hb in range(8):
            idx = ck * 8 + hb

            def S(sy, ck=ck, hb=hb, idx=idx):
                if idx == 0:
                    wt(sy, "evCs")
                    wt(sy, "sTo")
                    wt(sy, "qout")
                if idx >= 2:
                    wt(sy, "peF", 4 * (idx - 1))
                wt(sy, "peF2", 64 * ck)
                buf = idx % 2
                inc("lw1b", sy.dma_start(
                    Ub[:, 4096 + buf * 4096:8192 + buf * 4096].rearrange("p (cc h) -> p cc h", cc=8),
                    w1i_d[:, hb * 512:(hb + 1) * 512].rearrange("(cc p) h -> p cc h", p=128)), 16)
            SY(S)

            for ti in range(4):
                tg = ck * 4 + ti
                fidx = ck * 32 + hb * 4 + ti
                buf = idx % 2

                def Tt(t, ck=ck, hb=hb, ti=ti, tg=tg, fidx=fidx, idx=idx, buf=buf):
                    wt(t, "lw1b", 16 * (idx + 1))
                    if hb == 0 and ti == 0:
                        wt(t, "evK", ck * 20 + 4)
                        wt(t, "arow", 16 * 16)
                    if fidx >= 2:
                        wt(t, "evF", fidx - 1)
                    t.matmul(pFF[fidx % 2][:], rowA[0:1, tg * 128:(tg + 1) * 128],
                             b1b[0:1, hb * 512:(hb + 1) * 512], start=True, stop=False)
                    for cc in range(8):
                        last = t.matmul(pFF[fidx % 2][:],
                                        Ub[:, cc * 512 + ti * 128:cc * 512 + (ti + 1) * 128],
                                        Ub[:, 4096 + buf * 4096 + cc * 512:4096 + buf * 4096 + (cc + 1) * 512],
                                        start=False, stop=(cc == 7))
                    inc("peF", last)
                TE(Tt)

                def V(v, ck=ck, tg=tg, fidx=fidx):
                    wt(v, "peF", fidx + 1)
                    wt(v, "acS", 8 * ck)
                    if fidx >= 2:
                        wt(v, "acG", fidx - 1)
                    v.tensor_scalar(scr[:, (fidx % 2) * 512:(fidx % 2) * 512 + 512],
                                    pFF[fidx % 2][:], al1c[:, tg:tg + 1], None, ALU.mult)
                    inc("evF", v.engine_nop())
                VE(V)

                def V(v, ti=ti, hb=hb, fidx=fidx):
                    wt(v, "acG", fidx + 1)
                    v.tensor_reduce(wcols[:, ti * 8 + hb:ti * 8 + hb + 1],
                                    h1B[:, ti * 4096 + hb * 512:ti * 4096 + (hb + 1) * 512],
                                    AX.X, ALU.max, apply_absolute_value=True)
                    inc("hmx", v.engine_nop())
                VE(V)

            def S(s, ck=ck, hb=hb):
                for ti in range(4):
                    fidx = ck * 32 + hb * 4 + ti
                    sl = h1B[:, ti * 4096 + hb * 512:ti * 4096 + (hb + 1) * 512]
                    wt(s, "evF", fidx + 1)
                    inc("acG", s.activation(sl, scr[:, (fidx % 2) * 512:(fidx % 2) * 512 + 512],
                                            ACTF.Gelu))
                    inc("hsq", s.activation(scr2[:, 1024:1536], sl, ACTF.Square,
                                            accum_out=wcols[:, 32 + ti * 8 + hb:32 + ti * 8 + hb + 1]))
            SE(S)

        if ck == 0:
            b.dbg_dma("h1c0", h1B[:], waits=("hsq", "hmx"))

        def V(v, ck=ck):
            wt(v, "hsq", 32 * (ck + 1))
            b.bar(v, v.engine_nop())
            for ti in range(4):
                b.bar(v, v.tensor_reduce(wcols[:, 64 + ti:65 + ti],
                                         wcols[:, ti * 8:ti * 8 + 8], AX.X, ALU.max))
                b.bar(v, v.reduce_sum(wcols[:, 68 + ti:69 + ti],
                                      wcols[:, 32 + ti * 8:32 + ti * 8 + 8], AX.X))
            b.bar(v, v.tensor_scalar(wcols[:, 72:76], wcols[:, 68:72], 1.0 / HF, 1e-6,
                                     ALU.mult, ALU.add))
            inc("dve", v.engine_nop())
        VE(V)

        def S(s):
            wt(s, "dve")
            inc("act", s.activation(wcols[:, 68:72], wcols[:, 72:76], ACTF.Sqrt))
        SE(S)

        def V(v, ck=ck):
            wt(v, "act")
            b.bar(v, v.reciprocal(wcols[:, 72:76], wcols[:, 68:72]))
            b.bar(v, v.tensor_mul(wcols[:, 68:72], wcols[:, 72:76], wcols[:, 64:68]))
            b.bar(v, v.tensor_scalar(wcols[:, 68:72], wcols[:, 68:72], 1e-5, None, ALU.max))
            b.bar(v, v.reciprocal(wcols[:, 64:68], wcols[:, 68:72]))
            b.bar(v, v.tensor_scalar(wcols[:, 64:68], wcols[:, 64:68], 127.0, None, ALU.mult))
            b.bar(v, v.tensor_mul(wcols[:, 68:72], wcols[:, 72:76], wcols[:, 64:68]))
            b.bar(v, v.tensor_scalar(wcols[:, 72:76], wcols[:, 64:68], wcols[:, 79:80], None, ALU.mult))
            b.bar(v, v.reciprocal(wcols[:, 64:68], wcols[:, 72:76]))
            inc("q2c", v.engine_nop())
        VE(V)

        def Tt(t, ck=ck):
            wt(t, "q2c", ck + 1)
            inc("peW", t.transpose(pCR[0:4, 0:128], wcols[:, 72:76], identf[:]))
        TE(Tt)

        def V(v, ck=ck):
            wt(v, "peW")
            wt(v, "arow", 16 * (16 + 4 * ck))
            b.bar(v, v.tensor_copy(rstg[0:4, 0:128], pCR[0:4, 0:128]))
            inc("evCR", v.engine_nop())
        VE(V)

        def S(sy, ck=ck):
            wt(sy, "evCR", 2 + ck)
            if ck >= 1:
                wt(sy, "peF2", 64 * (ck - 1) + 8)
            for j in range(4):
                inc("arow", sy.dma_start(A2R[0:1, j * 128:(j + 1) * 128], rstg[j:j + 1, 0:128]), 16)
        SY(S)

        for ti in range(4):
            for hb in range(8):
                def V(v, ck=ck, ti=ti, hb=hb):
                    if ti == 0 and hb == 0:
                        wt(v, "q2c", ck + 1)
                        wt(v, "stq")
                    sl = h1B[:, ti * 4096 + hb * 512:ti * 4096 + (hb + 1) * 512]
                    v.tensor_scalar(scr[:, 0:512], sl, wcols[:, 68 + ti:69 + ti], MAGIC,
                                    ALU.mult, ALU.add)
                    v.tensor_scalar(ki32[:, 0:512], scr[:, 0:512], MAGIC, -128.0,
                                    ALU.subtract, ALU.max)
                    v.tensor_copy(sl, ki32[:, 0:512])
                    inc("k2v", v.engine_nop())
                VE(V)

        for ti in range(4):
            for jb in range(4):
                kb = ck * 20 + 4 + ti * 4 + jb

                def Tt(t, ck=ck, ti=ti, jb=jb, kb=kb):
                    wt(t, "k2v", ck * 32 + ti * 8 + 8)
                    wt(t, "evK", kb)
                    for j in range(8):
                        hc = jb * 8 + j
                        last = t.transpose(pK[:, j * 128:(j + 1) * 128],
                                           h1B[:, ti * 4096 + hc * 128:ti * 4096 + (hc + 1) * 128],
                                           identb)
                    inc("peK", last)
                TE(Tt)

                def V(v, ti=ti, jb=jb, kb=kb):
                    wt(v, "peK", kb + 1)
                    for j in range(8):
                        hc = jb * 8 + j
                        v.tensor_copy(Ub[:, hc * 512 + ti * 128:hc * 512 + (ti + 1) * 128],
                                      pK[:, j * 128:(j + 1) * 128])
                    inc("evK", v.engine_nop())
                VE(V)

        for hcg in range(8):
            gidx = ck * 8 + hcg

            def G(g, ck=ck, hcg=hcg, gidx=gidx):
                if gidx == 0:
                    wt(g, "evCs")
                    wt(g, "sTo")
                    wt(g, "qout")
                if gidx >= 2:
                    wt(g, "peF2", 8 * (gidx - 1))
                buf = gidx % 2
                inc("lw2b", g.dma_start(
                    Ub[:, 16384 + buf * 4096:16384 + (buf + 1) * 4096].rearrange("p (hc c) -> p hc c", hc=4),
                    w2i_d[hcg * 512:(hcg + 1) * 512, :].rearrange("(hc p) c -> p hc c", p=128)), 16)
            GE(G)

            for ti in range(4):
                for cb in range(2):
                    fidx2 = ck * 64 + hcg * 8 + ti * 2 + cb
                    buf = gidx % 2

                    def Tt(t, ck=ck, hcg=hcg, ti=ti, cb=cb, fidx2=fidx2, gidx=gidx, buf=buf):
                        wt(t, "lw2b", 16 * (gidx + 1))
                        if hcg == 0 and ti == 0 and cb == 0:
                            wt(t, "evK", 20 * (ck + 1))
                        if hcg == 0:
                            wt(t, "arow", 16 * (16 + 4 * (ck + 1)))
                        if fidx2 >= 2:
                            wt(t, "evF2", fidx2 - 1)
                        if hcg == 0:
                            t.matmul(pF2[fidx2 % 2][:], rowA[0:1, 2048 + ti * 128:2048 + (ti + 1) * 128],
                                     b2b[0:1, cb * 512:(cb + 1) * 512], start=True, stop=False)
                        for q in range(4):
                            hc = hcg * 4 + q
                            last = t.matmul(pF2[fidx2 % 2][:],
                                            Ub[:, hc * 512 + ti * 128:hc * 512 + (ti + 1) * 128],
                                            Ub[:, 16384 + buf * 4096 + q * 1024 + cb * 512:16384 + buf * 4096 + q * 1024 + (cb + 1) * 512],
                                            start=(q == 0 and hcg != 0), stop=(q == 3))
                        inc("peF2", last)
                    TE(Tt)

                    def V(v, hcg=hcg, ti=ti, cb=cb, fidx2=fidx2):
                        wt(v, "peF2", fidx2 + 1)
                        dst = yac[:, ti * 1024 + cb * 512:ti * 1024 + (cb + 1) * 512]
                        if hcg == 0:
                            v.tensor_copy(dst, pF2[fidx2 % 2][:])
                        else:
                            v.tensor_add(dst, dst, pF2[fidx2 % 2][:])
                        inc("evF2", v.engine_nop())
                    VE(V)

        if ck == 0:
            b.dbg_dma("yac0", yac[:], waits=("evF2",))
            b.dbg_dma("q2w", wcols[:, 64:76], waits=("q2c",))
            b.dbg_dma("wsc", wcols[:, 78:80], waits=("q2c",))
            b.dbg_dma("k2c0", Ub[:, 0:16384], waits=("evK",))

        for ti in range(4):
            for cb in range(2):
                tg = ck * 4 + ti
                eidx = ck * 8 + ti * 2 + cb

                def G(g, tg=tg, cb=cb, eidx=eidx):
                    if eidx == 0:
                        wt(g, "sTo")
                    if eidx >= 2:
                        wt(g, "u0d", eidx - 1)
                    inc("lsT", g.dma_start(sTb[:, (eidx % 2) * 512:(eidx % 2) * 512 + 512],
                                           sT_d[tg * 128:(tg + 1) * 128, cb * 512:(cb + 1) * 512]), 16)
                GE(G)

                def V(v, ck=ck, ti=ti, cb=cb, eidx=eidx):
                    wt(v, "evF2", ck * 64 + 56 + ti * 2 + cb + 1)
                    if eidx >= 2:
                        wt(v, "acS", eidx - 1)
                    v.tensor_scalar(scr[:, (eidx % 2) * 512:(eidx % 2) * 512 + 512],
                                    yac[:, ti * 1024 + cb * 512:ti * 1024 + (cb + 1) * 512],
                                    wcols[:, 64 + ti:65 + ti], None, ALU.mult)
                    inc("evGs", v.engine_nop())
                VE(V)

                if ti == 3 and cb == 1:
                    def S(s, ck=ck):
                        for e2 in range(ck * 8, ck * 8 + 8):
                            wt(s, "evGs", e2 + 1)
                            inc("acS", s.activation(
                                scr2.bitcast(BF16)[:, 3072 + (e2 % 2) * 512:3072 + (e2 % 2) * 512 + 512],
                                scr[:, (e2 % 2) * 512:(e2 % 2) * 512 + 512], ACTF.Silu))
                    SE(S)

                def V(v, tg=tg, cb=cb, eidx=eidx):
                    wt(v, "acS", eidx + 1)
                    wt(v, "lsT", 16 * (eidx + 1))
                    if eidx == 0:
                        wt(v, "rstI")
                    v.tensor_mul(scr[:, 1024:1536],
                                 scr2.bitcast(BF16)[:, 3072 + (eidx % 2) * 512:3072 + (eidx % 2) * 512 + 512],
                                 sTb[:, (eidx % 2) * 512:(eidx % 2) * 512 + 512])
                    rs = resT[:, tg * 1024 + cb * 512:tg * 1024 + (cb + 1) * 512]
                    v.tensor_add(rs, rs, scr[:, 1024:1536])
                    inc("u0d", v.engine_nop())
                VE(V)

    b.dbg_dma("x1", resT[:], waits=("u0d",))

    s1_d = nc.dram_tensor("s1d", [T, C], BF16, kind="Internal").ap()
    weF = we.rearrange("E r c -> (E r) c")

    stD.close()

    _dnew = ["evA2", "evGR", "evHG", "evLG", "evP", "evS2", "evT", "exq",
             "gbb", "gex", "gmx", "gpr", "ln1q", "ln1r", "ln1s", "lnx", "mx1",
             "peA", "peGB", "peGT", "peHG", "peLG", "peM", "peP", "pSL",
             "peT1", "psc", "rz1", "u1d", "u1s",
             "gr1", "s1o", "ldg", "ldwe", "ls1"]
    _dead = ["scores", "lnstats", "peT", "evA", "peS", "evB", "peB", "evB2",
             "peB2", "peG", "evG", "peN", "evN", "peQ", "evQ", "peVT", "evVT",
             "peO", "evO", "dveM", "expS", "acG", "evGs", "peSel", "evSel",
             "xdone", "peS2", "qab", "k1v", "k2v",
             "xload", "xq2", "ldcb", "lsT", "qout"]
    assert len(_dnew) == len(set(_dnew)) and len(_dead) == len(set(_dead))
    _SM = dict(zip(_dnew, _dead))
    _BASE = {m: val(m) for m in _SM.values()}
    _inc0, _wt0 = inc, wt

    def inc(name, inst, n=1):
        return _inc0(_SM.get(name, name), inst, n)

    def wt(eng, name, v=None):
        m = _SM.get(name, name)
        if v is None:
            _wt0(eng, m)
        else:
            _wt0(eng, m, _BASE.get(m, 0) + v)

    stE = ExitStack()
    xn1F = sbuf("xn1F", [128, NCC * T], BF16, stE)
    Grow = sbuf("Grow", [20, T], BF16, stE)
    bebuf = sbuf("bebuf", [20, C], BF16, stE)

    for tl in range(NTC):
        def V(v, tl=tl):
            if tl == 0:
                wt(v, "peF2", 64 * NTCH)
            v.reduce_sum(mcol[:, tl:tl + 1], resT[:, tl * C:(tl + 1) * C], AX.X)
            inc("ln1s", v.engine_nop())
        VE(V)

        if tl == NTC - 1:
            def S(s):
                wt(s, "u0d", 8 * NTCH)
                for t2 in range(NTC):
                    wt(s, "ln1s", t2 + 1)
                    inc("ln1q", s.activation(scr2[:, 0:C], resT[:, t2 * C:(t2 + 1) * C],
                                             ACTF.Square, accum_out=vcol[:, t2:t2 + 1]))
            SE(S)

    def V(v):
        wt(v, "ln1q", NTC)
        b.bar(v, v.engine_nop())
        b.bar(v, v.tensor_scalar(mcol[:], mcol[:], 1.0 / C, None, ALU.mult))
        b.bar(v, v.tensor_mul(scr2[:, 0:NTC], mcol[:], mcol[:]))
        b.bar(v, v.scalar_tensor_tensor(vcol[:], vcol[:], 1.0 / C, scr2[:, 0:NTC],
                                        ALU.mult, ALU.subtract))
        v.tensor_scalar(scr2[:, 0:NTC], vcol[:], 1e-5, None, ALU.add)
        inc("dve", v.engine_nop())
    VE(V)

    def S(s):
        wt(s, "dve")
        inc("act", s.activation(scr2[:, NTC:2 * NTC], scr2[:, 0:NTC], ACTF.Sqrt))
    SE(S)

    def V(v):
        wt(v, "act")
        b.bar(v, v.reciprocal(rcol[:], scr2[:, NTC:2 * NTC]))
        inc("ln1r", v.engine_nop())
    VE(V)

    stEa = ExitStack()
    xn1T = sbuf("xn1T", [128, NTC * C], BF16, stEa)
    PT = sbuf("PT", [128, T], BF16, stEa)
    PTkF = sbuf("PTkF", [128, T], BF16, stEa)
    s1st = sbuf("s1st", [128, 2 * 512], BF16, stEa)
    scb = sbuf("scb", [128, 2 * T], F32, stEa)

    for tl in range(NTC):
        def V(v, tl=tl):
            if tl == 0:
                wt(v, "ln1r", 1)
            v.tensor_scalar(scr[:, 0:C], resT[:, tl * C:(tl + 1) * C],
                            mcol[:, tl:tl + 1], rcol[:, tl:tl + 1],
                            ALU.subtract, ALU.mult)
            v.tensor_copy(xn1T[:, tl * C:(tl + 1) * C], scr[:, 0:C])
            inc("lnx", v.engine_nop())
        VE(V)

        def Tt(t, tl=tl):
            wt(t, "lnx", tl + 1)
            if tl >= 1:
                wt(t, "evT", tl)
            for j in range(NCC):
                last = t.transpose(pK[:, j * 128:(j + 1) * 128],
                                   xn1T[:, tl * C + j * 128:tl * C + (j + 1) * 128], identb)
            inc("peT1", last)
        TE(Tt)

        def V(v, tl=tl):
            wt(v, "peT1", tl + 1)
            for j in range(NCC):
                v.tensor_copy(xn1F[:, j * T + tl * 128:j * T + (tl + 1) * 128],
                              pK[:, j * 128:(j + 1) * 128])
            inc("evT", v.engine_nop())
        VE(V)

    for qt in range(NTC):
        for kb in range(4):
            sidx = qt * 4 + kb

            def Tt(t, qt=qt, kb=kb, sidx=sidx):
                if sidx == 0:
                    wt(t, "evT", NTC)
                if sidx >= 2:
                    wt(t, "evS2", sidx - 1)
                for cc in range(NCC):
                    last = t.matmul(pFF[sidx % 2][:],
                                    xn1F[:, cc * T + qt * 128:cc * T + (qt + 1) * 128],
                                    xn1F[:, cc * T + kb * 512:cc * T + (kb + 1) * 512],
                                    start=(cc == 0), stop=(cc == 7))
                inc("pSL", last)
            TE(Tt)

            def V(v, qt=qt, kb=kb, sidx=sidx):
                wt(v, "pSL", sidx + 1)
                if sidx % 4 == 0 and sidx >= 8:
                    wt(v, "exq", sidx // 4 - 1)
                v.tensor_copy(scb[:, (qt % 2) * T + kb * 512:(qt % 2) * T + (kb + 1) * 512],
                              pFF[sidx % 2][:])
                inc("evS2", v.engine_nop())
            VE(V)

        def V(v, qt=qt):
            wt(v, "evS2", qt * 4 + 4)
            pq = qt % 2
            b.bar(v, v.tensor_reduce(scr2[:, 40 + pq:41 + pq], scb[:, pq * T:pq * T + T], AX.X, ALU.max))
            b.bar(v, v.tensor_scalar(scr2[:, 44 + pq:45 + pq], scr2[:, 40 + pq:41 + pq], -1.0, None, ALU.mult))
            inc("mx1", v.engine_nop())
        VE(V)

        if qt % 4 == 3:
            def S(s, qt=qt):
                for q2 in range(qt - 3, qt + 1):
                    p2 = q2 % 2
                    wt(s, "mx1", q2 + 1)
                    if q2 >= 1:
                        wt(s, "evP", 2 * q2)
                    inc("exq", s.activation(PT[:], scb[:, p2 * T:p2 * T + T], ACTF.Exp,
                                            bias=scr2[:, 44 + p2:45 + p2],
                                            accum_out=scr2[:, 46 + p2:47 + p2]))
            SE(S)

        def V(v, qt=qt):
            wt(v, "exq", qt + 1)
            pq = qt % 2
            b.bar(v, v.reciprocal(scr2[:, 48 + pq:49 + pq], scr2[:, 46 + pq:47 + pq]))
            inc("rz1", v.engine_nop())
        VE(V)

        for half in range(2):
            def Tt(t, qt=qt, half=half):
                wt(t, "exq", qt + 1)
                wt(t, "evP", 2 * qt + half)
                for j in range(8):
                    last = t.transpose(pK[:, j * 128:(j + 1) * 128],
                                       PT[:, half * 1024 + j * 128:half * 1024 + (j + 1) * 128],
                                       identb)
                inc("peP", last)
            TE(Tt)

            def V(v, qt=qt, half=half):
                wt(v, "peP", 2 * qt + half + 1)
                for j in range(8):
                    v.tensor_copy(PTkF[:, half * 1024 + j * 128:half * 1024 + (j + 1) * 128],
                                  pK[:, j * 128:(j + 1) * 128])
                inc("evP", v.engine_nop())
            VE(V)

        for cb in range(2):
            aidx = qt * 2 + cb

            def Tt(t, qt=qt, cb=cb, aidx=aidx):
                wt(t, "evP", 2 * qt + 2)
                if aidx >= 2:
                    wt(t, "evA2", aidx - 1)
                for kt in range(NTC):
                    last = t.matmul(pF2[aidx % 2][:],
                                    PTkF[:, kt * 128:(kt + 1) * 128],
                                    xn1T[:, kt * C + cb * 512:kt * C + (cb + 1) * 512],
                                    start=(kt == 0), stop=(kt == NTC - 1))
                inc("peA", last)
            TE(Tt)

            def V(v, cb=cb, qt=qt, aidx=aidx):
                wt(v, "peA", aidx + 1)
                wt(v, "rz1", qt + 1)
                if aidx >= 2:
                    wt(v, "s1o", 16 * (aidx - 1))
                v.tensor_scalar(s1st[:, (aidx % 2) * 512:(aidx % 2) * 512 + 512],
                                pF2[aidx % 2][:], scr2[:, 48 + (qt % 2):49 + (qt % 2)], None, ALU.mult)
                inc("evA2", v.engine_nop())
            VE(V)

            def S(sy, qt=qt, cb=cb, aidx=aidx):
                wt(sy, "evA2", aidx + 1)
                inc("s1o", sy.dma_start(
                    s1_d[qt * 128:(qt + 1) * 128, cb * 512:(cb + 1) * 512],
                    s1st[:, (aidx % 2) * 512:(aidx % 2) * 512 + 512]), 16)
            SY(S)

    stEa.close()

    stEg = ExitStack()
    wg1b = sbuf("wg1b", [128, NCC * 256], BF16, stEg)
    wg2b = sbuf("wg2b", [128, 2 * 40], BF16, stEg)
    hgB = sbuf("hgB", [128, 2 * T], BF16, stEg)
    lgF = sbuf("lgF", [40, T], F32, stEg)
    bgc = sbuf("bgc", [128, 3], F32, stEg)

    def G(g):
        wt(g, "peF2", 64 * NTCH)
        wt(g, "acS", 8 * NTCH)
        wt(g, "s1o")
        inc("ldg", g.dma_start(wg1b[:].rearrange("p (cc h) -> p cc h", cc=NCC),
                               wg1.rearrange("(cc p) h -> p cc h", p=128)), 16)
        inc("ldg", g.dma_start(wg2b[:].rearrange("p (hh j) -> p hh j", hh=2),
                               wg2.rearrange("(hh p) j -> p hh j", p=128)), 16)
        inc("ldg", g.dma_start(bgc[:, 0:2], bg1.rearrange("(hh p) o -> p (hh o)", p=128),
                               allow_slow_non_contiguous=True), 16)
        inc("ldg", g.dma_start(bgc[0:40, 2:3], bg2), 16)
        inc("ldg", g.dma_start(bebuf[:], be), 16)
    GE(G)

    for hh in range(2):
        for qc in range(4):
            gidx = hh * 4 + qc

            def Tt(t, hh=hh, qc=qc, gidx=gidx):
                if gidx == 0:
                    wt(t, "ldg")
                    wt(t, "evS2", 64)
                if gidx >= 2:
                    wt(t, "evHG", gidx - 1)
                for cc in range(NCC):
                    last = t.matmul(pFF[gidx % 2][:],
                                    wg1b[:, cc * 256 + hh * 128:cc * 256 + (hh + 1) * 128],
                                    xn1F[:, cc * T + qc * 512:cc * T + (qc + 1) * 512],
                                    start=(cc == 0), stop=(cc == 7))
                inc("peHG", last)
            TE(Tt)

            if qc == 3:
                def S(s, hh=hh):
                    for q2 in range(4):
                        g2 = hh * 4 + q2
                        wt(s, "peHG", g2 + 1)
                        inc("evHG", s.activation(hgB[:, hh * T + q2 * 512:hh * T + (q2 + 1) * 512],
                                                 pFF[g2 % 2][:], ACTF.Gelu,
                                                 bias=bgc[:, hh:hh + 1]))
                SE(S)

    for qc in range(4):
        def Tt(t, qc=qc):
            wt(t, "evHG", 4 + qc + 1)
            if qc == 0:
                wt(t, "evA2", 32)
            if qc >= 2:
                wt(t, "evLG", qc - 1)
            for hh in range(2):
                last = t.matmul(pF2[qc % 2][0:40, :],
                                wg2b[:, hh * 40:(hh + 1) * 40],
                                hgB[:, hh * T + qc * 512:hh * T + (qc + 1) * 512],
                                start=(hh == 0), stop=(hh == 1))
            inc("peLG", last)
        TE(Tt)

        def V(v, qc=qc):
            wt(v, "peLG", qc + 1)
            v.tensor_scalar(lgF[:, qc * 512:(qc + 1) * 512], pF2[qc % 2][0:40, :],
                            bgc[0:40, 2:3], None, ALU.add)
            inc("evLG", v.engine_nop())
        VE(V)

    for tl in range(NTC):
        def Tt(t, tl=tl):
            wt(t, "evLG", 4)
            if tl >= 1:
                wt(t, "evGR", 2 * tl)
            inc("peGT", t.transpose(pCR[:, 0:40], lgF[:, tl * 128:(tl + 1) * 128],
                                    identf[0:40, 0:40]))
        TE(Tt)

        def V(v, tl=tl):
            wt(v, "peGT", 2 * tl + 1)
            b.bar(v, v.tensor_copy(scr[:, 0:20], pCR[:, 0:20]))
            b.bar(v, v.tensor_reduce(scr[:, 24:25], scr[:, 0:20], AX.X, ALU.max))
            b.bar(v, v.tensor_scalar(scr[:, 25:26], scr[:, 24:25], -1.0, None, ALU.mult))
            inc("gmx", v.engine_nop())
        VE(V)

        def S(s, tl=tl):
            wt(s, "gmx", tl + 1)
            inc("gex", s.activation(scr[:, 64:84], scr[:, 0:20], ACTF.Exp,
                                    bias=scr[:, 25:26], accum_out=scr[:, 26:27]))
        SE(S)

        def V(v, tl=tl):
            wt(v, "gex", tl + 1)
            b.bar(v, v.reciprocal(scr[:, 27:28], scr[:, 26:27]))
            b.bar(v, v.tensor_scalar(scr[:, 64:84], scr[:, 64:84], scr[:, 27:28],
                                     None, ALU.mult))
            b.bar(v, v.tensor_reduce(scr[:, 24:25], scr[:, 64:84], AX.X, ALU.max))
            b.bar(v, v.tensor_scalar(scr[:, 40:60], scr[:, 64:84], scr[:, 24:25],
                                     None, ALU.is_equal))
            b.bar(v, v.scalar_tensor_tensor(scr[:, 40:60], scr[:, 40:60], -2e30,
                                            scr[:, 64:84], ALU.mult, ALU.add))
            b.bar(v, v.tensor_reduce(scr[:, 33:34], scr[:, 40:60], AX.X, ALU.max))
            b.bar(v, v.tensor_scalar(scr[:, 40:60], scr[:, 64:84], scr[:, 33:34],
                                     None, ALU.is_ge))
            b.bar(v, v.tensor_mul(scr[:, 64:84], scr[:, 64:84], scr[:, 40:60]))
            inc("gpr", v.engine_nop())
        VE(V)

        def Tt(t, tl=tl):
            wt(t, "gpr", tl + 1)
            inc("peGT", t.transpose(pCR[0:20, 0:128], scr[:, 64:84], identf[:]))
        TE(Tt)

        def V(v, tl=tl):
            wt(v, "peGT", 2 * tl + 2)
            b.bar(v, v.tensor_copy(Grow[0:20, tl * 128:(tl + 1) * 128], pCR[0:20, 0:128]))
            inc("evGR", v.engine_nop(), 2)
        VE(V)

    b.dbg_dma("s1t", s1_d, waits=(_SM["s1o"],))
    b.dbg_dma("grw", Grow[:], waits=(_SM["evGR"],))
    b.dbg_dma("lgt", lgF[:], waits=(_SM["evLG"],))

    stEg.close()
    stFp.close()

    stEm = ExitStack()
    gbcAll = sbuf("gbcAll", [128, (NE - 3) * 1024], BF16, stEm)

    def gsl(e):
        if e < 3:
            return scr2.bitcast(BF16)[:, e * 1024:(e + 1) * 1024]
        return gbcAll[:, (e - 3) * 1024:(e - 2) * 1024]
    xn1G = sbuf("xn1G", [128, 2 * NCC * 1024], BF16, stEm)
    webuf = sbuf("webuf", [128, 2 * NCC * 512], BF16, stEm)
    grow1 = sbuf("grow1", [1, 1024], BF16, stEm)
    s1b = sbuf("s1b", [128, 2 * 512], BF16, stEm)
    stMp = ExitStack()
    pM = [psum("pM%d" % i, [128, 512], F32, stMp) for i in range(8)]

    for grp in range(2):
        for e in range(NE):
            def S(sy, grp=grp, e=e):
                if grp == 0 and e == 0:
                    wt(sy, "evGR", 2 * NTC)
                if grp == 1 and e == 0:
                    wt(sy, "peM", 8 * 2 * NE)
                wt(sy, "gbb", 2 * (grp * NE + e))
                inc("gr1", sy.dma_start(grow1[:], Grow[e:e + 1, grp * 1024:(grp + 1) * 1024]), 16)
            SY(S)

            def Tt(t, grp=grp, e=e):
                wt(t, "gr1", 16 * (grp * NE + e + 1))
                wt(t, "gbb", 2 * (grp * NE + e))
                if grp == 1 and e == 0:
                    wt(t, "u1s", 16)
                for st in range(2):
                    inc("peGB", t.matmul(pM[st][:],
                                         onesrow[:],
                                         grow1[0:1, st * 512:(st + 1) * 512],
                                         start=True, stop=True))
            TE(Tt)

            def V(v, grp=grp, e=e):
                wt(v, "peGB", (grp * NE + e + 1) * 2)
                for st in range(2):
                    v.tensor_copy(gsl(e)[:, st * 512:(st + 1) * 512], pM[st][:])
                inc("gbb", v.engine_nop(), 2)
            VE(V)

        for cb in range(2):
            sw = grp * 2 + cb
            for e in range(NE):
                eidx = sw * NE + e

                def G(g, grp=grp, cb=cb, e=e, eidx=eidx):
                    if eidx < 2:
                        wt(g, "evGR", 2 * NTC)
                    if eidx >= 2:
                        wt(g, "peM", 8 * (eidx - 1))
                    buf = eidx % 2
                    inc("ldwe", g.dma_start(
                        webuf[:, buf * 4096:(buf + 1) * 4096].rearrange("p (cc c) -> p cc c", cc=NCC),
                        weF[e * C:(e + 1) * C, cb * 512:(cb + 1) * 512].rearrange("(cc p) c -> p cc c", p=128)), 16)
                GE(G)

                def V(v, grp=grp, e=e, eidx=eidx):
                    if eidx >= 2:
                        wt(v, "peM", 8 * (eidx - 1))
                    xb = (eidx % 2) * 8192
                    for cc in range(NCC):
                        v.tensor_mul(xn1G[:, xb + cc * 1024:xb + (cc + 1) * 1024],
                                     xn1F[:, cc * T + grp * 1024:cc * T + (grp + 1) * 1024],
                                     gsl(e))
                    inc("psc", v.engine_nop())
                VE(V)

                def Tt(t, grp=grp, cb=cb, e=e, eidx=eidx, sw=sw):
                    wt(t, "ldwe", 16 * (eidx + 1))
                    wt(t, "psc", eidx + 1)
                    if e == 0:
                        wt(t, "gbb", (grp + 1) * NE * 2)
                        wt(t, "u1s", 8 * sw)
                    buf = eidx % 2
                    for tl in range(8):
                        if e == 0:
                            t.matmul(pM[tl][:],
                                     Grow[0:20, grp * 1024 + tl * 128:grp * 1024 + (tl + 1) * 128],
                                     bebuf[0:20, cb * 512:(cb + 1) * 512],
                                     start=True, stop=False)
                        xb = (eidx % 2) * 8192
                        for cc in range(NCC):
                            last = t.matmul(pM[tl][:],
                                            xn1G[:, xb + cc * 1024 + tl * 128:xb + cc * 1024 + (tl + 1) * 128],
                                            webuf[:, buf * 4096 + cc * 512:buf * 4096 + (cc + 1) * 512],
                                            start=False, stop=(e == NE - 1 and cc == NCC - 1))
                    inc("peM", last, 8)
                TE(Tt)

            for tl in range(8):
                tg = grp * 8 + tl
                vidx = sw * 8 + tl

                def G(g, tg=tg, cb=cb, vidx=vidx):
                    if vidx < 2:
                        wt(g, "evGR", 2 * NTC)
                        wt(g, "s1o")
                    if vidx >= 2:
                        wt(g, "u1d", vidx - 1)
                    inc("ls1", g.dma_start(s1b[:, (vidx % 2) * 512:(vidx % 2) * 512 + 512],
                                           s1_d[tg * 128:(tg + 1) * 128, cb * 512:(cb + 1) * 512]), 16)
                GE(G)

                if tl == 7:
                    def S(s, sw=sw):
                        wt(s, "peM", 8 * (sw + 1) * NE)
                        for t2 in range(8):
                            v2 = sw * 8 + t2
                            if v2 >= 2:
                                wt(s, "u1d", v2 - 1)
                            inc("u1s", s.activation(
                                scr2.bitcast(BF16)[:, 3072 + (v2 % 2) * 512:3072 + (v2 % 2) * 512 + 512],
                                pM[t2][:], ACTF.Silu))
                    SE(S)

                def V(v, tg=tg, cb=cb, vidx=vidx):
                    wt(v, "u1s", vidx + 1)
                    wt(v, "ls1", 16 * (vidx + 1))
                    v.tensor_mul(scr[:, 1024:1536],
                                 scr2.bitcast(BF16)[:, 3072 + (vidx % 2) * 512:3072 + (vidx % 2) * 512 + 512],
                                 s1b[:, (vidx % 2) * 512:(vidx % 2) * 512 + 512])
                    rs = resT[:, tg * 1024 + cb * 512:tg * 1024 + (cb + 1) * 512]
                    v.tensor_add(rs, rs, scr[:, 1024:1536])
                    inc("u1d", v.engine_nop())
                VE(V)

    stMp.close()
    stEm.close()
    stE.close()

    def G(g):
        wt(g, "u1d")
        inc("outq", g.dma_start(out.rearrange("(tc p) c -> p tc c", p=128),
                                resT[:].rearrange("p (tc c) -> p tc c", tc=NTC)), 16)
        for nm in sorted(b.sems):
            wt(g, nm)
    GE(G)

    b.es.close()
    return nc, b.dram, b.DBG



def _prep(inputs):
    f = np.ascontiguousarray
    dqn = inputs["L0_attn_w_dq_nope"].reshape(96, NH, 32)
    dqr = inputs["L0_attn_w_dq_rope"].reshape(96, NH, 64)
    wink = inputs["L0_attn_w_wink"].reshape(C, NH, 96)
    selk = inputs["L0_attn_w_selk"].reshape(C, NH, 96)
    dkn = inputs["L0_attn_w_dk_nope"].reshape(32, NH, 32)
    dv = inputs["L0_attn_w_dv"].reshape(32, NH, 32)
    winv = inputs["L0_attn_w_winv"].reshape(C, NH, 32)
    selv = inputs["L0_attn_w_selv"].reshape(C, NH, 32)
    t = np.arange(T, dtype=np.float64)
    invf = 10000.0 ** (-np.arange(0, ROPE, 2, dtype=np.float64) / ROPE)
    ang = np.outer(invf, t)
    cosv, sinv = np.cos(ang), np.sin(ang)
    C4 = np.tile(cosv, (4, 1)); S4 = np.tile(sinv, (4, 1))
    c16, s16 = cosv / 16.0, sinv / 16.0
    KCS = np.concatenate([c16, s16, s16, c16], 0)
    KCS2 = np.concatenate([s16, c16, c16, s16], 0)
    ident = np.eye(128)
    Eseg4 = np.zeros((128, 4))
    for j in range(4):
        Eseg4[32 * j:32 * j + 32, j] = 1.0
    Eall64 = np.zeros((128, 4)); Eall64[0:64, :] = 1.0
    tri = (np.arange(128)[None, :] >= np.arange(128)[:, None]).astype(np.float64)
    constsf = np.concatenate([C4, S4, ident, Eseg4, Eall64, tri], axis=1).astype(np.float32)
    constsk = np.concatenate([KCS, KCS2], axis=1).astype(np.float32)
    return {
        "constsf": f(constsf),
        "constsk": f(constsk),
        "w_cq": f(inputs["L0_attn_w_cq"]),
        "g_qnorm": f(inputs["L0_attn_g_qnorm"].reshape(96, 1)),
        "wq_nope": f(dqn.reshape(96, NH * 32)),
        "wq_x1": f(dqr[:, :, 0:32].reshape(96, NH * 32)),
        "wq_x2": f(dqr[:, :, 32:64].reshape(96, NH * 32)),
        "w_ckv": f(inputs["L0_attn_w_ckv"]),
        "g_kvnorm": f(inputs["L0_attn_g_kvnorm"].reshape(32, 1)),
        "wk_nope": f(dkn.reshape(32, NH * 32)),
        "w_dv": f(dv.reshape(32, NH * 32)),
        "w_krope": f(inputs["L0_attn_w_krope"]),
        "w_small": f(np.concatenate([inputs["L0_attn_w_imp"], inputs["L0_attn_w_gate"]], axis=1)),
        "ww_nope": f(wink[:, :, 0:32].reshape(C, NH * 32)),
        "ww_x1": f(wink[:, :, 32:64].reshape(C, NH * 32)),
        "ww_x2": f(wink[:, :, 64:96].reshape(C, NH * 32)),
        "w_winv": f(winv.reshape(C, NH * 32)),
        "ws_nope": f(selk[:, :, 0:32].reshape(C, NH * 32)),
        "ws_x1": f(selk[:, :, 32:64].reshape(C, NH * 32)),
        "ws_x2": f(selk[:, :, 64:96].reshape(C, NH * 32)),
        "w_selv": f(selv.reshape(C, NH * 32)),
        "w_proj": f(inputs["L0_attn_w_proj"]),
        "w1t": f(inputs["L0_ffn_w1"].T),
        "b1": f(inputs["L0_ffn_b1"].reshape(1, HF)),
        "w2t": f(inputs["L0_ffn_w2"].T),
        "b2": f(inputs["L0_ffn_b2"].reshape(1, C)),
        "wg1": f(inputs["L1_ffn_wg1"]),
        "bg1": f(inputs["L1_ffn_bg1"].reshape(256, 1)),
        "wg2": f(inputs["L1_ffn_wg2"]),
        "bg2": f(inputs["L1_ffn_bg2"].reshape(40, 1)),
        "we": f(inputs["L1_ffn_We"]),
        "be": f(inputs["L1_ffn_be"]),
    }


def kernel(**inputs):
    com = _prep(inputs)
    x = np.ascontiguousarray(inputs["L0_attn_x"].astype(np.float32, copy=False))
    nc, dram, _ = build_nc()
    in_maps = []
    for i in range(8):
        m = dict(com)
        m["x"] = np.ascontiguousarray(x[i])
        in_maps.append(m)
    res = run_bass_kernel_spmd(nc, in_maps, core_ids=list(range(8)))
    return np.stack([res.results[i]["out"] for i in range(8)], axis=0)
